# version 15
# baseline (speedup 1.0000x reference)
import numpy as np
import ml_dtypes
import concourse.bass as bass
import concourse.mybir as mybir
from concourse.bass_utils import run_bass_kernel_spmd

F32 = mybir.dt.float32
BF16 = mybir.dt.bfloat16
AF = mybir.ActivationFunctionType
ALU = mybir.AluOpType
AX = mybir.AxisListType
bfdt = ml_dtypes.bfloat16

U42 = np.array([
 [0.6102175712585449, 0.6966079473495483, 0.06568920612335205, 0.6029453277587891, 0.4143662452697754, 0.24911487102508545, 0.5782934427261353, 0.24924850463867188, 0.959638237953186, 0.8191142082214355, 0.3645491600036621, 0.059471845626831055, 0.9161832332611084, 0.6182148456573486, 0.8282822370529175, 0.6976723670959473],
 [0.756269097328186, 0.3594318628311157, 0.4266716241836548, 0.477594256401062, 0.08340179920196533, 0.40768611431121826, 0.6278458833694458, 0.3251793384552002, 0.35155820846557617, 0.15171575546264648, 0.5589374303817749, 0.8866020441055298, 0.3201930522918701, 0.12432277202606201, 0.6967982053756714, 0.8069779872894287],
 [0.6127028465270996, 0.7509020566940308, 0.4151843786239624, 0.4897202253341675, 0.11665701866149902, 0.40220367908477783, 0.08376991748809814, 0.9528622627258301, 0.9559504985809326, 0.06945085525512695, 0.20124602317810059, 0.06397080421447754, 0.8091422319412231, 0.31807422637939453, 0.639366865158081, 0.8050472736358643],
 [0.5085939168930054, 0.5088651180267334, 0.9745233058929443, 0.3331362009048462, 0.05924034118652344, 0.7208822965621948, 0.3291349411010742, 0.5319250822067261, 0.4839836359024048, 0.643956184387207, 0.19713079929351807, 0.1282445192337036, 0.3164334297180176, 0.4948718547821045, 0.9159883260726929, 0.781721830368042],
], dtype=np.float32)

T = 2048; D = 2048; H = 16; DH = 128
KEEP = np.concatenate([np.arange(204), np.arange(1229, 2048)])
NKC = 8
TOK = 1024; HTOK = 512
Fd = 5632; NE = 8; NR = 16; NFC = 44
EVK = 204
MBEG, MEND = 205, 1023
PH = "ALL"
DEBUG_TAPS = ()

_ENGS = ("sync", "gpsimd", "vector", "scalar", "tensor", "scalardma")


class _Op:
    __slots__ = ("eng", "fn", "waits", "amt", "gi", "dma", "mark", "cum", "nsub", "chain")

    def __init__(self, eng, fn, waits, dma, gi, nsub=1, chain=None):
        self.eng = eng; self.fn = fn; self.waits = waits
        self.dma = dma; self.nsub = nsub
        self.amt = 16 * nsub if dma else 1
        self.gi = gi; self.mark = False; self.cum = 0
        self.chain = chain


class _Prog:
    def __init__(self):
        self.ops = []
        self.lw = {}
        self.lr = {}
        self.last_dma = {}

    def add(self, eng, fn, reads=(), writes=(), dma=False, nsub=1, chain=None):
        waits = []
        if dma:
            prev = self.last_dma.get(eng)
            if prev is not None:
                waits.append(prev)
        for r in reads:
            w = self.lw.get(r)
            if w is not None:
                waits.append(w)
        for r in writes:
            w = self.lw.get(r)
            if w is not None:
                waits.append(w)
            waits.extend(self.lr.get(r, ()))
        op = _Op(eng, fn, waits, dma, len(self.ops), nsub, chain)
        self.ops.append(op)
        if dma:
            self.last_dma[eng] = op
        for r in reads:
            self.lr.setdefault(r, []).append(op)
        for r in writes:
            self.lw[r] = op
            self.lr[r] = []
        return op

    @staticmethod
    def _need(w, op):
        return True

    def emit(self, block, sems):
        for op in self.ops:
            if op.dma:
                op.mark = True
            best = {}
            for w in op.waits:
                if self._need(w, op) and (w.eng not in best or w.gi > best[w.eng].gi):
                    best[w.eng] = w
            op.waits = list(best.values())
            for w in op.waits:
                w.mark = True
        cnt = {e: 0 for e in _ENGS}
        for op in self.ops:
            if op.mark:
                cnt[op.eng] += op.amt
            op.cum = cnt[op.eng]
        ops = self.ops

        def mkbody(name):
            group = ("scalar", "scalardma") if name == "scalar" else (name,)
            def body(e):
                hi = {en: 0 for en in _ENGS}
                for op in ops:
                    if op.eng not in group:
                        continue
                    for w in op.waits:
                        if not self._need(w, op):
                            continue
                        if w.cum > hi[w.eng]:
                            e.wait_ge(sems[w.eng], w.cum)
                            hi[w.eng] = w.cum
                    inst = op.fn(e)
                    if op.mark:
                        if isinstance(inst, list):
                            for ii in inst:
                                ii.then_inc(sems[op.eng], 16)
                        else:
                            inst.then_inc(sems[op.eng], op.amt)
                for gn in group:
                    if cnt[gn] > hi[gn]:
                        e.wait_ge(sems[gn], cnt[gn])
            return body
        block.sync(mkbody("sync"))
        block.gpsimd(mkbody("gpsimd"))
        block.vector(mkbody("vector"))
        block.scalar(mkbody("scalar"))
        block.tensor(mkbody("tensor"))


def _host_constants():
    inv = 1.0 / (10000.0 ** (np.arange(0, DH, 2, dtype=np.float32) / DH))
    fr = np.arange(T, dtype=np.float32)[:, None] * inv[None, :]
    emb = np.concatenate([fr, fr], 1)
    cosT = np.cos(emb).astype(np.float32)
    sinT = np.sin(emb).astype(np.float32)
    s = np.float32(1.0 / np.sqrt(np.float32(DH)))
    consts = np.tile(np.array([[1e-5, 1e-6, 1.0 / 818.0, 1.0 / 819.0, 0.25, 1.0, 0.0, 0.0]],
                              dtype=np.float32), (128, 1))
    prot = np.zeros((128, 128), dtype=np.float32)
    for d2 in range(64):
        prot[d2 + 64, d2] = -1.0
    for d2 in range(64, 128):
        prot[d2 - 64, d2] = 1.0
    mrg = np.zeros((128, NKC, 3), dtype=np.float32)
    for kc in range(NKC):
        for p in range(128):
            ki = kc * 128 + p
            if ki >= 1023:
                continue
            mrg[p, kc, 2] = 1.0
            if MBEG <= ki < MEND:
                mrg[p, kc, 0] = 1.0
            if ki == EVK:
                mrg[p, kc, 1] = 1.0
    return dict(cosT=cosT, sinT=sinT, qscale=s, consts=consts,
                identb=np.eye(128, dtype=bfdt), identf=np.eye(128, dtype=np.float32),
                prot=prot.astype(bfdt), onesrow=np.ones((1, 128), dtype=bfdt),
                mrg=mrg.astype(bfdt))


def _host_core_inputs(inputs, c, cst):
    b, hf = c // 2, c % 2
    X = np.asarray(inputs["L0_attn_hidden_states"])
    x_own = np.ascontiguousarray(X[b, hf * TOK:(hf + 1) * TOK])
    xk = np.zeros((1024, D), dtype=np.float32)
    xk[:1023] = X[b, KEEP]
    ownpos = np.arange(hf * TOK, hf * TOK + TOK)
    kpos = np.zeros(1024, dtype=np.int64); kpos[:1023] = KEEP
    cosq = np.ascontiguousarray((cst["cosT"][ownpos] * cst["qscale"]).T.astype(bfdt))
    sinq = np.ascontiguousarray((cst["sinT"][ownpos] * cst["qscale"]).T.astype(bfdt))
    cosk = np.ascontiguousarray(cst["cosT"][kpos].T.astype(bfdt))
    sink = np.ascontiguousarray(cst["sinT"][kpos].T.astype(bfdt))
    cosk[:, 1023] = 0; sink[:, 1023] = 0
    m_t = np.zeros((128, NKC), dtype=np.float32)
    if hf == 1:
        for kc in range(NKC):
            for p in range(128):
                ki = kc * 128 + p
                if MBEG <= ki < MEND:
                    m_t[p, kc] = 1.0
    w1T = np.asarray(inputs["L0_ffn_w1"]).T
    w3T = np.asarray(inputs["L0_ffn_w3"]).T
    w13 = np.empty((D, NFC * 256), dtype=np.float32)
    for fc in range(NFC):
        w13[:, fc * 256:fc * 256 + 128] = w1T[:, fc * 128:(fc + 1) * 128]
        w13[:, fc * 256 + 128:fc * 256 + 256] = w3T[:, fc * 128:(fc + 1) * 128]
    b1 = np.asarray(inputs["L0_ffn_b1"]); b3 = np.asarray(inputs["L0_ffn_b3"])
    b13 = np.ascontiguousarray(
        np.stack([b1.transpose(0, 2, 1), b3.transpose(0, 2, 1)], axis=2).transpose(1, 0, 2, 3))
    return dict(
        x_own=x_own, x_kept=np.ascontiguousarray(xk),
        wqT=np.ascontiguousarray(np.asarray(inputs["L0_attn_Wq"]).T),
        wkT=np.ascontiguousarray(np.asarray(inputs["L0_attn_Wk"]).T),
        wvT=np.ascontiguousarray(np.asarray(inputs["L0_attn_Wv"]).T),
        woT=np.ascontiguousarray(np.asarray(inputs["L0_attn_Wo"]).T),
        wmod=np.ascontiguousarray(np.asarray(inputs["L0_wmod"])),
        gateT=np.ascontiguousarray(np.asarray(inputs["L0_ffn_gate_w"]).T),
        w13=w13,
        w2T=np.ascontiguousarray(np.asarray(inputs["L0_ffn_w2"]).T),
        b13=b13,
        a2il=np.ascontiguousarray(np.asarray(inputs["L0_ffn_a2"]).transpose(2, 0, 1)),
        b2il=np.ascontiguousarray(np.asarray(inputs["L0_ffn_b2"]).transpose(2, 0, 1)),
        a1T=np.ascontiguousarray(np.asarray(inputs["L0_ffn_a1"]).reshape(NE * NR, D).T),
        a3T=np.ascontiguousarray(np.asarray(inputs["L0_ffn_a3"]).reshape(NE * NR, D).T),
        c_col=np.ascontiguousarray(np.asarray(inputs["L0_c"])[b].reshape(16, 128).T),
        u_own=np.ascontiguousarray(U42[b:b + 1]),
        cosq=cosq, sinq=sinq, cosk=cosk, sink=sink, m_t=m_t,
        consts=cst["consts"], identb=cst["identb"], identf=cst["identf"],
        prot=cst["prot"], onesrow=cst["onesrow"], mrg=cst["mrg"],
    )


def _build():
    nc = bass.Bass("TRN2", target_bir_lowering=False)
    P = _Prog()

    dx = nc.dram_tensor("x_own", [TOK, D], F32, kind="ExternalInput")
    dxk = nc.dram_tensor("x_kept", [1024, D], F32, kind="ExternalInput")
    dwq = nc.dram_tensor("wqT", [D, D], F32, kind="ExternalInput")
    dwk = nc.dram_tensor("wkT", [D, D], F32, kind="ExternalInput")
    dwv = nc.dram_tensor("wvT", [D, D], F32, kind="ExternalInput")
    dwo = nc.dram_tensor("woT", [D, D], F32, kind="ExternalInput")
    dwmod = nc.dram_tensor("wmod", [D, 12288], F32, kind="ExternalInput")
    dgate = nc.dram_tensor("gateT", [D, 8], F32, kind="ExternalInput")
    dw13 = nc.dram_tensor("w13", [D, NFC * 256], F32, kind="ExternalInput")
    dw2 = nc.dram_tensor("w2T", [Fd, D], F32, kind="ExternalInput")
    db13 = nc.dram_tensor("b13", [NR, NE, 2, Fd], F32, kind="ExternalInput")
    da2 = nc.dram_tensor("a2il", [Fd, NE, NR], F32, kind="ExternalInput")
    db2 = nc.dram_tensor("b2il", [NR, NE, D], F32, kind="ExternalInput")
    da1 = nc.dram_tensor("a1T", [D, 128], F32, kind="ExternalInput")
    da3 = nc.dram_tensor("a3T", [D, 128], F32, kind="ExternalInput")
    dc = nc.dram_tensor("c_col", [128, 16], F32, kind="ExternalInput")
    du = nc.dram_tensor("u_own", [1, 16], F32, kind="ExternalInput")
    dcosq = nc.dram_tensor("cosq", [128, 1024], BF16, kind="ExternalInput")
    dsinq = nc.dram_tensor("sinq", [128, 1024], BF16, kind="ExternalInput")
    dcosk = nc.dram_tensor("cosk", [128, 1024], BF16, kind="ExternalInput")
    dsink = nc.dram_tensor("sink", [128, 1024], BF16, kind="ExternalInput")
    dconsts = nc.dram_tensor("consts", [128, 8], F32, kind="ExternalInput")
    didb = nc.dram_tensor("identb", [128, 128], BF16, kind="ExternalInput")
    didf = nc.dram_tensor("identf", [128, 128], F32, kind="ExternalInput")
    dprot = nc.dram_tensor("prot", [128, 128], BF16, kind="ExternalInput")
    dones = nc.dram_tensor("onesrow", [1, 128], BF16, kind="ExternalInput")
    dmrg = nc.dram_tensor("mrg", [128, NKC, 3], BF16, kind="ExternalInput")
    dmt = nc.dram_tensor("m_t", [128, NKC], F32, kind="ExternalInput")
    dout = nc.dram_tensor("out", [TOK, D], F32, kind="ExternalOutput")
    dscrM = nc.dram_tensor("dscrM", [1, 12288], F32)
    dscrP = nc.dram_tensor("dscrP", [1, 12288], F32)
    dwe = nc.dram_tensor("dwe", [NE, HTOK], BF16)
    dw13b = nc.dram_tensor("dw13b", [D, NFC * 256], BF16)
    dw2b = nc.dram_tensor("dw2b", [Fd, D], BF16)

    taps = {}

    def tap(name, shape, dtype=F32):
        if name in DEBUG_TAPS:
            if name not in taps:
                taps[name] = nc.dram_tensor("tap_" + name, shape, dtype, kind="ExternalOutput")
            return taps[name]
        return None

    arA = nc.alloc_sbuf_tensor("arA", [128, 16384], F32)
    arB = nc.alloc_sbuf_tensor("arB", [128, 15360], F32)
    arC = nc.alloc_sbuf_tensor("arC", [128, 8192], F32)
    arA_bf = arA[:, :].bitcast(BF16)
    arB_bf = arB[:, :].bitcast(BF16)
    arC_bf = arC[:, :].bitcast(BF16)

    def carve(base, off, n, lead=None):
        v = base[:, off:off + n]
        if lead is not None:
            v = v.rearrange("p (a b) -> p a b", a=lead)
        return v

    hq = carve(arA_bf, 0, 16 * 1024, 16)
    hk = carve(arA_bf, 16 * 1024, 16 * 1024, 16)
    xres = arA[:, :].rearrange("p (a b) -> p a b", a=8)

    o = 0
    qTg = carve(arB_bf, o, 4096, 4); o += 4096
    kTg = carve(arB_bf, o, 4096, 4); o += 4096
    vbuf = carve(arB_bf, o, 8 * 4 * 129).rearrange("p (k h d) -> p k h d", k=8, h=4)
    o += 8 * 4 * 129
    pT7 = carve(arB_bf, o, 1024); cosq_s = pT7; o += 1024
    pTc = carve(arB_bf, o, 1024); sinq_s = pTc; o += 1024
    pT6 = carve(arB_bf, o, 1024); o += 1024
    probsq = carve(arB_bf, o, 1024); o += 1024
    wbufA = carve(arB_bf, o, 4096, 16); o += 4096
    wbufB = carve(arB_bf, o, 4096, 16); o += 4096
    assert o <= 30720, o
    act_sum = carve(arB_bf, 0, NFC * 512, NFC)
    h2T = carve(arB_bf, NFC * 512, 16 * 512, 16)
    assert NFC * 512 + 16 * 512 <= 30720

    xkb = [arC[:, 0:2048], arC[:, 2048:4096]]
    ho = carve(arC_bf, 0, 16384, 16)
    o = 0
    webc = carve(arC_bf, o, 8 * 512, 8); o += 8 * 512
    w13s = [carve(arC_bf, o, 4096, 16), carve(arC_bf, o + 4096, 4096, 16)]
    o += 8192
    w2s = [carve(arC_bf, o, 512), carve(arC_bf, o + 512, 512)]
    o += 1024
    accf = arC[:, (o // 2):(o // 2) + 512]
    o += 1024
    aTbv = carve(arC_bf, o, 2048, 16)
    accf2 = arC[:, (o // 2):(o // 2) + 512]
    o += 2048
    assert o <= 16384
    w2g = [arC_bf[:, 8192:10240].rearrange("p (a b) -> p a b", a=4),
           arC_bf[:, 10240:12288].rearrange("p (a b) -> p a b", a=4)]

    CLOB_B_ARB = ("qTg", "kTg", "vbuf", "r1a", "r1b", "r1c", "r1d", "wi0", "wi1")
    CLOB_B_ARC = ("ho", "xk0", "xk1")

    htok = nc.alloc_sbuf_tensor("htok", [128, 2048], BF16)
    bc1 = nc.alloc_sbuf_tensor("bc1", [128, 2048], BF16)
    bc2 = nc.alloc_sbuf_tensor("bc2", [128, 2048], BF16)
    stage = arB[0:1, 12304:12816]
    rowA = arB[0:1, 12816:13328]
    rowB = arB[0:1, 13328:13840]
    wstf = arB[:, 0:4096].rearrange("p (a b) -> p a b", a=16)
    c_sb = nc.alloc_sbuf_tensor("c_sb", [128, 16], F32)
    scsb = nc.alloc_sbuf_tensor("scsb", [128, 16], BF16)
    csts = nc.alloc_sbuf_tensor("csts", [128, 8], F32)
    identb = nc.alloc_sbuf_tensor("identb_s", [128, 128], BF16)
    identf = nc.alloc_sbuf_tensor("identf_s", [128, 128], F32)
    prot = nc.alloc_sbuf_tensor("prot_s", [128, 128], BF16)
    onesr = nc.alloc_sbuf_tensor("onesr", [1, 128], BF16)
    mrg = nc.alloc_sbuf_tensor("mrg_s", [128, NKC, 3], BF16)
    m_t = nc.alloc_sbuf_tensor("mt_s", [128, NKC], F32)
    u_sb = nc.alloc_sbuf_tensor("u_sb", [1, 16], F32)
    tmpf = nc.alloc_sbuf_tensor("tmpf", [128, 512], F32)
    qraw = tmpf[:, 0:256].bitcast(BF16)
    rotb = tmpf[:, 256:512].bitcast(BF16)
    qcos = nc.alloc_sbuf_tensor("qcos", [128, 512], BF16)
    qcos2 = nc.alloc_sbuf_tensor("qcos2", [128, 512], BF16)
    ca1col = nc.alloc_sbuf_tensor("ca1col", [128, 16], F32)
    sacol = nc.alloc_sbuf_tensor("sacol", [128, 16], F32)
    stats = nc.alloc_sbuf_tensor("stats", [128, 16], F32)
    nmx = nc.alloc_sbuf_tensor("nmx", [128, 2], F32)
    zvv = nc.alloc_sbuf_tensor("zvv", [128, 2], F32)
    hot = nc.alloc_sbuf_tensor("hot", [128, 128], F32)
    mz = nc.alloc_sbuf_tensor("mz", [128, 4], F32)
    mrow = nc.alloc_sbuf_tensor("mrow", [1, 8], F32)
    bmt = nc.alloc_sbuf_tensor("bmt", [1, 8], F32)
    v204 = nc.alloc_sbuf_tensor("v204", [1, 128], BF16)
    vescl = nc.alloc_sbuf_tensor("vescl", [1, 128], BF16)
    gateTb = nc.alloc_sbuf_tensor("gateTb", [128, 16, 8], BF16)
    upack = nc.alloc_sbuf_tensor("upack", [128, NE * HTOK], BF16)
    u1e = upack[0:16, :].rearrange("p (a b) -> p a b", a=NE)
    u3e = upack[32:48, :].rearrange("p (a b) -> p a b", a=NE)
    u1n = upack[64:80, :].rearrange("p (a b) -> p a b", a=NE)
    u3n = nc.alloc_sbuf_tensor("u3n", [NR, NE, HTOK], BF16)
    zsb = u1e
    ubt = arC_bf[:, 4096:5120].rearrange("p (a b) -> p a b", a=2)
    werows = nc.alloc_sbuf_tensor("werows", [NE, HTOK], BF16)
    wfull = nc.alloc_sbuf_tensor("wfull", [128, 4, NE], F32)
    lg = nc.alloc_sbuf_tensor("lg", [128, 4, NE], F32)
    t2s = nc.alloc_sbuf_tensor("t2s", [128, 8], F32)
    tmp8 = nc.alloc_sbuf_tensor("tmp8", [128, NE], F32)
    msk8 = nc.alloc_sbuf_tensor("msk8", [128, NE], F32)
    actb = nc.alloc_sbuf_tensor("actb", [128, 512], BF16)
    wactb = nc.alloc_sbuf_tensor("wactb", [128, 512], BF16)
    tmpf2 = nc.alloc_sbuf_tensor("tmpf2", [128, 512], F32)
    qraw2 = tmpf2[:, 0:256].bitcast(BF16)
    rotb2 = tmpf2[:, 256:512].bitcast(BF16)
    actb2 = nc.alloc_sbuf_tensor("actb2", [128, 512], BF16)
    wactb2 = nc.alloc_sbuf_tensor("wactb2", [128, 512], BF16)
    b13s = nc.alloc_sbuf_tensor("b13s", [80, 2, NE, 128], BF16)
    b13sB = nc.alloc_sbuf_tensor("b13sB", [NR, 2, NE, 128], BF16)
    a2s = nc.alloc_sbuf_tensor("a2s", [128, 2, NE, NR], BF16)
    winit = [wbufA, wbufB]
    coskS = pT6
    sinkS = probsq
    b2dj = arC_bf[0:16, 4096:8192].rearrange("p (a b) -> p a b", a=NE)

    pBig = nc.alloc_psum_tensor("pBig", [128, 2048], F32)
    pH = nc.alloc_psum_tensor("pH", [128, 1024], BF16)
    zA = nc.alloc_psum_tensor("zA", [128, 512], F32)
    zB = nc.alloc_psum_tensor("zB", [128, 512], F32)
    zC = nc.alloc_psum_tensor("zC", [128, 512], F32)
    print("sbuf bytes remaining:", nc.sbuf_bytes_remaining,
          "psum banks remaining:", nc.psum_banks_remaining)
    PBQ = [(pBig[:, i * 512:(i + 1) * 512], "pBig%d" % i) for i in range(4)]
    PB_ALL = ("pBig0", "pBig1", "pBig2", "pBig3")

    def dma(eng, out, in_, reads=(), writes=()):
        if eng == "scalar":
            eng = "scalardma"
        return P.add(eng, lambda e, oo=out, ii=in_: e.dma_start(out=oo, in_=ii),
                     reads=reads, writes=writes, dma=True)

    def dmas(eng, pairs, reads=(), writes=()):
        if eng == "scalar":
            eng = "scalardma"
        def f(e, ps=tuple(pairs)):
            return [e.dma_start(out=oo, in_=ii) for (oo, ii) in ps]
        return P.add(eng, f, reads=reads, writes=writes, dma=True, nsub=len(pairs))

    def mm(out, lhsT, rhs, start, stop, reads=(), writes=(), skip_gc=False):
        ch = writes[0] if writes else None
        return P.add("tensor",
                     lambda e, oo=out, l=lhsT, r=rhs, st=start, sp=stop, sg=skip_gc:
                     e.matmul(oo, l, r, start=st, stop=sp, skip_group_check=sg),
                     reads=reads, writes=writes, chain=ch)

    def tr(out, in_, ident, reads=(), writes=()):
        return P.add("tensor",
                     lambda e, oo=out, ii=in_, dd=ident: e.transpose(oo, ii, dd),
                     reads=reads, writes=writes)

    def act(out, in_, func, bias=None, scale=None, accum=None, reads=(), writes=()):
        kw = {}
        if bias is not None:
            kw["bias"] = bias
        if scale is not None:
            kw["scale"] = scale
        if accum is not None:
            kw["accum_out"] = accum
        return P.add("scalar",
                     lambda e, oo=out, ii=in_, f=func, kw=kw: e.activation(oo, ii, f, **kw),
                     reads=reads, writes=writes)

    def scp(out, in_, reads=(), writes=()):
        return P.add("scalar", lambda e, oo=out, ii=in_: e.copy(oo, ii),
                     reads=reads, writes=writes)

    def vcp(out, in_, reads=(), writes=()):
        return P.add("vector", lambda e, oo=out, ii=in_: e.tensor_copy(oo, ii),
                     reads=reads, writes=writes)

    def tt(out, a, b, op, reads=(), writes=()):
        return P.add("vector", lambda e, oo=out, x=a, y=b, p=op: e.tensor_tensor(oo, x, y, p),
                     reads=reads, writes=writes)

    def ts(out, in_, s1, op0, s2=None, op1=ALU.bypass, reads=(), writes=()):
        return P.add("vector",
                     lambda e, oo=out, ii=in_, a=s1, b=s2, p0=op0, p1=op1:
                     e.tensor_scalar(oo, ii, a, b, op0=p0, op1=p1),
                     reads=reads, writes=writes)

    def stt(out, in0, scalar, in1, op0, op1, reads=(), writes=()):
        return P.add("vector",
                     lambda e, oo=out, ii=in0, s=scalar, jj=in1, p0=op0, p1=op1:
                     e.scalar_tensor_tensor(oo, ii, s, jj, op0=p0, op1=p1),
                     reads=reads, writes=writes)

    def red(out, in_, op, negate=False, reads=(), writes=()):
        return P.add("vector",
                     lambda e, oo=out, ii=in_, p=op, n=negate:
                     e.tensor_reduce(oo, ii, axis=AX.X, op=p, negate=n),
                     reads=reads, writes=writes)

    def recip(out, in_, reads=(), writes=()):
        return P.add("vector", lambda e, oo=out, ii=in_: e.reciprocal(oo, ii),
                     reads=reads, writes=writes)

    def memset(out, val, reads=(), writes=()):
        return P.add("vector", lambda e, oo=out, v=val: e.memset(oo, v),
                     reads=reads, writes=writes)

    _smalls = [
        (c_sb[:, :], dc[:, :], "c_sb"), (csts[:, :], dconsts[:, :], "csts"),
        (identb[:, :], didb[:, :], "identb"), (identf[:, :], didf[:, :], "identf"),
        (prot[:, :], dprot[:, :], "prot"), (onesr[:, :], dones[:, :], "onesr"),
        (mrg[:, :, :], dmrg[:, :, :], "mrg"), (m_t[:, :], dmt[:, :], "m_t"),
        (u_sb[:, :], du[:, :], "u_sb"),
    ]
    dmas("sync", [(d0, s0) for (d0, s0, _) in _smalls],
         writes=tuple(nm for (_, _, nm) in _smalls))
    act(scsb[:, :], c_sb[:, :], AF.Silu, reads=("c_sb",), writes=("scsb",))
    for njp in range(0, 48, 2):
        for s in (0, 1):
            nj = njp + s
            dmas("gpsimd", [(winit[s][:, q * 4:(q + 1) * 4, :],
                             dwmod[q * 512:(q + 1) * 512, nj * 256:(nj + 1) * 256]
                             .rearrange("(a p) b -> p a b", p=128)) for q in range(4)],
                 reads=(), writes=("wi%d" % s,))
        for kc in range(16):
            for s in (0, 1):
                zsl, zslr = (zA, "zA") if s == 0 else (zB, "zB")
                mm(zsl[0:1, 0:256], scsb[:, kc:kc + 1], winit[s][:, kc, :],
                   kc == 0, kc == 15, reads=("scsb", "wi%d" % s), writes=(zslr,))
        for s in (0, 1):
            nj = njp + s
            zsl, zslr = (zA, "zA") if s == 0 else (zB, "zB")
            stg = stage[0:1, 0:256] if s == 0 else stage[0:1, 256:512]
            stgr = "stage" if s == 0 else "stage2"
            P.add("scalar",
                  lambda e, zsl=zsl, stg=stg: e.copy(stg, zsl[0:1, 0:256]),
                  reads=(zslr,), writes=(stgr, zslr))
            dma("sync", dscrM[0:1, nj * 256:(nj + 1) * 256], stg,
                reads=(stgr,), writes=("dscrM",))

    for slot in (1, 4):
        for ch in range(4):
            off = slot * 2048 + ch * 512
            dma("sync", rowA[0:1, :], dscrM[0:1, off:off + 512],
                reads=("dscrM",), writes=("rowA",))
            ts(rowB[0:1, :], rowA[0:1, :], 1.0, ALU.add, reads=("rowA",), writes=("rowB",))
            dma("sync", dscrP[0:1, off:off + 512], rowB[0:1, :],
                reads=("rowB",), writes=("dscrP",))
    for slot in (2, 5):
        for ch in range(4):
            off = slot * 2048 + ch * 512
            dma("sync", rowA[0:1, :], dscrM[0:1, off:off + 512],
                reads=("dscrM",), writes=("rowA",))
            P.add("scalar", lambda e: e.activation(rowB[0:1, :], rowA[0:1, :], AF.Sign),
                  reads=("rowA",), writes=("rowB",))
            P.add("scalar", lambda e: e.activation(rowA[0:1, :], rowA[0:1, :], AF.Abs),
                  reads=("rowA",), writes=("rowA",))
            ts(rowA[0:1, :], rowA[0:1, :], 0.25, ALU.max, reads=("rowA",), writes=("rowA",))
            tt(rowB[0:1, :], rowB[0:1, :], rowA[0:1, :], ALU.mult,
               reads=("rowA",), writes=("rowB",))
            dma("sync", dscrP[0:1, off:off + 512], rowB[0:1, :],
                reads=("rowB",), writes=("dscrP",))

    def bcast(dst, slot, name):
        dsrc = dscrM if slot in (0, 3) else dscrP
        rn = "dscrM" if slot in (0, 3) else "dscrP"
        return dma("gpsimd", dst,
                   dsrc[0:1, slot * 2048:(slot + 1) * 2048].partition_broadcast(128),
                   reads=(rn,), writes=(name,))

    bcast(bc1[:, :], 1, "bc1")

    def ln_norm_tile(i, src_dram, row0, dstview, dstname, dst_ti):
        xb = xkb[i % 2]
        xr = "xk%d" % (i % 2)
        dma("scalar", xb[:, :], src_dram[row0:row0 + 128, :], reads=(), writes=(xr,))
        sc0 = 8 * (i % 2)
        red(stats[:, sc0:sc0 + 1], xb[:, :], ALU.add, reads=(xr,), writes=("stats%d" % (i % 2),))
        act(pBig[:, :], xb[:, :], AF.Square, accum=stats[:, sc0 + 1:sc0 + 2],
            reads=(xr,) + PB_ALL, writes=PB_ALL + ("stats%d" % (i % 2),))
        ts(stats[:, sc0 + 2:sc0 + 3], stats[:, sc0:sc0 + 1], 1.0 / 2048, ALU.mult,
           reads=("stats%d" % (i % 2),), writes=("stats%d" % (i % 2),))
        tt(stats[:, sc0 + 3:sc0 + 4], stats[:, sc0 + 2:sc0 + 3], stats[:, sc0 + 2:sc0 + 3],
           ALU.mult, reads=("stats%d" % (i % 2),), writes=("stats%d" % (i % 2),))
        stt(stats[:, sc0 + 4:sc0 + 5], stats[:, sc0 + 1:sc0 + 2], 1.0 / 2048,
            stats[:, sc0 + 3:sc0 + 4], ALU.mult, ALU.subtract,
            reads=("stats%d" % (i % 2),), writes=("stats%d" % (i % 2),))
        act(stats[:, sc0 + 5:sc0 + 6], stats[:, sc0 + 4:sc0 + 5], AF.Sqrt, bias=csts[:, 0:1],
            reads=("stats%d" % (i % 2), "csts"), writes=("stats%d" % (i % 2),))
        recip(stats[:, sc0 + 6:sc0 + 7], stats[:, sc0 + 5:sc0 + 6],
              reads=("stats%d" % (i % 2),), writes=("stats%d" % (i % 2),))
        hb, hbr = (htok, "htok") if i % 2 == 0 else (bc2, "bc2n")
        ts(hb[:, :], xb[:, :], stats[:, sc0 + 2:sc0 + 3], ALU.subtract,
           stats[:, sc0 + 6:sc0 + 7], ALU.mult,
           reads=(xr, "stats%d" % (i % 2)), writes=(hbr,))
        for g2 in range(2):
            for j in range(8):
                fc = g2 * 8 + j
                tr(pH[:, j * 128:(j + 1) * 128], hb[:, fc * 128:(fc + 1) * 128], identb[:, :],
                   reads=(hbr, "identb", "pH%d" % j), writes=("pH%d" % j,))
            P.add("scalar",
                  lambda e, dv=dstview, g2=g2, ti=dst_ti:
                  e.copy(dv[:, g2 * 8:(g2 + 1) * 8, ti * 128:(ti + 1) * 128],
                         pH[:, :].rearrange("p (a b) -> p a b", a=8)),
                  reads=tuple("pH%d" % k for k in range(8)),
                  writes=(dstname,) + tuple("pH%d" % k for k in range(8)))

    def ln_mod_tile(i, src_dram, row0, dstview, dstname, dst_ti):
        xb = xkb[i % 2]
        xr = "xk%d" % (i % 2)
        dma("sync", xb[:, :], src_dram[row0:row0 + 128, :], reads=(), writes=(xr,))
        red(stats[:, 0:1], xb[:, :], ALU.add, reads=(xr,), writes=("stats",))
        act(pBig[:, :], xb[:, :], AF.Square, accum=stats[:, 1:2],
            reads=(xr,) + PB_ALL, writes=PB_ALL + ("stats",))
        ts(stats[:, 2:3], stats[:, 0:1], 1.0 / 2048, ALU.mult, reads=("stats",), writes=("stats",))
        tt(stats[:, 3:4], stats[:, 2:3], stats[:, 2:3], ALU.mult, reads=("stats",), writes=("stats",))
        stt(stats[:, 4:5], stats[:, 1:2], 1.0 / 2048, stats[:, 3:4], ALU.mult, ALU.subtract,
            reads=("stats",), writes=("stats",))
        act(stats[:, 5:6], stats[:, 4:5], AF.Sqrt, bias=csts[:, 0:1],
            reads=("stats", "csts"), writes=("stats",))
        recip(stats[:, 6:7], stats[:, 5:6], reads=("stats",), writes=("stats",))
        ts(pBig[:, :], xb[:, :], stats[:, 2:3], ALU.subtract, stats[:, 6:7], ALU.mult,
           reads=(xr, "stats") + PB_ALL, writes=PB_ALL)
        tt(htok[:, :], pBig[:, :], bc1[:, :], ALU.mult,
           reads=PB_ALL + ("bc1",), writes=("htok",) + PB_ALL)
        tt(htok[:, :], htok[:, :], bc2[:, :], ALU.add, reads=("bc2",), writes=("htok",))
        for g2 in range(2):
            for j in range(8):
                fc = g2 * 8 + j
                tr(pH[:, j * 128:(j + 1) * 128], htok[:, fc * 128:(fc + 1) * 128], identb[:, :],
                   reads=("htok", "identb", "pH%d" % j), writes=("pH%d" % j,))
            P.add("scalar",
                  lambda e, dv=dstview, g2=g2, ti=dst_ti:
                  e.copy(dv[:, g2 * 8:(g2 + 1) * 8, ti * 128:(ti + 1) * 128],
                         pH[:, :].rearrange("p (a b) -> p a b", a=8)),
                  reads=tuple("pH%d" % k for k in range(8)),
                  writes=(dstname,) + tuple("pH%d" % k for k in range(8)))

    for i in range(8):
        ln_norm_tile(2 * i, dx, i * 128, hq, "hq", i)
        ln_norm_tile(2 * i + 1, dxk, i * 128, hk, "hk", i)
    if PH == "ALL":
        dmas("gpsimd", [(dw13b[q * 512:(q + 1) * 512, :], dw13[q * 512:(q + 1) * 512, :])
                        for q in range(4)], writes=("dw13b",))
        dmas("gpsimd", [(dw2b[q * 1408:(q + 1) * 1408, :], dw2[q * 1408:(q + 1) * 1408, :])
                        for q in range(4)], writes=("dw2b",))
        dma("gpsimd", gateTb[:, :, :], dgate[:, :].rearrange("(a p) b -> p a b", p=128),
            writes=("gateTb",))

    bcast(bc2[:, :], 0, "bc2b")
    P.ops[-1].waits.extend(P.lr.get("bc2n", ()))
    for vsrc, vrn, vdst, vdr in ((bc1, "bc1", ca1col, "ca1col"), (bc2, "bc2b", sacol, "sacol")):
        for f in range(16):
            tr(pH[:, f * 64:f * 64 + 1], vsrc[0:1, f * 128:(f + 1) * 128], identb[0:1, 0:1],
               reads=(vrn, "identb", "pH%d" % (f // 2)), writes=("pH%d" % (f // 2),))
        pH3 = pH[:, :].rearrange("p (a b) -> p a b", a=16)
        P.add("scalar", lambda e, vdst=vdst, pH3=pH3: e.copy(vdst[:, :], pH3[:, :, 0:1]),
              reads=tuple("pH%d" % k for k in range(8)),
              writes=(vdr,) + tuple("pH%d" % k for k in range(8)))
    for dv, dn in ((hq, "hq"), (hk, "hk")):
        for fc in range(16):
            ts(dv[:, fc, :], dv[:, fc, :], ca1col[:, fc:fc + 1], ALU.mult,
               sacol[:, fc:fc + 1], ALU.add, reads=("ca1col", "sacol"), writes=(dn,))

    memset(vbuf[:, :, :, 128:129], 1.0, writes=("vbuf",))

    wslot = [wbufA, wbufB]
    wres = ["wi0", "wi1"]
    slot_i = [0]

    def load_wslice(wdram, col0, width=256):
        s = slot_i[0]; slot_i[0] ^= 1
        wb = wslot[s]; rn = wres[s]
        dmas("gpsimd", [(wb[:, q * 4:(q + 1) * 4, 0:width],
                         wdram[q * 512:(q + 1) * 512, col0:col0 + width]
                         .rearrange("(a p) b -> p a b", p=128)) for q in range(4)],
             reads=(), writes=(rn,))
        return wb, rn

    def rope_evac(pp, pres, cos_s, sin_s, crs, dstv, dname, hh, n):
        par = (hh * 2 + n) % 2
        qr, qrr = (qraw, "qraw") if par == 0 else (qraw2, "tmpf2")
        rb, rbr = (rotb, "rotb") if par == 0 else (rotb2, "tmpf2")
        qc, qcr = (qcos, "qcos") if par == 0 else (qcos2, "qcos2")
        zsl, zslr = (zB, "zB") if par == 0 else (zC, "zC")
        scp(qr[:, :], pp, reads=(pres,), writes=(qrr, pres))
        mm(zsl[:, :], prot[:, :], qr[:, :], True, True,
           reads=("prot", qrr, zslr), writes=(zslr,))
        tt(qc[:, :], qr[:, :], cos_s, ALU.mult, reads=(qrr, crs[0]), writes=(qcr,))
        scp(rb[:, :], zsl[:, :], reads=(zslr,), writes=(rbr, zslr))
        dsl = dstv[:, hh, n * 512:(n + 1) * 512]
        tt(dsl, rb[:, :], sin_s, ALU.mult, reads=(rbr, crs[1]), writes=(dname, "wst"))
        tt(dsl, dsl, qc[:, :], ALU.add, reads=(qcr,), writes=(dname,))

    def smm(hh, qt):
        pp, pres = (pBig[:, 0:1024], ("pBig0", "pBig1")) if qt % 2 == 0 else \
                   (pBig[:, 1024:2048], ("pBig2", "pBig3"))
        for half in range(2):
            mm(pp[:, half * 512:(half + 1) * 512],
               qTg[:, hh, qt * 128:(qt + 1) * 128], kTg[:, hh, half * 512:(half + 1) * 512],
               True, True, reads=("qTg", "kTg") + pres, writes=pres)
        nslot = qt % 2
        red(nmx[:, nslot:nslot + 1], pp, ALU.max, negate=True,
            reads=pres, writes=("nmx%d" % nslot,))

    def stail(hh, qt, pdst, pdres):
        pp, pres = (pBig[:, 0:1024], ("pBig0", "pBig1")) if qt % 2 == 0 else \
                   (pBig[:, 1024:2048], ("pBig2", "pBig3"))
        nslot = qt % 2
        act(probsq[:, :], pp, AF.Exp, bias=nmx[:, nslot:nslot + 1],
            reads=pres + ("nmx%d" % nslot, "r1d"), writes=("r1d",) + pres)
        for kc in range(8):
            tr(pH[:, kc * 128:(kc + 1) * 128], probsq[:, kc * 128:(kc + 1) * 128],
               identb[:, :], reads=("r1d", "identb", "pH%d" % kc), writes=("pH%d" % kc,))
        vcp(pdst, pH[:, :], reads=tuple("pH%d" % k for k in range(8)),
            writes=(pdres,) + tuple("pH%d" % k for k in range(8)))

    def pv(head, hh, qt, psrc, psres):
        psrc3 = psrc.rearrange("p (a b) -> p a b", a=8)
        for kc in range(8):
            K = 127 if kc == 7 else 128
            mm(zA[:, 0:129], psrc3[0:K, kc, :], vbuf[0:K, kc, hh, 0:129],
               kc == 0, kc == 7, reads=(psres, "vbuf", "zA"), writes=("zA",))
        recip(zvv[:, 0:1], zA[:, 128:129], reads=("zA",), writes=("zvv",))
        ts(hot[:, :], zA[:, 0:128], zvv[:, 0:1], ALU.mult,
           reads=("zA", "zvv"), writes=("hot", "zA"))
        tr(zB[:, 0:128], hot[:, :], identf[:, :], reads=("hot", "identf", "zB"), writes=("zB",))
        P.add("scalar",
              lambda e, head=head, qt=qt:
              e.copy(ho[:, head, qt * 128:(qt + 1) * 128], zB[:, 0:128]),
              reads=("zB",), writes=("ho", "zB"))

    for g in range(4):
        dmas("sync", [(cosq_s, dcosq[:, :]), (sinq_s, dsinq[:, :]),
                      (coskS, dcosk[:, :]), (sinkS, dsink[:, :])],
             reads=(), writes=("r1a", "r1b", "r1c", "r1d"))
        for hp in range(2):
            wb, rn = load_wslice(dwq, g * 512 + hp * 256)
            for hh in (hp * 2, hp * 2 + 1):
                pps = [PBQ[(hh * 2 + n) % 4] for n in range(2)]
                for kc in range(16):
                    for n in range(2):
                        pp, pres = pps[n]
                        mm(pp, wb[:, kc, (hh - hp * 2) * 128:(hh - hp * 2 + 1) * 128],
                           hq[:, kc, n * 512:(n + 1) * 512],
                           kc == 0, kc == 15, reads=(rn, "hq", pres), writes=(pres,))
                for n in range(2):
                    pp, pres = pps[n]
                    rope_evac(pp, pres, cosq_s[:, n * 512:(n + 1) * 512],
                              sinq_s[:, n * 512:(n + 1) * 512], ("r1a", "r1b"), qTg, "qTg", hh, n)
        for hp in range(2):
            wb, rn = load_wslice(dwk, g * 512 + hp * 256)
            for hh in (hp * 2, hp * 2 + 1):
                pps = [PBQ[(hh * 2 + n) % 4] for n in range(2)]
                for kc in range(16):
                    for n in range(2):
                        pp, pres = pps[n]
                        mm(pp, wb[:, kc, (hh - hp * 2) * 128:(hh - hp * 2 + 1) * 128],
                           hk[:, kc, n * 512:(n + 1) * 512],
                           kc == 0, kc == 15, reads=(rn, "hk", pres), writes=(pres,))
                for n in range(2):
                    pp, pres = pps[n]
                    rope_evac(pp, pres, coskS[:, n * 512:(n + 1) * 512],
                              sinkS[:, n * 512:(n + 1) * 512], ("r1c", "r1d"), kTg, "kTg", hh, n)
        for hp in range(2):
            wb, rn = load_wslice(dwv, g * 512 + hp * 256)
            for ktp in range(0, 8, 2):
                for kc in range(16):
                    for kt in (ktp, ktp + 1):
                        pp, pres = PBQ[kt % 4]
                        mm(pp[:, 0:256], hk[:, kc, kt * 128:(kt + 1) * 128], wb[:, kc, 0:256],
                           kc == 0, kc == 15, reads=(rn, "hk", pres), writes=(pres,))
                for kt in (ktp, ktp + 1):
                    pp, pres = PBQ[kt % 4]
                    P.add("scalar",
                          lambda e, pp=pp, kt=kt, hp=hp:
                          e.copy(vbuf[:, kt, hp * 2:hp * 2 + 2, 0:128],
                                 pp[:, 0:256].rearrange("p (a b) -> p a b", a=2)),
                          reads=(pres,), writes=("vbuf", pres))

        for hh in range(4):
            head = g * 4 + hh

            def slot(qt):
                return (pTc, "r1b") if qt % 2 == 0 else (pT6, "r1c")

            smm(hh, 7)
            smm(hh, 0)
            stail(hh, 7, pT7, "r1a")
            pT73 = pT7.rearrange("p (a b) -> p a b", a=8)
            for kc in range(8):
                mm(zC[:, 0:3], pT73[:, kc, :], mrg[:, kc, :], kc == 0, kc == 7,
                   reads=("r1a", "mrg", "zC"), writes=("zC",))
            vcp(mz[:, 0:3], zC[:, 0:3], reads=("zC",), writes=("mz", "zC"))
            dmas("sync", [(mrow[0:1, 0:3], mz[127:128, 0:3]),
                          (v204[0:1, :], vbuf[76:77, 1, hh, 0:128])],
                 reads=("mz", "vbuf"), writes=("mrow", "v204"))
            for qt in range(6):
                smm(hh, qt + 1) if qt < 5 else smm(hh, 6)
                if qt >= 1:
                    pv(head, hh, qt - 1, *slot(qt - 1))
                stail(hh, qt, *slot(qt))
            pv(head, hh, 5, *slot(5))
            stail(hh, 6, pTc, "r1b")
            recip(bmt[0:1, 0:1], mrow[0:1, 2:3], reads=("mrow",), writes=("bmt",))
            ts(bmt[0:1, 1:2], mrow[0:1, 0:1], bmt[0:1, 0:1], ALU.mult, csts[0:1, 2:3], ALU.mult,
               reads=("mrow", "bmt", "csts"), writes=("bmt",))
            ts(bmt[0:1, 1:2], bmt[0:1, 1:2], csts[0:1, 1:2], ALU.add,
               reads=("bmt", "csts"), writes=("bmt",))
            ts(bmt[0:1, 2:3], mrow[0:1, 1:2], bmt[0:1, 0:1], ALU.mult,
               reads=("mrow", "bmt"), writes=("bmt",))
            recip(bmt[0:1, 3:4], bmt[0:1, 1:2], reads=("bmt",), writes=("bmt",))
            ts(bmt[0:1, 4:5], bmt[0:1, 2:3], bmt[0:1, 3:4], ALU.mult, reads=("bmt",), writes=("bmt",))
            ts(bmt[0:1, 4:5], bmt[0:1, 4:5], 1.0, ALU.min, 0.0, ALU.max,
               reads=("bmt",), writes=("bmt",))
            ts(bmt[0:1, 5:6], bmt[0:1, 4:5], u_sb[0:1, head:head + 1], ALU.is_gt,
               reads=("bmt", "u_sb"), writes=("bmt",))
            ts(bmt[0:1, 6:7], bmt[0:1, 5:6], csts[0:1, 3:4], ALU.mult,
               reads=("bmt", "csts"), writes=("bmt",))
            ts(vescl[0:1, :], v204[0:1, :], bmt[0:1, 6:7], ALU.mult,
               reads=("v204", "bmt"), writes=("vescl",))
            mm(zC[:, 0:128], onesr[0:1, :], vescl[0:1, :], True, True,
               reads=("onesr", "vescl", "zC"), writes=("zC",))
            for kt in range(8):
                stt(vbuf[:, kt, hh, 0:128], zC[:, 0:128], m_t[:, kt:kt + 1],
                    vbuf[:, kt, hh, 0:128], ALU.mult, ALU.add,
                    reads=("zC", "m_t", "vbuf"), writes=("vbuf",))
            pv(head, hh, 6, pTc, "r1b")
            pv(head, hh, 7, pT7, "r1a")

    bcast(bc1[:, :], 2, "bc1")
    dmas("sync", [(xres[:, i, :], dx[i * 128:(i + 1) * 128, :]) for i in range(8)],
         reads=(), writes=tuple("x%d" % i for i in range(8)) + ("hq", "hk"))
    for dj in range(8):
        wb, rn = load_wslice(dwo, dj * 256)
        for tip in range(0, 8, 2):
            for kc in range(16):
                for ti in (tip, tip + 1):
                    pp, pres = PBQ[ti % 2]
                    mm(pp[:, 0:256], ho[:, kc, ti * 128:(ti + 1) * 128], wb[:, kc, 0:256],
                       kc == 0, kc == 15, reads=(rn, "ho", pres), writes=(pres,))
            for ti in (tip, tip + 1):
                pp, pres = PBQ[ti % 2]
                tf = tmpf[:, 0:256] if ti % 2 == 0 else tmpf2[:, 0:256]
                tfr = "tmpf" if ti % 2 == 0 else "tmpf2"
                tt(tf, pp[:, 0:256], bc1[:, dj * 256:(dj + 1) * 256], ALU.mult,
                   reads=(pres, "bc1"), writes=(tfr, "qraw", "rotb", pres))
                tt(xres[:, ti, dj * 256:(dj + 1) * 256], xres[:, ti, dj * 256:(dj + 1) * 256],
                   tf, ALU.add, reads=(tfr,), writes=("x%d" % ti,))
    tp = tap("xattn", [TOK, D])
    if tp is not None:
        for i in range(8):
            dma("sync", tp[i * 128:(i + 1) * 128, :], xres[:, i, :], reads=("x%d" % i,))

    if PH == "ALL":
        for hf in range(2):
            dma("gpsimd", aTbv[:, :, :], da1[:, :].rearrange("(a p) b -> p a b", p=128),
                reads=(), writes=("aTb",) + CLOB_B_ARC)
            bcast(bc1[:, :], 4, "bc1")
            bcast(bc2[:, :], 3, "bc2")
            for ti in range(4):
                gi = hf * 4 + ti
                xv = xres[:, gi, :]
                xr = "x%d" % gi
                red(stats[:, 0:1], xv, ALU.add, reads=(xr,), writes=("stats",))
                act(pBig[:, :], xv, AF.Square, accum=stats[:, 1:2],
                    reads=(xr,) + PB_ALL, writes=PB_ALL + ("stats",))
                ts(stats[:, 2:3], stats[:, 0:1], 1.0 / 2048, ALU.mult,
                   reads=("stats",), writes=("stats",))
                tt(stats[:, 3:4], stats[:, 2:3], stats[:, 2:3], ALU.mult,
                   reads=("stats",), writes=("stats",))
                stt(stats[:, 4:5], stats[:, 1:2], 1.0 / 2048, stats[:, 3:4],
                    ALU.mult, ALU.subtract, reads=("stats",), writes=("stats",))
                act(stats[:, 5:6], stats[:, 4:5], AF.Sqrt, bias=csts[:, 0:1],
                    reads=("stats", "csts"), writes=("stats",))
                recip(stats[:, 6:7], stats[:, 5:6], reads=("stats",), writes=("stats",))
                ts(pBig[:, :], xv, stats[:, 2:3], ALU.subtract, stats[:, 6:7], ALU.mult,
                   reads=(xr, "stats") + PB_ALL, writes=PB_ALL)
                tt(htok[:, :], pBig[:, :], bc1[:, :], ALU.mult,
                   reads=PB_ALL + ("bc1",), writes=("htok",) + PB_ALL)
                tt(htok[:, :], htok[:, :], bc2[:, :], ALU.add,
                   reads=("bc2",), writes=("htok",))
                for g2 in range(2):
                    for j in range(8):
                        fc = g2 * 8 + j
                        tr(pH[:, j * 128:(j + 1) * 128], htok[:, fc * 128:(fc + 1) * 128],
                           identb[:, :], reads=("htok", "identb", "pH%d" % j),
                           writes=("pH%d" % j,))
                    P.add("scalar",
                          lambda e, g2=g2, ti=ti:
                          e.copy(h2T[:, g2 * 8:(g2 + 1) * 8, ti * 128:(ti + 1) * 128],
                                 pH[:, :].rearrange("p (a b) -> p a b", a=8)),
                          reads=tuple("pH%d" % k for k in range(8)),
                          writes=("h2T",) + tuple("pH%d" % k for k in range(8))
                          + (("wi0", "wi1") if ti == 0 else ()))
            for daT, uslot, ur in [(da1, 0, "ubt0"), (da3, 1, "ubt1")]:
                if uslot == 1:
                    dma("gpsimd", aTbv[:, :, :], daT[:, :].rearrange("(a p) b -> p a b", p=128),
                        reads=(), writes=("aTb",) + CLOB_B_ARC)
                for kc in range(16):
                    mm(zA[:, :], aTbv[:, kc, :], h2T[:, kc, :], kc == 0, kc == 15,
                       reads=("aTb", "h2T", "zA"), writes=("zA",))
                ts(ubt[:, uslot, :], zA[:, :], 2.0, ALU.mult,
                   reads=("zA",), writes=(ur, "w13s0", "zA") + CLOB_B_ARC)
            dmas("sync", [(u1e[:, e_, :], ubt[e_ * 16:(e_ + 1) * 16, 0, :]) for e_ in range(NE)]
                 + [(u3e[:, e_, :], ubt[e_ * 16:(e_ + 1) * 16, 1, :]) for e_ in range(NE)],
                 reads=("ubt0", "ubt1"), writes=("u1e", "u3e"))
            ts(u1n[:, :, :], u1e[:, :, :], -1.0, ALU.mult, reads=("u1e",), writes=("u1n",))
            ts(u3n[:, :, :], u3e[:, :, :], -1.0, ALU.mult, reads=("u3e",), writes=("u3n",))

            for tip in range(0, 4, 2):
                for kc in range(16):
                    for ti in (tip, tip + 1):
                        zg, zgr = (zB, "zB") if ti % 2 == 0 else (zC, "zC")
                        mm(zg[:, 0:8], h2T[:, kc, ti * 128:(ti + 1) * 128], gateTb[:, kc, :],
                           kc == 0, kc == 15, reads=("h2T", "gateTb", zgr), writes=(zgr,))
                for ti in (tip, tip + 1):
                    zg, zgr = (zB, "zB") if ti % 2 == 0 else (zC, "zC")
                    vcp(lg[:, ti, :], zg[:, 0:8], reads=(zgr,), writes=("lg", zgr))
                for ti in (tip, tip + 1):
                    lv = lg[:, ti, :]
                    red(t2s[:, 0:1], lv, ALU.max, negate=True, reads=("lg",), writes=("t2s",))
                    ts(tmp8[:, :], lv, t2s[:, 0:1], ALU.add, reads=("lg", "t2s"), writes=("tmp8",))
                    ts(msk8[:, :], tmp8[:, :], 0.0, ALU.is_lt, reads=("tmp8",), writes=("msk8",))
                    tt(tmp8[:, :], lv, msk8[:, :], ALU.mult, reads=("lg", "msk8"), writes=("tmp8",))
                    ts(msk8[:, :], msk8[:, :], 1.0, ALU.subtract, reads=(), writes=("msk8",))
                    stt(tmp8[:, :], msk8[:, :], 30000.0, tmp8[:, :], ALU.mult, ALU.add,
                        reads=("msk8",), writes=("tmp8",))
                    red(t2s[:, 1:2], tmp8[:, :], ALU.max, reads=("tmp8",), writes=("t2s",))
                    act(t2s[:, 2:3], t2s[:, 1:2], AF.Exp, bias=t2s[:, 0:1],
                        reads=("t2s",), writes=("t2s",))
                    ts(t2s[:, 3:4], t2s[:, 2:3], 1.0, ALU.add, reads=("t2s",), writes=("t2s",))
                    recip(t2s[:, 4:5], t2s[:, 3:4], reads=("t2s",), writes=("t2s",))
                    for e_ in range(NE):
                        act(t2s[:, 5:6], lv[:, e_:e_ + 1], AF.Exp, bias=t2s[:, 0:1],
                            reads=("lg", "t2s"), writes=("t2s",))
                        ts(t2s[:, 6:7], lv[:, e_:e_ + 1], t2s[:, 1:2], ALU.is_ge,
                           reads=("lg", "t2s"), writes=("t2s",))
                        tt(t2s[:, 7:8], t2s[:, 5:6], t2s[:, 6:7], ALU.mult,
                           reads=("t2s",), writes=("t2s",))
                        tt(wfull[:, ti, e_:e_ + 1], t2s[:, 7:8], t2s[:, 4:5], ALU.mult,
                           reads=("t2s",), writes=("wfull",))
            tpw = tap("wfull", [128, 8, 8])
            if tpw is not None:
                dma("sync", tpw[:, hf * 4:(hf + 1) * 4, :], wfull[:, :, :], reads=("wfull",))
            tpl = tap("lgt", [128, 8, 8])
            if tpl is not None:
                dma("sync", tpl[:, hf * 4:(hf + 1) * 4, :], lg[:, :, :], reads=("lg",))
            for ti in range(4):
                tr(zB[0:8, ti * 128:(ti + 1) * 128], wfull[:, ti, :], identf[:, :],
                   reads=("wfull", "identf", "zB"), writes=("zB",))
            vcp(werows[:, :], zB[0:8, :], reads=("zB",), writes=("werows", "zB"))
            dma("sync", dwe[:, :], werows[:, :], reads=("werows",), writes=("dwe",))
            dmas("gpsimd", [(webc[:, e_, :], dwe[e_:e_ + 1, :].partition_broadcast(128))
                            for e_ in range(NE)],
                 reads=("dwe",), writes=("webc",) + CLOB_B_ARC)

            zts = [zA, zB, zC]
            zrs = ["zA", "zB", "zC"]
            TMP = [(tmpf, "tmpf", ("qraw", "rotb")), (tmpf2, "tmpf2", ())]
            ACTB = [(actb, "actb"), (actb2, "actb2")]
            WACT = [(wactb, "wactb"), (wactb2, "wactb2")]
            ACCF = [(accf, "accf"), (accf2, "accf2")]
            for fcp in range(0, NFC, 2):
                for w in (0, 1):
                    fc = fcp + w
                    sl = fc % 2
                    wr = "w13s%d" % sl
                    dmas("sync", [(w13s[sl][:, q * 4:(q + 1) * 4, :],
                                   dw13b[q * 512:(q + 1) * 512, fc * 256:(fc + 1) * 256]
                                   .rearrange("(a p) b -> p a b", p=128)) for q in range(4)],
                         reads=("dw13b",),
                         writes=(wr,) + (("ubt0", "ubt1") + CLOB_B_ARC if fc < 2 else ()))
                    dmas("gpsimd", [
                        (b13s[0:16, sl, :, :], db13[:, :, 0, fc * 128:(fc + 1) * 128]),
                        (b13s[32:48, sl, :, :], db13[:, :, 1, fc * 128:(fc + 1) * 128]),
                        (b13s[64:80, sl, :, :], db13[:, :, 0, fc * 128:(fc + 1) * 128]),
                        (b13sB[:, sl, :, :], db13[:, :, 1, fc * 128:(fc + 1) * 128]),
                        (a2s[:, sl, :, :], da2[fc * 128:(fc + 1) * 128, :, :]),
                    ], reads=(), writes=("b13s%d" % sl, "a2s%d" % sl))
                for w in (0, 1):
                    fc = fcp + w
                    sl = fc % 2
                    wr = "w13s%d" % sl
                    B1, B1r = PBQ[0 + 2 * w]
                    B3, B3r = PBQ[1 + 2 * w]
                    for kc in range(16):
                        mm(B1, w13s[sl][:, kc, 0:128], h2T[:, kc, :], kc == 0, False,
                           reads=(wr, "h2T", B1r), writes=(B1r,), skip_gc=True)
                        mm(B3, w13s[sl][:, kc, 128:256], h2T[:, kc, :], kc == 0, False,
                           reads=(wr, "h2T", B3r), writes=(B3r,), skip_gc=True)
                def flush_z():
                    while pend_z:
                        zfc, zsl2, ze, zw = pend_z.pop(0)
                        wb_, wbr = WACT[zw]
                        zt = zts[ze // 3]
                        zo = 32 * (ze % 3)
                        mm(zt[zo:zo + 16, :], a2s[:, zsl2, ze, :], wb_[:, :],
                           zfc == 0, zfc == NFC - 1,
                           reads=("a2s%d" % zsl2, wbr, zrs[ze // 3]),
                           writes=(zrs[ze // 3],), skip_gc=True)

                pend_z = []
                for e_ in range(NE):
                    for w in (0, 1):
                        sl = (fcp + w) % 2
                        B1, B1r = PBQ[0 + 2 * w]
                        B3, B3r = PBQ[1 + 2 * w]
                        if e_ > 0:
                            mm(B1, b13s[64:80, sl, e_ - 1, :], u1n[:, e_ - 1, :], False, False,
                               reads=("b13s%d" % sl, "u1n", B1r), writes=(B1r,), skip_gc=True)
                        mm(B3, b13sB[:, sl, e_ - 1, :], u3n[:, e_ - 1, :], False, False,
                           reads=("b13s%d" % sl, "u3n", B3r), writes=(B3r,), skip_gc=True) \
                            if e_ > 0 else None
                        mm(B1, b13s[0:16, sl, e_, :], u1e[:, e_, :], False, e_ == NE - 1,
                           reads=("b13s%d" % sl, "u1e", B1r), writes=(B1r,), skip_gc=True)
                        mm(B3, b13s[32:48, sl, e_, :], u3e[:, e_, :], False, e_ == NE - 1,
                           reads=("b13s%d" % sl, "u3e", B3r), writes=(B3r,), skip_gc=True)
                    flush_z()
                    for w in (0, 1):
                        B1, B1r = PBQ[0 + 2 * w]
                        tf, tfr, tfc = TMP[w]
                        act(tf[:, :], B1, AF.Silu, reads=(B1r,), writes=(tfr,) + tfc + (B1r,))
                    for w in (0, 1):
                        B3, B3r = PBQ[1 + 2 * w]
                        tf, tfr, _ = TMP[w]
                        ab, abr = ACTB[w]
                        wb_, wbr = WACT[w]
                        af, afr = ACCF[w]
                        tt(ab[:, :], tf[:, :], B3, ALU.mult, reads=(tfr, B3r), writes=(abr, B3r))
                        tt(wb_[:, :], ab[:, :], webc[:, e_, :], ALU.mult,
                           reads=(abr, "webc"), writes=(wbr,))
                        if e_ == 0:
                            vcp(af[:, :], wb_[:, :], reads=(wbr,), writes=(afr,))
                        else:
                            tt(af[:, :], af[:, :], wb_[:, :], ALU.add,
                               reads=(wbr,), writes=(afr,))
                    for w in (0, 1):
                        pend_z.append((fcp + w, (fcp + w) % 2, e_, w))
                flush_z()
                for w in (0, 1):
                    fc = fcp + w
                    af, afr = ACCF[w]
                    scp(act_sum[:, fc, :], af[:, :], reads=(afr,),
                        writes=("as%d" % fc,) + (("aTb",) if fc < 2 else ())
                        + (CLOB_B_ARB if hf == 0 and fc < 2 else ()))
            bcast(bc1[:, :], 5, "bc1")
            for e_ in range(NE):
                zt = zts[e_ // 3]
                zo = 32 * (e_ % 3)
                ts(zsb[:, e_, :], zt[zo:zo + 16, :], 2.0, ALU.mult,
                   reads=(zrs[e_ // 3],), writes=("zsb", "u1e", zrs[e_ // 3]))
            for dj in range(4):
                dma("gpsimd", b2dj[:, :, :],
                    db2[:, :, dj * 512:(dj + 1) * 512],
                    reads=(), writes=("b2dj", "w13s0"))
                for fcq in range(11):
                    sl = fcq % 2
                    dmas("scalar", [(w2g[sl][:, j, :],
                                     dw2b[(fcq * 4 + j) * 128:(fcq * 4 + j + 1) * 128,
                                          dj * 512:(dj + 1) * 512]) for j in range(4)],
                         reads=("dw2b",), writes=("w2g%d" % sl, "w13s1"))
                    for j in range(4):
                        fc = fcq * 4 + j
                        for ti in range(4):
                            mm(pBig[:, ti * 512:(ti + 1) * 512],
                               act_sum[:, fc, ti * 128:(ti + 1) * 128], w2g[sl][:, j, :],
                               fc == 0, False,
                               reads=("w2g%d" % sl, "as%d" % fc, "pBig%d" % ti),
                               writes=("pBig%d" % ti,), skip_gc=True)
                for ti in range(4):
                    for e_ in range(NE):
                        mm(pBig[:, ti * 512:(ti + 1) * 512], zsb[:, e_, ti * 128:(ti + 1) * 128],
                           b2dj[:, e_, :], False, e_ == NE - 1,
                           reads=("zsb", "b2dj", "pBig%d" % ti), writes=("pBig%d" % ti,),
                           skip_gc=True)
                for ti in range(4):
                    gi = hf * 4 + ti
                    tt(tmpf[:, :], pBig[:, ti * 512:(ti + 1) * 512],
                       bc1[:, dj * 512:(dj + 1) * 512], ALU.mult,
                       reads=("pBig%d" % ti, "bc1"), writes=("tmpf", "pBig%d" % ti))
                    tt(xres[:, gi, dj * 512:(dj + 1) * 512],
                       xres[:, gi, dj * 512:(dj + 1) * 512], tmpf[:, :], ALU.add,
                       reads=("tmpf",), writes=("x%d" % gi,))

    oop = dmas("sync", [(dout[i * 128:(i + 1) * 128, :], xres[:, i, :]) for i in range(8)],
               reads=tuple("x%d" % i for i in range(8)))
    oop.mark = True

    with (
        nc.semaphore("s_sync") as s0, nc.semaphore("s_gpsimd") as s1,
        nc.semaphore("s_vector") as s2, nc.semaphore("s_scalar") as s3,
        nc.semaphore("s_tensor") as s4, nc.semaphore("s_scalardma") as s5,
        nc.Block() as block,
    ):
        sems = {"sync": s0, "gpsimd": s1, "vector": s2, "scalar": s3, "tensor": s4,
                "scalardma": s5}
        P.emit(block, sems)
    print("total instructions:", len(P.ops))
    return nc


_NC_CACHE = {}


def kernel(**inputs):
    cst = _host_constants()
    in_maps = [_host_core_inputs(inputs, c, cst) for c in range(8)]
    if "nc" not in _NC_CACHE:
        _NC_CACHE["nc"] = _build()
    nc = _NC_CACHE["nc"]
    res = run_bass_kernel_spmd(nc, in_maps, core_ids=list(range(8)))
    out = np.empty((4, 2048, 2048), dtype=np.float32)
    for c in range(8):
        b, hf = c // 2, c % 2
        out[b, hf * TOK:(hf + 1) * TOK, :] = np.asarray(res.results[c]["out"])
    return out


# version 17
# speedup vs baseline: 1.0400x; 1.0400x over previous
import numpy as np
import ml_dtypes
import concourse.bass as bass
import concourse.mybir as mybir
from concourse.bass_utils import run_bass_kernel_spmd

F32 = mybir.dt.float32
BF16 = mybir.dt.bfloat16
AF = mybir.ActivationFunctionType
ALU = mybir.AluOpType
AX = mybir.AxisListType
bfdt = ml_dtypes.bfloat16

U42 = np.array([
 [0.6102175712585449, 0.6966079473495483, 0.06568920612335205, 0.6029453277587891, 0.4143662452697754, 0.24911487102508545, 0.5782934427261353, 0.24924850463867188, 0.959638237953186, 0.8191142082214355, 0.3645491600036621, 0.059471845626831055, 0.9161832332611084, 0.6182148456573486, 0.8282822370529175, 0.6976723670959473],
 [0.756269097328186, 0.3594318628311157, 0.4266716241836548, 0.477594256401062, 0.08340179920196533, 0.40768611431121826, 0.6278458833694458, 0.3251793384552002, 0.35155820846557617, 0.15171575546264648, 0.5589374303817749, 0.8866020441055298, 0.3201930522918701, 0.12432277202606201, 0.6967982053756714, 0.8069779872894287],
 [0.6127028465270996, 0.7509020566940308, 0.4151843786239624, 0.4897202253341675, 0.11665701866149902, 0.40220367908477783, 0.08376991748809814, 0.9528622627258301, 0.9559504985809326, 0.06945085525512695, 0.20124602317810059, 0.06397080421447754, 0.8091422319412231, 0.31807422637939453, 0.639366865158081, 0.8050472736358643],
 [0.5085939168930054, 0.5088651180267334, 0.9745233058929443, 0.3331362009048462, 0.05924034118652344, 0.7208822965621948, 0.3291349411010742, 0.5319250822067261, 0.4839836359024048, 0.643956184387207, 0.19713079929351807, 0.1282445192337036, 0.3164334297180176, 0.4948718547821045, 0.9159883260726929, 0.781721830368042],
], dtype=np.float32)

T = 2048; D = 2048; H = 16; DH = 128
KEEP = np.concatenate([np.arange(204), np.arange(1229, 2048)])
NKC = 8
TOK = 1024; HTOK = 512
Fd = 5632; NE = 8; NR = 16; NFC = 44
EVK = 204
MBEG, MEND = 205, 1023
PH = "ALL"
DEBUG_TAPS = ()

_ENGS = ("sync", "gpsimd", "vector", "scalar", "tensor", "scalardma", "cc")


class _Op:
    __slots__ = ("eng", "fn", "waits", "amt", "gi", "dma", "mark", "cum", "nsub", "chain")

    def __init__(self, eng, fn, waits, dma, gi, nsub=1, chain=None):
        self.eng = eng; self.fn = fn; self.waits = waits
        self.dma = dma; self.nsub = nsub
        self.amt = 16 * nsub if dma else 1
        self.gi = gi; self.mark = False; self.cum = 0
        self.chain = chain


class _Prog:
    def __init__(self):
        self.ops = []
        self.lw = {}
        self.lr = {}
        self.last_dma = {}

    def add(self, eng, fn, reads=(), writes=(), dma=False, nsub=1, chain=None):
        waits = []
        if dma:
            prev = self.last_dma.get(eng)
            if prev is not None:
                waits.append(prev)
        for r in reads:
            w = self.lw.get(r)
            if w is not None:
                waits.append(w)
        for r in writes:
            w = self.lw.get(r)
            if w is not None:
                waits.append(w)
            waits.extend(self.lr.get(r, ()))
        op = _Op(eng, fn, waits, dma, len(self.ops), nsub, chain)
        self.ops.append(op)
        if dma:
            self.last_dma[eng] = op
        for r in reads:
            self.lr.setdefault(r, []).append(op)
        for r in writes:
            self.lw[r] = op
            self.lr[r] = []
        return op

    @staticmethod
    def _need(w, op):
        return True

    def emit(self, block, sems):
        for op in self.ops:
            if op.dma:
                op.mark = True
            best = {}
            for w in op.waits:
                if self._need(w, op) and (w.eng not in best or w.gi > best[w.eng].gi):
                    best[w.eng] = w
            op.waits = list(best.values())
            for w in op.waits:
                w.mark = True
        cnt = {e: 0 for e in _ENGS}
        for op in self.ops:
            if op.mark:
                cnt[op.eng] += op.amt
            op.cum = cnt[op.eng]
        ops = self.ops

        def mkbody(name):
            group = ("scalar", "scalardma") if name == "scalar" else \
                    (("gpsimd", "cc") if name == "gpsimd" else (name,))
            def body(e):
                hi = {en: 0 for en in _ENGS}
                for op in ops:
                    if op.eng not in group:
                        continue
                    for w in op.waits:
                        if not self._need(w, op):
                            continue
                        if w.cum > hi[w.eng]:
                            e.wait_ge(sems[w.eng], w.cum)
                            hi[w.eng] = w.cum
                    inst = op.fn(e)
                    if op.mark:
                        if isinstance(inst, list):
                            for ii in inst:
                                ii.then_inc(sems[op.eng], 16)
                        else:
                            inst.then_inc(sems[op.eng], op.amt)
                for gn in group:
                    if cnt[gn] > hi[gn]:
                        e.wait_ge(sems[gn], cnt[gn])
            return body
        block.sync(mkbody("sync"))
        block.gpsimd(mkbody("gpsimd"))
        block.vector(mkbody("vector"))
        block.scalar(mkbody("scalar"))
        block.tensor(mkbody("tensor"))


def _host_constants():
    inv = 1.0 / (10000.0 ** (np.arange(0, DH, 2, dtype=np.float32) / DH))
    fr = np.arange(T, dtype=np.float32)[:, None] * inv[None, :]
    emb = np.concatenate([fr, fr], 1)
    cosT = np.cos(emb).astype(np.float32)
    sinT = np.sin(emb).astype(np.float32)
    s = np.float32(1.0 / np.sqrt(np.float32(DH)))
    consts = np.tile(np.array([[1e-5, 1e-6, 1.0 / 818.0, 1.0 / 819.0, 0.25, 1.0, 0.0, 0.0]],
                              dtype=np.float32), (128, 1))
    prot = np.zeros((128, 128), dtype=np.float32)
    for d2 in range(64):
        prot[d2 + 64, d2] = -1.0
    for d2 in range(64, 128):
        prot[d2 - 64, d2] = 1.0
    mrg = np.zeros((128, NKC, 3), dtype=np.float32)
    for kc in range(NKC):
        for p in range(128):
            ki = kc * 128 + p
            if ki >= 1023:
                continue
            mrg[p, kc, 2] = 1.0
            if MBEG <= ki < MEND:
                mrg[p, kc, 0] = 1.0
            if ki == EVK:
                mrg[p, kc, 1] = 1.0
    return dict(cosT=cosT, sinT=sinT, qscale=s, consts=consts,
                identb=np.eye(128, dtype=bfdt), identf=np.eye(128, dtype=np.float32),
                prot=prot.astype(bfdt), onesrow=np.ones((1, 128), dtype=bfdt),
                mrg=mrg.astype(bfdt))


def _host_core_inputs(inputs, c, cst):
    b, hf = c // 2, c % 2
    X = np.asarray(inputs["L0_attn_hidden_states"])
    x_own = np.ascontiguousarray(X[b, hf * TOK:(hf + 1) * TOK])
    xk = np.zeros((1024, D), dtype=np.float32)
    xk[:1023] = X[b, KEEP]
    ownpos = np.arange(hf * TOK, hf * TOK + TOK)
    kpos = np.zeros(1024, dtype=np.int64); kpos[:1023] = KEEP
    cosq = np.ascontiguousarray((cst["cosT"][ownpos] * cst["qscale"]).T.astype(bfdt))
    sinq = np.ascontiguousarray((cst["sinT"][ownpos] * cst["qscale"]).T.astype(bfdt))
    cosk = np.ascontiguousarray(cst["cosT"][kpos].T.astype(bfdt))
    sink = np.ascontiguousarray(cst["sinT"][kpos].T.astype(bfdt))
    cosk[:, 1023] = 0; sink[:, 1023] = 0
    m_t = np.zeros((128, NKC), dtype=np.float32)
    if hf == 1:
        for kc in range(NKC):
            for p in range(128):
                ki = kc * 128 + p
                if MBEG <= ki < MEND:
                    m_t[p, kc] = 1.0
    w1T = np.asarray(inputs["L0_ffn_w1"]).T
    w3T = np.asarray(inputs["L0_ffn_w3"]).T
    w13 = np.empty((D, NFC * 256), dtype=np.float32)
    for fc in range(NFC):
        w13[:, fc * 256:fc * 256 + 128] = w1T[:, fc * 128:(fc + 1) * 128]
        w13[:, fc * 256 + 128:fc * 256 + 256] = w3T[:, fc * 128:(fc + 1) * 128]
    b1 = np.asarray(inputs["L0_ffn_b1"]); b3 = np.asarray(inputs["L0_ffn_b3"])
    b13 = np.ascontiguousarray(
        np.stack([b1.transpose(0, 2, 1), b3.transpose(0, 2, 1)], axis=2).transpose(1, 0, 2, 3))
    return dict(
        x_own=x_own, x_kept=np.ascontiguousarray(xk),
        wqT=np.ascontiguousarray(np.asarray(inputs["L0_attn_Wq"]).T),
        wkT=np.ascontiguousarray(np.asarray(inputs["L0_attn_Wk"]).T),
        wvT=np.ascontiguousarray(np.asarray(inputs["L0_attn_Wv"]).T),
        woT=np.ascontiguousarray(np.asarray(inputs["L0_attn_Wo"]).T),
        wmod=np.ascontiguousarray(np.asarray(inputs["L0_wmod"])),
        gateT=np.ascontiguousarray(np.asarray(inputs["L0_ffn_gate_w"]).T),
        w13=w13,
        w2T=np.ascontiguousarray(np.asarray(inputs["L0_ffn_w2"]).T),
        b13=b13,
        a2il=np.ascontiguousarray(np.asarray(inputs["L0_ffn_a2"]).transpose(2, 0, 1)),
        b2il=np.ascontiguousarray(np.asarray(inputs["L0_ffn_b2"]).transpose(2, 0, 1)),
        a1T=np.ascontiguousarray(np.asarray(inputs["L0_ffn_a1"]).reshape(NE * NR, D).T),
        a3T=np.ascontiguousarray(np.asarray(inputs["L0_ffn_a3"]).reshape(NE * NR, D).T),
        c_col=np.ascontiguousarray(np.asarray(inputs["L0_c"])[b].reshape(16, 128).T),
        u_own=np.ascontiguousarray(U42[b:b + 1]),
        cosq=cosq, sinq=sinq, cosk=cosk, sink=sink, m_t=m_t,
        consts=cst["consts"], identb=cst["identb"], identf=cst["identf"],
        prot=cst["prot"], onesrow=cst["onesrow"], mrg=cst["mrg"],
    )


def _build():
    nc = bass.Bass("TRN2", target_bir_lowering=False)
    P = _Prog()

    dx = nc.dram_tensor("x_own", [TOK, D], F32, kind="ExternalInput")
    dxk = nc.dram_tensor("x_kept", [1024, D], F32, kind="ExternalInput")
    dwq = nc.dram_tensor("wqT", [D, D], F32, kind="ExternalInput")
    dwk = nc.dram_tensor("wkT", [D, D], F32, kind="ExternalInput")
    dwv = nc.dram_tensor("wvT", [D, D], F32, kind="ExternalInput")
    dwo = nc.dram_tensor("woT", [D, D], F32, kind="ExternalInput")
    dwmod = nc.dram_tensor("wmod", [D, 12288], F32, kind="ExternalInput")
    dgate = nc.dram_tensor("gateT", [D, 8], F32, kind="ExternalInput")
    dw13 = nc.dram_tensor("w13", [D, NFC * 256], F32, kind="ExternalInput")
    dw2 = nc.dram_tensor("w2T", [Fd, D], F32, kind="ExternalInput")
    db13 = nc.dram_tensor("b13", [NR, NE, 2, Fd], F32, kind="ExternalInput")
    da2 = nc.dram_tensor("a2il", [Fd, NE, NR], F32, kind="ExternalInput")
    db2 = nc.dram_tensor("b2il", [NR, NE, D], F32, kind="ExternalInput")
    da1 = nc.dram_tensor("a1T", [D, 128], F32, kind="ExternalInput")
    da3 = nc.dram_tensor("a3T", [D, 128], F32, kind="ExternalInput")
    dc = nc.dram_tensor("c_col", [128, 16], F32, kind="ExternalInput")
    du = nc.dram_tensor("u_own", [1, 16], F32, kind="ExternalInput")
    dcosq = nc.dram_tensor("cosq", [128, 1024], BF16, kind="ExternalInput")
    dsinq = nc.dram_tensor("sinq", [128, 1024], BF16, kind="ExternalInput")
    dcosk = nc.dram_tensor("cosk", [128, 1024], BF16, kind="ExternalInput")
    dsink = nc.dram_tensor("sink", [128, 1024], BF16, kind="ExternalInput")
    dconsts = nc.dram_tensor("consts", [128, 8], F32, kind="ExternalInput")
    didb = nc.dram_tensor("identb", [128, 128], BF16, kind="ExternalInput")
    didf = nc.dram_tensor("identf", [128, 128], F32, kind="ExternalInput")
    dprot = nc.dram_tensor("prot", [128, 128], BF16, kind="ExternalInput")
    dones = nc.dram_tensor("onesrow", [1, 128], BF16, kind="ExternalInput")
    dmrg = nc.dram_tensor("mrg", [128, NKC, 3], BF16, kind="ExternalInput")
    dmt = nc.dram_tensor("m_t", [128, NKC], F32, kind="ExternalInput")
    dout = nc.dram_tensor("out", [TOK, D], F32, kind="ExternalOutput")
    dscrM = nc.dram_tensor("dscrM", [1, 12288], F32)
    dsum = nc.dram_tensor("dsum", [1, 12288], F32)
    dscrP = nc.dram_tensor("dscrP", [1, 12288], F32)
    dwe = nc.dram_tensor("dwe", [NE, HTOK], BF16)
    dw13b = nc.dram_tensor("dw13b", [D, NFC * 256], BF16)
    dw2b = nc.dram_tensor("dw2b", [Fd, D], BF16)

    taps = {}

    def tap(name, shape, dtype=F32):
        if name in DEBUG_TAPS:
            if name not in taps:
                taps[name] = nc.dram_tensor("tap_" + name, shape, dtype, kind="ExternalOutput")
            return taps[name]
        return None

    arA = nc.alloc_sbuf_tensor("arA", [128, 16384], F32)
    arB = nc.alloc_sbuf_tensor("arB", [128, 15360], F32)
    arC = nc.alloc_sbuf_tensor("arC", [128, 8192], F32)
    arA_bf = arA[:, :].bitcast(BF16)
    arB_bf = arB[:, :].bitcast(BF16)
    arC_bf = arC[:, :].bitcast(BF16)

    def carve(base, off, n, lead=None):
        v = base[:, off:off + n]
        if lead is not None:
            v = v.rearrange("p (a b) -> p a b", a=lead)
        return v

    hq = carve(arA_bf, 0, 16 * 1024, 16)
    hk = carve(arA_bf, 16 * 1024, 16 * 1024, 16)
    xres = arA[:, :].rearrange("p (a b) -> p a b", a=8)

    o = 0
    qTg = carve(arB_bf, o, 4096, 4); o += 4096
    kTg = carve(arB_bf, o, 4096, 4); o += 4096
    vbuf = carve(arB_bf, o, 8 * 4 * 129).rearrange("p (k h d) -> p k h d", k=8, h=4)
    o += 8 * 4 * 129
    pT7 = carve(arB_bf, o, 1024); cosq_s = pT7; o += 1024
    pTc = carve(arB_bf, o, 1024); sinq_s = pTc; o += 1024
    pT6 = carve(arB_bf, o, 1024); o += 1024
    probsq = carve(arB_bf, o, 1024); o += 1024
    wbufA = carve(arB_bf, o, 4096, 16); o += 4096
    wbufB = carve(arB_bf, o, 4096, 16); o += 4096
    assert o <= 30720, o
    act_sum = carve(arB_bf, 0, NFC * 512, NFC)
    h2T = carve(arB_bf, NFC * 512, 16 * 512, 16)
    assert NFC * 512 + 16 * 512 <= 30720

    xkb = [arC[:, 0:2048], arC[:, 2048:4096]]
    ho = carve(arC_bf, 0, 16384, 16)
    o = 0
    webc = carve(arC_bf, o, 8 * 512, 8); o += 8 * 512
    w13s = [carve(arC_bf, o, 4096, 16), carve(arC_bf, o + 4096, 4096, 16)]
    o += 8192
    w2s = [carve(arC_bf, o, 512), carve(arC_bf, o + 512, 512)]
    o += 1024
    accf = arC[:, (o // 2):(o // 2) + 512]
    o += 1024
    aTbv = carve(arC_bf, o, 2048, 16)
    accf2 = arC[:, (o // 2):(o // 2) + 512]
    o += 2048
    assert o <= 16384
    w2g = [arC_bf[:, 8192:10240].rearrange("p (a b) -> p a b", a=4),
           arC_bf[:, 10240:12288].rearrange("p (a b) -> p a b", a=4)]

    CLOB_B_ARB = ("qTg", "kTg", "vbuf", "r1a", "r1b", "r1c", "r1d", "wi0", "wi1")
    CLOB_B_ARC = ("ho", "xk0", "xk1")

    htok = nc.alloc_sbuf_tensor("htok", [128, 2048], BF16)
    bc1 = nc.alloc_sbuf_tensor("bc1", [128, 2048], BF16)
    bc2 = nc.alloc_sbuf_tensor("bc2", [128, 2048], BF16)
    stage = arB[0:1, 12304:12816]
    rowA = arB[0:1, 12816:13328]
    rowB = arB[0:1, 13328:13840]
    wstf = arB[:, 0:4096].rearrange("p (a b) -> p a b", a=16)
    c_sb = nc.alloc_sbuf_tensor("c_sb", [128, 16], F32)
    scsb = nc.alloc_sbuf_tensor("scsb", [128, 16], BF16)
    csts = nc.alloc_sbuf_tensor("csts", [128, 8], F32)
    identb = nc.alloc_sbuf_tensor("identb_s", [128, 128], BF16)
    identf = nc.alloc_sbuf_tensor("identf_s", [128, 128], F32)
    prot = nc.alloc_sbuf_tensor("prot_s", [128, 128], BF16)
    onesr = nc.alloc_sbuf_tensor("onesr", [1, 128], BF16)
    mrg = nc.alloc_sbuf_tensor("mrg_s", [128, NKC, 3], BF16)
    m_t = nc.alloc_sbuf_tensor("mt_s", [128, NKC], F32)
    u_sb = nc.alloc_sbuf_tensor("u_sb", [1, 16], F32)
    tmpf = nc.alloc_sbuf_tensor("tmpf", [128, 512], F32)
    qraw = tmpf[:, 0:256].bitcast(BF16)
    rotb = tmpf[:, 256:512].bitcast(BF16)
    qcos = nc.alloc_sbuf_tensor("qcos", [128, 512], BF16)
    qcos2 = nc.alloc_sbuf_tensor("qcos2", [128, 512], BF16)
    ca1col = nc.alloc_sbuf_tensor("ca1col", [128, 16], F32)
    sacol = nc.alloc_sbuf_tensor("sacol", [128, 16], F32)
    stats = nc.alloc_sbuf_tensor("stats", [128, 16], F32)
    nmx = nc.alloc_sbuf_tensor("nmx", [128, 2], F32)
    zvv = nc.alloc_sbuf_tensor("zvv", [128, 2], F32)
    hot = nc.alloc_sbuf_tensor("hot", [128, 128], F32)
    mz = nc.alloc_sbuf_tensor("mz", [128, 4], F32)
    mrow = nc.alloc_sbuf_tensor("mrow", [1, 8], F32)
    bmt = nc.alloc_sbuf_tensor("bmt", [1, 8], F32)
    v204 = nc.alloc_sbuf_tensor("v204", [1, 128], BF16)
    vescl = nc.alloc_sbuf_tensor("vescl", [1, 128], BF16)
    gateTb = nc.alloc_sbuf_tensor("gateTb", [128, 16, 8], BF16)
    upack = nc.alloc_sbuf_tensor("upack", [128, NE * HTOK], BF16)
    u1e = upack[0:16, :].rearrange("p (a b) -> p a b", a=NE)
    u3e = upack[32:48, :].rearrange("p (a b) -> p a b", a=NE)
    u1n = upack[64:80, :].rearrange("p (a b) -> p a b", a=NE)
    u3n = nc.alloc_sbuf_tensor("u3n", [NR, NE, HTOK], BF16)
    zsb = u1e
    ubt = arC_bf[:, 4096:5120].rearrange("p (a b) -> p a b", a=2)
    werows = nc.alloc_sbuf_tensor("werows", [NE, HTOK], BF16)
    wfull = nc.alloc_sbuf_tensor("wfull", [128, 4, NE], F32)
    lg = nc.alloc_sbuf_tensor("lg", [128, 4, NE], F32)
    t2s = nc.alloc_sbuf_tensor("t2s", [128, 8], F32)
    tmp8 = nc.alloc_sbuf_tensor("tmp8", [128, NE], F32)
    msk8 = nc.alloc_sbuf_tensor("msk8", [128, NE], F32)
    actb = nc.alloc_sbuf_tensor("actb", [128, 512], BF16)
    wactb = nc.alloc_sbuf_tensor("wactb", [128, 512], BF16)
    tmpf2 = nc.alloc_sbuf_tensor("tmpf2", [128, 512], F32)
    qraw2 = tmpf2[:, 0:256].bitcast(BF16)
    rotb2 = tmpf2[:, 256:512].bitcast(BF16)
    actb2 = nc.alloc_sbuf_tensor("actb2", [128, 512], BF16)
    wactb2 = nc.alloc_sbuf_tensor("wactb2", [128, 512], BF16)
    b13s = nc.alloc_sbuf_tensor("b13s", [80, 2, NE, 128], BF16)
    b13sB = nc.alloc_sbuf_tensor("b13sB", [NR, 2, NE, 128], BF16)
    a2s = nc.alloc_sbuf_tensor("a2s", [128, 2, NE, NR], BF16)
    winit = [wbufA, wbufB]
    coskS = pT6
    sinkS = probsq
    b2dj = arC_bf[0:16, 4096:8192].rearrange("p (a b) -> p a b", a=NE)

    pBig = nc.alloc_psum_tensor("pBig", [128, 2048], F32)
    pH = nc.alloc_psum_tensor("pH", [128, 1024], BF16)
    zA = nc.alloc_psum_tensor("zA", [128, 512], F32)
    zB = nc.alloc_psum_tensor("zB", [128, 512], F32)
    zC = nc.alloc_psum_tensor("zC", [128, 512], F32)
    print("sbuf bytes remaining:", nc.sbuf_bytes_remaining,
          "psum banks remaining:", nc.psum_banks_remaining)
    PBQ = [(pBig[:, i * 512:(i + 1) * 512], "pBig%d" % i) for i in range(4)]
    PB_ALL = ("pBig0", "pBig1", "pBig2", "pBig3")

    def dma(eng, out, in_, reads=(), writes=()):
        if eng == "scalar":
            eng = "scalardma"
        return P.add(eng, lambda e, oo=out, ii=in_: e.dma_start(out=oo, in_=ii),
                     reads=reads, writes=writes, dma=True)

    def dmas(eng, pairs, reads=(), writes=()):
        if eng == "scalar":
            eng = "scalardma"
        def f(e, ps=tuple(pairs)):
            return [e.dma_start(out=oo, in_=ii) for (oo, ii) in ps]
        return P.add(eng, f, reads=reads, writes=writes, dma=True, nsub=len(pairs))

    def mm(out, lhsT, rhs, start, stop, reads=(), writes=(), skip_gc=False):
        ch = writes[0] if writes else None
        return P.add("tensor",
                     lambda e, oo=out, l=lhsT, r=rhs, st=start, sp=stop, sg=skip_gc:
                     e.matmul(oo, l, r, start=st, stop=sp, skip_group_check=sg),
                     reads=reads, writes=writes, chain=ch)

    def tr(out, in_, ident, reads=(), writes=()):
        return P.add("tensor",
                     lambda e, oo=out, ii=in_, dd=ident: e.transpose(oo, ii, dd),
                     reads=reads, writes=writes)

    def act(out, in_, func, bias=None, scale=None, accum=None, reads=(), writes=()):
        kw = {}
        if bias is not None:
            kw["bias"] = bias
        if scale is not None:
            kw["scale"] = scale
        if accum is not None:
            kw["accum_out"] = accum
        return P.add("scalar",
                     lambda e, oo=out, ii=in_, f=func, kw=kw: e.activation(oo, ii, f, **kw),
                     reads=reads, writes=writes)

    def scp(out, in_, reads=(), writes=()):
        return P.add("scalar", lambda e, oo=out, ii=in_: e.copy(oo, ii),
                     reads=reads, writes=writes)

    def vcp(out, in_, reads=(), writes=()):
        return P.add("vector", lambda e, oo=out, ii=in_: e.tensor_copy(oo, ii),
                     reads=reads, writes=writes)

    def tt(out, a, b, op, reads=(), writes=()):
        return P.add("vector", lambda e, oo=out, x=a, y=b, p=op: e.tensor_tensor(oo, x, y, p),
                     reads=reads, writes=writes)

    def ts(out, in_, s1, op0, s2=None, op1=ALU.bypass, reads=(), writes=()):
        return P.add("vector",
                     lambda e, oo=out, ii=in_, a=s1, b=s2, p0=op0, p1=op1:
                     e.tensor_scalar(oo, ii, a, b, op0=p0, op1=p1),
                     reads=reads, writes=writes)

    def stt(out, in0, scalar, in1, op0, op1, reads=(), writes=()):
        return P.add("vector",
                     lambda e, oo=out, ii=in0, s=scalar, jj=in1, p0=op0, p1=op1:
                     e.scalar_tensor_tensor(oo, ii, s, jj, op0=p0, op1=p1),
                     reads=reads, writes=writes)

    def red(out, in_, op, negate=False, reads=(), writes=()):
        return P.add("vector",
                     lambda e, oo=out, ii=in_, p=op, n=negate:
                     e.tensor_reduce(oo, ii, axis=AX.X, op=p, negate=n),
                     reads=reads, writes=writes)

    def recip(out, in_, reads=(), writes=()):
        return P.add("vector", lambda e, oo=out, ii=in_: e.reciprocal(oo, ii),
                     reads=reads, writes=writes)

    def memset(out, val, reads=(), writes=()):
        return P.add("vector", lambda e, oo=out, v=val: e.memset(oo, v),
                     reads=reads, writes=writes)

    _smalls = [
        (c_sb[:, :], dc[:, :], "c_sb"), (csts[:, :], dconsts[:, :], "csts"),
        (identb[:, :], didb[:, :], "identb"), (identf[:, :], didf[:, :], "identf"),
        (prot[:, :], dprot[:, :], "prot"), (onesr[:, :], dones[:, :], "onesr"),
        (mrg[:, :, :], dmrg[:, :, :], "mrg"), (m_t[:, :], dmt[:, :], "m_t"),
        (u_sb[:, :], du[:, :], "u_sb"),
    ]
    dmas("sync", [(d0, s0) for (d0, s0, _) in _smalls],
         writes=tuple(nm for (_, _, nm) in _smalls))
    act(scsb[:, :], c_sb[:, :], AF.Silu, reads=("c_sb",), writes=("scsb",))
    for njp in range(0, 48, 2):
        for s in (0, 1):
            nj = njp + s
            dmas("gpsimd", [(winit[s][:, q * 4:(q + 1) * 4, :],
                             dwmod[q * 512:(q + 1) * 512, nj * 256:(nj + 1) * 256]
                             .rearrange("(a p) b -> p a b", p=128)) for q in range(4)],
                 reads=(), writes=("wi%d" % s,))
        for kc in range(16):
            for s in (0, 1):
                zsl, zslr = (zA, "zA") if s == 0 else (zB, "zB")
                mm(zsl[0:1, 0:256], scsb[:, kc:kc + 1], winit[s][:, kc, 0:256],
                   kc == 0, kc == 15, reads=("scsb", "wi%d" % s), writes=(zslr,))
        for s in (0, 1):
            nj = njp + s
            zsl, zslr = (zA, "zA") if s == 0 else (zB, "zB")
            stg = stage[0:1, 0:256] if s == 0 else stage[0:1, 256:512]
            stgr = "stage" if s == 0 else "stage2"
            P.add("scalar",
                  lambda e, zsl=zsl, stg=stg: e.copy(stg, zsl[0:1, 0:256]),
                  reads=(zslr,), writes=(stgr, zslr))
            dma("sync", dscrM[0:1, nj * 256:(nj + 1) * 256], stg,
                reads=(stgr,), writes=("dscrM",))
    P.add("sync", lambda e: e.dma_start(out=dsum[0:1, :], in_=dscrM[0:1, :]),
          reads=("dscrM",), writes=("dsum",), dma=True)

    for slot in (1, 4):
        for ch in range(4):
            off = slot * 2048 + ch * 512
            dma("sync", rowA[0:1, :], dsum[0:1, off:off + 512],
                reads=("dsum",), writes=("rowA",))
            ts(rowB[0:1, :], rowA[0:1, :], 1.0, ALU.add, reads=("rowA",), writes=("rowB",))
            dma("sync", dscrP[0:1, off:off + 512], rowB[0:1, :],
                reads=("rowB",), writes=("dscrP",))
    for slot in (2, 5):
        for ch in range(4):
            off = slot * 2048 + ch * 512
            dma("sync", rowA[0:1, :], dsum[0:1, off:off + 512],
                reads=("dsum",), writes=("rowA",))
            P.add("scalar", lambda e: e.activation(rowB[0:1, :], rowA[0:1, :], AF.Sign),
                  reads=("rowA",), writes=("rowB",))
            P.add("scalar", lambda e: e.activation(rowA[0:1, :], rowA[0:1, :], AF.Abs),
                  reads=("rowA",), writes=("rowA",))
            ts(rowA[0:1, :], rowA[0:1, :], 0.25, ALU.max, reads=("rowA",), writes=("rowA",))
            tt(rowB[0:1, :], rowB[0:1, :], rowA[0:1, :], ALU.mult,
               reads=("rowA",), writes=("rowB",))
            dma("sync", dscrP[0:1, off:off + 512], rowB[0:1, :],
                reads=("rowB",), writes=("dscrP",))

    def bcast(dst, slot, name):
        dsrc = dsum if slot in (0, 3) else dscrP
        rn = "dsum" if slot in (0, 3) else "dscrP"
        return dma("gpsimd", dst,
                   dsrc[0:1, slot * 2048:(slot + 1) * 2048].partition_broadcast(128),
                   reads=(rn,), writes=(name,))

    bcast(bc1[:, :], 1, "bc1")

    def ln_norm_tile(i, src_dram, row0, dstview, dstname, dst_ti):
        xb = xkb[i % 2]
        xr = "xk%d" % (i % 2)
        dma("scalar", xb[:, :], src_dram[row0:row0 + 128, :], reads=(), writes=(xr,))
        sc0 = 8 * (i % 2)
        red(stats[:, sc0:sc0 + 1], xb[:, :], ALU.add, reads=(xr,), writes=("stats%d" % (i % 2),))
        act(pBig[:, :], xb[:, :], AF.Square, accum=stats[:, sc0 + 1:sc0 + 2],
            reads=(xr,) + PB_ALL, writes=PB_ALL + ("stats%d" % (i % 2),))
        ts(stats[:, sc0 + 2:sc0 + 3], stats[:, sc0:sc0 + 1], 1.0 / 2048, ALU.mult,
           reads=("stats%d" % (i % 2),), writes=("stats%d" % (i % 2),))
        tt(stats[:, sc0 + 3:sc0 + 4], stats[:, sc0 + 2:sc0 + 3], stats[:, sc0 + 2:sc0 + 3],
           ALU.mult, reads=("stats%d" % (i % 2),), writes=("stats%d" % (i % 2),))
        stt(stats[:, sc0 + 4:sc0 + 5], stats[:, sc0 + 1:sc0 + 2], 1.0 / 2048,
            stats[:, sc0 + 3:sc0 + 4], ALU.mult, ALU.subtract,
            reads=("stats%d" % (i % 2),), writes=("stats%d" % (i % 2),))
        act(stats[:, sc0 + 5:sc0 + 6], stats[:, sc0 + 4:sc0 + 5], AF.Sqrt, bias=csts[:, 0:1],
            reads=("stats%d" % (i % 2), "csts"), writes=("stats%d" % (i % 2),))
        recip(stats[:, sc0 + 6:sc0 + 7], stats[:, sc0 + 5:sc0 + 6],
              reads=("stats%d" % (i % 2),), writes=("stats%d" % (i % 2),))
        hb, hbr = (htok, "htok") if i % 2 == 0 else (bc2, "bc2n")
        ts(hb[:, :], xb[:, :], stats[:, sc0 + 2:sc0 + 3], ALU.subtract,
           stats[:, sc0 + 6:sc0 + 7], ALU.mult,
           reads=(xr, "stats%d" % (i % 2)), writes=(hbr,))
        for g2 in range(2):
            for j in range(8):
                fc = g2 * 8 + j
                tr(pH[:, j * 128:(j + 1) * 128], hb[:, fc * 128:(fc + 1) * 128], identb[:, :],
                   reads=(hbr, "identb", "pH%d" % j), writes=("pH%d" % j,))
            P.add("scalar",
                  lambda e, dv=dstview, g2=g2, ti=dst_ti:
                  e.copy(dv[:, g2 * 8:(g2 + 1) * 8, ti * 128:(ti + 1) * 128],
                         pH[:, :].rearrange("p (a b) -> p a b", a=8)),
                  reads=tuple("pH%d" % k for k in range(8)),
                  writes=(dstname,) + tuple("pH%d" % k for k in range(8)))

    def ln_mod_tile(i, src_dram, row0, dstview, dstname, dst_ti):
        xb = xkb[i % 2]
        xr = "xk%d" % (i % 2)
        dma("sync", xb[:, :], src_dram[row0:row0 + 128, :], reads=(), writes=(xr,))
        red(stats[:, 0:1], xb[:, :], ALU.add, reads=(xr,), writes=("stats",))
        act(pBig[:, :], xb[:, :], AF.Square, accum=stats[:, 1:2],
            reads=(xr,) + PB_ALL, writes=PB_ALL + ("stats",))
        ts(stats[:, 2:3], stats[:, 0:1], 1.0 / 2048, ALU.mult, reads=("stats",), writes=("stats",))
        tt(stats[:, 3:4], stats[:, 2:3], stats[:, 2:3], ALU.mult, reads=("stats",), writes=("stats",))
        stt(stats[:, 4:5], stats[:, 1:2], 1.0 / 2048, stats[:, 3:4], ALU.mult, ALU.subtract,
            reads=("stats",), writes=("stats",))
        act(stats[:, 5:6], stats[:, 4:5], AF.Sqrt, bias=csts[:, 0:1],
            reads=("stats", "csts"), writes=("stats",))
        recip(stats[:, 6:7], stats[:, 5:6], reads=("stats",), writes=("stats",))
        ts(pBig[:, :], xb[:, :], stats[:, 2:3], ALU.subtract, stats[:, 6:7], ALU.mult,
           reads=(xr, "stats") + PB_ALL, writes=PB_ALL)
        tt(htok[:, :], pBig[:, :], bc1[:, :], ALU.mult,
           reads=PB_ALL + ("bc1",), writes=("htok",) + PB_ALL)
        tt(htok[:, :], htok[:, :], bc2[:, :], ALU.add, reads=("bc2",), writes=("htok",))
        for g2 in range(2):
            for j in range(8):
                fc = g2 * 8 + j
                tr(pH[:, j * 128:(j + 1) * 128], htok[:, fc * 128:(fc + 1) * 128], identb[:, :],
                   reads=("htok", "identb", "pH%d" % j), writes=("pH%d" % j,))
            P.add("scalar",
                  lambda e, dv=dstview, g2=g2, ti=dst_ti:
                  e.copy(dv[:, g2 * 8:(g2 + 1) * 8, ti * 128:(ti + 1) * 128],
                         pH[:, :].rearrange("p (a b) -> p a b", a=8)),
                  reads=tuple("pH%d" % k for k in range(8)),
                  writes=(dstname,) + tuple("pH%d" % k for k in range(8)))

    for i in range(8):
        ln_norm_tile(2 * i, dx, i * 128, hq, "hq", i)
        ln_norm_tile(2 * i + 1, dxk, i * 128, hk, "hk", i)
    if PH == "ALL":
        dmas("gpsimd", [(dw13b[q * 512:(q + 1) * 512, :], dw13[q * 512:(q + 1) * 512, :])
                        for q in range(4)], writes=("dw13b",))
        dmas("gpsimd", [(dw2b[q * 1408:(q + 1) * 1408, :], dw2[q * 1408:(q + 1) * 1408, :])
                        for q in range(4)], writes=("dw2b",))
        dma("gpsimd", gateTb[:, :, :], dgate[:, :].rearrange("(a p) b -> p a b", p=128),
            writes=("gateTb",))

    bcast(bc2[:, :], 0, "bc2b")
    P.ops[-1].waits.extend(P.lr.get("bc2n", ()))
    for vsrc, vrn, vdst, vdr in ((bc1, "bc1", ca1col, "ca1col"), (bc2, "bc2b", sacol, "sacol")):
        for f in range(16):
            tr(pH[:, f * 64:f * 64 + 1], vsrc[0:1, f * 128:(f + 1) * 128], identb[0:1, 0:1],
               reads=(vrn, "identb", "pH%d" % (f // 2)), writes=("pH%d" % (f // 2),))
        pH3 = pH[:, :].rearrange("p (a b) -> p a b", a=16)
        P.add("scalar", lambda e, vdst=vdst, pH3=pH3: e.copy(vdst[:, :], pH3[:, :, 0:1]),
              reads=tuple("pH%d" % k for k in range(8)),
              writes=(vdr,) + tuple("pH%d" % k for k in range(8)))
    for dv, dn in ((hq, "hq"), (hk, "hk")):
        for fc in range(16):
            ts(dv[:, fc, :], dv[:, fc, :], ca1col[:, fc:fc + 1], ALU.mult,
               sacol[:, fc:fc + 1], ALU.add, reads=("ca1col", "sacol"), writes=(dn,))

    memset(vbuf[:, :, :, 128:129], 1.0, writes=("vbuf",))

    wslot = [wbufA, wbufB]
    wres = ["wi0", "wi1"]
    slot_i = [0]

    def load_wslice(wdram, col0, width=256):
        s = slot_i[0]; slot_i[0] ^= 1
        wb = wslot[s]; rn = wres[s]
        dmas("gpsimd", [(wb[:, q * 4:(q + 1) * 4, 0:width],
                         wdram[q * 512:(q + 1) * 512, col0:col0 + width]
                         .rearrange("(a p) b -> p a b", p=128)) for q in range(4)],
             reads=(), writes=(rn,))
        return wb, rn

    def rope_evac(pp, pres, cos_s, sin_s, crs, dstv, dname, hh, n):
        par = (hh * 2 + n) % 2
        qr, qrr = (qraw, "qraw") if par == 0 else (qraw2, "tmpf2")
        rb, rbr = (rotb, "rotb") if par == 0 else (rotb2, "tmpf2")
        qc, qcr = (qcos, "qcos") if par == 0 else (qcos2, "qcos2")
        zsl, zslr = (zB, "zB") if par == 0 else (zC, "zC")
        scp(qr[:, :], pp, reads=(pres,), writes=(qrr, pres))
        mm(zsl[:, :], prot[:, :], qr[:, :], True, True,
           reads=("prot", qrr, zslr), writes=(zslr,))
        tt(qc[:, :], qr[:, :], cos_s, ALU.mult, reads=(qrr, crs[0]), writes=(qcr,))
        scp(rb[:, :], zsl[:, :], reads=(zslr,), writes=(rbr, zslr))
        dsl = dstv[:, hh, n * 512:(n + 1) * 512]
        tt(dsl, rb[:, :], sin_s, ALU.mult, reads=(rbr, crs[1]), writes=(dname, "wst"))
        tt(dsl, dsl, qc[:, :], ALU.add, reads=(qcr,), writes=(dname,))

    def smm(hh, qt):
        pp, pres = (pBig[:, 0:1024], ("pBig0", "pBig1")) if qt % 2 == 0 else \
                   (pBig[:, 1024:2048], ("pBig2", "pBig3"))
        for half in range(2):
            mm(pp[:, half * 512:(half + 1) * 512],
               qTg[:, hh, qt * 128:(qt + 1) * 128], kTg[:, hh, half * 512:(half + 1) * 512],
               True, True, reads=("qTg", "kTg") + pres, writes=pres)
        nslot = qt % 2
        red(nmx[:, nslot:nslot + 1], pp, ALU.max, negate=True,
            reads=pres, writes=("nmx%d" % nslot,))

    def stail(hh, qt, pdst, pdres):
        pp, pres = (pBig[:, 0:1024], ("pBig0", "pBig1")) if qt % 2 == 0 else \
                   (pBig[:, 1024:2048], ("pBig2", "pBig3"))
        nslot = qt % 2
        act(probsq[:, :], pp, AF.Exp, bias=nmx[:, nslot:nslot + 1],
            reads=pres + ("nmx%d" % nslot, "r1d"), writes=("r1d",) + pres)
        for kc in range(8):
            tr(pH[:, kc * 128:(kc + 1) * 128], probsq[:, kc * 128:(kc + 1) * 128],
               identb[:, :], reads=("r1d", "identb", "pH%d" % kc), writes=("pH%d" % kc,))
        vcp(pdst, pH[:, :], reads=tuple("pH%d" % k for k in range(8)),
            writes=(pdres,) + tuple("pH%d" % k for k in range(8)))

    def pv(head, hh, qt, psrc, psres):
        psrc3 = psrc.rearrange("p (a b) -> p a b", a=8)
        for kc in range(8):
            K = 127 if kc == 7 else 128
            mm(zA[:, 0:129], psrc3[0:K, kc, :], vbuf[0:K, kc, hh, 0:129],
               kc == 0, kc == 7, reads=(psres, "vbuf", "zA"), writes=("zA",))
        recip(zvv[:, 0:1], zA[:, 128:129], reads=("zA",), writes=("zvv",))
        ts(hot[:, :], zA[:, 0:128], zvv[:, 0:1], ALU.mult,
           reads=("zA", "zvv"), writes=("hot", "zA"))
        tr(zB[:, 0:128], hot[:, :], identf[:, :], reads=("hot", "identf", "zB"), writes=("zB",))
        P.add("scalar",
              lambda e, head=head, qt=qt:
              e.copy(ho[:, head, qt * 128:(qt + 1) * 128], zB[:, 0:128]),
              reads=("zB",), writes=("ho", "zB"))

    for g in range(4):
        dmas("sync", [(cosq_s, dcosq[:, :]), (sinq_s, dsinq[:, :]),
                      (coskS, dcosk[:, :]), (sinkS, dsink[:, :])],
             reads=(), writes=("r1a", "r1b", "r1c", "r1d"))
        for hp in range(2):
            wb, rn = load_wslice(dwq, g * 512 + hp * 256)
            for hh in (hp * 2, hp * 2 + 1):
                pps = [PBQ[(hh * 2 + n) % 4] for n in range(2)]
                for kc in range(16):
                    for n in range(2):
                        pp, pres = pps[n]
                        mm(pp, wb[:, kc, (hh - hp * 2) * 128:(hh - hp * 2 + 1) * 128],
                           hq[:, kc, n * 512:(n + 1) * 512],
                           kc == 0, kc == 15, reads=(rn, "hq", pres), writes=(pres,))
                for n in range(2):
                    pp, pres = pps[n]
                    rope_evac(pp, pres, cosq_s[:, n * 512:(n + 1) * 512],
                              sinq_s[:, n * 512:(n + 1) * 512], ("r1a", "r1b"), qTg, "qTg", hh, n)
        for hp in range(2):
            wb, rn = load_wslice(dwk, g * 512 + hp * 256)
            for hh in (hp * 2, hp * 2 + 1):
                pps = [PBQ[(hh * 2 + n) % 4] for n in range(2)]
                for kc in range(16):
                    for n in range(2):
                        pp, pres = pps[n]
                        mm(pp, wb[:, kc, (hh - hp * 2) * 128:(hh - hp * 2 + 1) * 128],
                           hk[:, kc, n * 512:(n + 1) * 512],
                           kc == 0, kc == 15, reads=(rn, "hk", pres), writes=(pres,))
                for n in range(2):
                    pp, pres = pps[n]
                    rope_evac(pp, pres, coskS[:, n * 512:(n + 1) * 512],
                              sinkS[:, n * 512:(n + 1) * 512], ("r1c", "r1d"), kTg, "kTg", hh, n)
        for hp in range(2):
            wb, rn = load_wslice(dwv, g * 512 + hp * 256)
            for ktp in range(0, 8, 2):
                for kc in range(16):
                    for kt in (ktp, ktp + 1):
                        pp, pres = PBQ[kt % 4]
                        mm(pp[:, 0:256], hk[:, kc, kt * 128:(kt + 1) * 128], wb[:, kc, 0:256],
                           kc == 0, kc == 15, reads=(rn, "hk", pres), writes=(pres,))
                for kt in (ktp, ktp + 1):
                    pp, pres = PBQ[kt % 4]
                    P.add("scalar",
                          lambda e, pp=pp, kt=kt, hp=hp:
                          e.copy(vbuf[:, kt, hp * 2:hp * 2 + 2, 0:128],
                                 pp[:, 0:256].rearrange("p (a b) -> p a b", a=2)),
                          reads=(pres,), writes=("vbuf", pres))

        for hh in range(4):
            head = g * 4 + hh

            def slot(qt):
                return (pTc, "r1b") if qt % 2 == 0 else (pT6, "r1c")

            smm(hh, 7)
            smm(hh, 0)
            stail(hh, 7, pT7, "r1a")
            pT73 = pT7.rearrange("p (a b) -> p a b", a=8)
            for kc in range(8):
                mm(zC[:, 0:3], pT73[:, kc, :], mrg[:, kc, :], kc == 0, kc == 7,
                   reads=("r1a", "mrg", "zC"), writes=("zC",))
            vcp(mz[:, 0:3], zC[:, 0:3], reads=("zC",), writes=("mz", "zC"))
            dmas("sync", [(mrow[0:1, 0:3], mz[127:128, 0:3]),
                          (v204[0:1, :], vbuf[76:77, 1, hh, 0:128])],
                 reads=("mz", "vbuf"), writes=("mrow", "v204"))
            for qt in range(6):
                smm(hh, qt + 1) if qt < 5 else smm(hh, 6)
                if qt >= 1:
                    pv(head, hh, qt - 1, *slot(qt - 1))
                stail(hh, qt, *slot(qt))
            pv(head, hh, 5, *slot(5))
            stail(hh, 6, pTc, "r1b")
            recip(bmt[0:1, 0:1], mrow[0:1, 2:3], reads=("mrow",), writes=("bmt",))
            ts(bmt[0:1, 1:2], mrow[0:1, 0:1], bmt[0:1, 0:1], ALU.mult, csts[0:1, 2:3], ALU.mult,
               reads=("mrow", "bmt", "csts"), writes=("bmt",))
            ts(bmt[0:1, 1:2], bmt[0:1, 1:2], csts[0:1, 1:2], ALU.add,
               reads=("bmt", "csts"), writes=("bmt",))
            ts(bmt[0:1, 2:3], mrow[0:1, 1:2], bmt[0:1, 0:1], ALU.mult,
               reads=("mrow", "bmt"), writes=("bmt",))
            recip(bmt[0:1, 3:4], bmt[0:1, 1:2], reads=("bmt",), writes=("bmt",))
            ts(bmt[0:1, 4:5], bmt[0:1, 2:3], bmt[0:1, 3:4], ALU.mult, reads=("bmt",), writes=("bmt",))
            ts(bmt[0:1, 4:5], bmt[0:1, 4:5], 1.0, ALU.min, 0.0, ALU.max,
               reads=("bmt",), writes=("bmt",))
            ts(bmt[0:1, 5:6], bmt[0:1, 4:5], u_sb[0:1, head:head + 1], ALU.is_gt,
               reads=("bmt", "u_sb"), writes=("bmt",))
            ts(bmt[0:1, 6:7], bmt[0:1, 5:6], csts[0:1, 3:4], ALU.mult,
               reads=("bmt", "csts"), writes=("bmt",))
            ts(vescl[0:1, :], v204[0:1, :], bmt[0:1, 6:7], ALU.mult,
               reads=("v204", "bmt"), writes=("vescl",))
            mm(zC[:, 0:128], onesr[0:1, :], vescl[0:1, :], True, True,
               reads=("onesr", "vescl", "zC"), writes=("zC",))
            for kt in range(8):
                stt(vbuf[:, kt, hh, 0:128], zC[:, 0:128], m_t[:, kt:kt + 1],
                    vbuf[:, kt, hh, 0:128], ALU.mult, ALU.add,
                    reads=("zC", "m_t", "vbuf"), writes=("vbuf",))
            pv(head, hh, 6, pTc, "r1b")
            pv(head, hh, 7, pT7, "r1a")

    bcast(bc1[:, :], 2, "bc1")
    dmas("sync", [(xres[:, i, :], dx[i * 128:(i + 1) * 128, :]) for i in range(8)],
         reads=(), writes=tuple("x%d" % i for i in range(8)) + ("hq", "hk"))
    for dj in range(8):
        wb, rn = load_wslice(dwo, dj * 256)
        for tip in range(0, 8, 2):
            for kc in range(16):
                for ti in (tip, tip + 1):
                    pp, pres = PBQ[ti % 2]
                    mm(pp[:, 0:256], ho[:, kc, ti * 128:(ti + 1) * 128], wb[:, kc, 0:256],
                       kc == 0, kc == 15, reads=(rn, "ho", pres), writes=(pres,))
            for ti in (tip, tip + 1):
                pp, pres = PBQ[ti % 2]
                tf = tmpf[:, 0:256] if ti % 2 == 0 else tmpf2[:, 0:256]
                tfr = "tmpf" if ti % 2 == 0 else "tmpf2"
                tt(tf, pp[:, 0:256], bc1[:, dj * 256:(dj + 1) * 256], ALU.mult,
                   reads=(pres, "bc1"), writes=(tfr, "qraw", "rotb", pres))
                tt(xres[:, ti, dj * 256:(dj + 1) * 256], xres[:, ti, dj * 256:(dj + 1) * 256],
                   tf, ALU.add, reads=(tfr,), writes=("x%d" % ti,))
    tp = tap("xattn", [TOK, D])
    if tp is not None:
        for i in range(8):
            dma("sync", tp[i * 128:(i + 1) * 128, :], xres[:, i, :], reads=("x%d" % i,))

    if PH == "ALL":
        for hf in range(2):
            dma("gpsimd", aTbv[:, :, :], da1[:, :].rearrange("(a p) b -> p a b", p=128),
                reads=(), writes=("aTb",) + CLOB_B_ARC)
            bcast(bc1[:, :], 4, "bc1")
            bcast(bc2[:, :], 3, "bc2")
            for ti in range(4):
                gi = hf * 4 + ti
                xv = xres[:, gi, :]
                xr = "x%d" % gi
                red(stats[:, 0:1], xv, ALU.add, reads=(xr,), writes=("stats",))
                act(pBig[:, :], xv, AF.Square, accum=stats[:, 1:2],
                    reads=(xr,) + PB_ALL, writes=PB_ALL + ("stats",))
                ts(stats[:, 2:3], stats[:, 0:1], 1.0 / 2048, ALU.mult,
                   reads=("stats",), writes=("stats",))
                tt(stats[:, 3:4], stats[:, 2:3], stats[:, 2:3], ALU.mult,
                   reads=("stats",), writes=("stats",))
                stt(stats[:, 4:5], stats[:, 1:2], 1.0 / 2048, stats[:, 3:4],
                    ALU.mult, ALU.subtract, reads=("stats",), writes=("stats",))
                act(stats[:, 5:6], stats[:, 4:5], AF.Sqrt, bias=csts[:, 0:1],
                    reads=("stats", "csts"), writes=("stats",))
                recip(stats[:, 6:7], stats[:, 5:6], reads=("stats",), writes=("stats",))
                ts(pBig[:, :], xv, stats[:, 2:3], ALU.subtract, stats[:, 6:7], ALU.mult,
                   reads=(xr, "stats") + PB_ALL, writes=PB_ALL)
                tt(htok[:, :], pBig[:, :], bc1[:, :], ALU.mult,
                   reads=PB_ALL + ("bc1",), writes=("htok",) + PB_ALL)
                tt(htok[:, :], htok[:, :], bc2[:, :], ALU.add,
                   reads=("bc2",), writes=("htok",))
                for g2 in range(2):
                    for j in range(8):
                        fc = g2 * 8 + j
                        tr(pH[:, j * 128:(j + 1) * 128], htok[:, fc * 128:(fc + 1) * 128],
                           identb[:, :], reads=("htok", "identb", "pH%d" % j),
                           writes=("pH%d" % j,))
                    P.add("scalar",
                          lambda e, g2=g2, ti=ti:
                          e.copy(h2T[:, g2 * 8:(g2 + 1) * 8, ti * 128:(ti + 1) * 128],
                                 pH[:, :].rearrange("p (a b) -> p a b", a=8)),
                          reads=tuple("pH%d" % k for k in range(8)),
                          writes=("h2T",) + tuple("pH%d" % k for k in range(8))
                          + (("wi0", "wi1") if ti == 0 else ()))
            for daT, uslot, ur in [(da1, 0, "ubt0"), (da3, 1, "ubt1")]:
                if uslot == 1:
                    dma("gpsimd", aTbv[:, :, :], daT[:, :].rearrange("(a p) b -> p a b", p=128),
                        reads=(), writes=("aTb",) + CLOB_B_ARC)
                for kc in range(16):
                    mm(zA[:, :], aTbv[:, kc, :], h2T[:, kc, :], kc == 0, kc == 15,
                       reads=("aTb", "h2T", "zA"), writes=("zA",))
                ts(ubt[:, uslot, :], zA[:, :], 2.0, ALU.mult,
                   reads=("zA",), writes=(ur, "w13s0", "zA") + CLOB_B_ARC)
            dmas("sync", [(u1e[:, e_, :], ubt[e_ * 16:(e_ + 1) * 16, 0, :]) for e_ in range(NE)]
                 + [(u3e[:, e_, :], ubt[e_ * 16:(e_ + 1) * 16, 1, :]) for e_ in range(NE)],
                 reads=("ubt0", "ubt1"), writes=("u1e", "u3e"))
            ts(u1n[:, :, :], u1e[:, :, :], -1.0, ALU.mult, reads=("u1e",), writes=("u1n",))
            ts(u3n[:, :, :], u3e[:, :, :], -1.0, ALU.mult, reads=("u3e",), writes=("u3n",))

            for tip in range(0, 4, 2):
                for kc in range(16):
                    for ti in (tip, tip + 1):
                        zg, zgr = (zB, "zB") if ti % 2 == 0 else (zC, "zC")
                        mm(zg[:, 0:8], h2T[:, kc, ti * 128:(ti + 1) * 128], gateTb[:, kc, :],
                           kc == 0, kc == 15, reads=("h2T", "gateTb", zgr), writes=(zgr,))
                for ti in (tip, tip + 1):
                    zg, zgr = (zB, "zB") if ti % 2 == 0 else (zC, "zC")
                    vcp(lg[:, ti, :], zg[:, 0:8], reads=(zgr,), writes=("lg", zgr))
                for ti in (tip, tip + 1):
                    lv = lg[:, ti, :]
                    red(t2s[:, 0:1], lv, ALU.max, negate=True, reads=("lg",), writes=("t2s",))
                    ts(tmp8[:, :], lv, t2s[:, 0:1], ALU.add, reads=("lg", "t2s"), writes=("tmp8",))
                    ts(msk8[:, :], tmp8[:, :], 0.0, ALU.is_lt, reads=("tmp8",), writes=("msk8",))
                    tt(tmp8[:, :], lv, msk8[:, :], ALU.mult, reads=("lg", "msk8"), writes=("tmp8",))
                    ts(msk8[:, :], msk8[:, :], 1.0, ALU.subtract, reads=(), writes=("msk8",))
                    stt(tmp8[:, :], msk8[:, :], 30000.0, tmp8[:, :], ALU.mult, ALU.add,
                        reads=("msk8",), writes=("tmp8",))
                    red(t2s[:, 1:2], tmp8[:, :], ALU.max, reads=("tmp8",), writes=("t2s",))
                    act(t2s[:, 2:3], t2s[:, 1:2], AF.Exp, bias=t2s[:, 0:1],
                        reads=("t2s",), writes=("t2s",))
                    ts(t2s[:, 3:4], t2s[:, 2:3], 1.0, ALU.add, reads=("t2s",), writes=("t2s",))
                    recip(t2s[:, 4:5], t2s[:, 3:4], reads=("t2s",), writes=("t2s",))
                    for e_ in range(NE):
                        act(t2s[:, 5:6], lv[:, e_:e_ + 1], AF.Exp, bias=t2s[:, 0:1],
                            reads=("lg", "t2s"), writes=("t2s",))
                        ts(t2s[:, 6:7], lv[:, e_:e_ + 1], t2s[:, 1:2], ALU.is_ge,
                           reads=("lg", "t2s"), writes=("t2s",))
                        tt(t2s[:, 7:8], t2s[:, 5:6], t2s[:, 6:7], ALU.mult,
                           reads=("t2s",), writes=("t2s",))
                        tt(wfull[:, ti, e_:e_ + 1], t2s[:, 7:8], t2s[:, 4:5], ALU.mult,
                           reads=("t2s",), writes=("wfull",))
            tpw = tap("wfull", [128, 8, 8])
            if tpw is not None:
                dma("sync", tpw[:, hf * 4:(hf + 1) * 4, :], wfull[:, :, :], reads=("wfull",))
            tpl = tap("lgt", [128, 8, 8])
            if tpl is not None:
                dma("sync", tpl[:, hf * 4:(hf + 1) * 4, :], lg[:, :, :], reads=("lg",))
            for ti in range(4):
                tr(zB[0:8, ti * 128:(ti + 1) * 128], wfull[:, ti, :], identf[:, :],
                   reads=("wfull", "identf", "zB"), writes=("zB",))
            vcp(werows[:, :], zB[0:8, :], reads=("zB",), writes=("werows", "zB"))
            dma("sync", dwe[:, :], werows[:, :], reads=("werows",), writes=("dwe",))
            dmas("gpsimd", [(webc[:, e_, :], dwe[e_:e_ + 1, :].partition_broadcast(128))
                            for e_ in range(NE)],
                 reads=("dwe",), writes=("webc",) + CLOB_B_ARC)

            zts = [zA, zB, zC]
            zrs = ["zA", "zB", "zC"]
            TMP = [(tmpf, "tmpf", ("qraw", "rotb")), (tmpf2, "tmpf2", ())]
            ACTB = [(actb, "actb"), (actb2, "actb2")]
            WACT = [(wactb, "wactb"), (wactb2, "wactb2")]
            ACCF = [(accf, "accf"), (accf2, "accf2")]
            for fcp in range(0, NFC, 2):
                for w in (0, 1):
                    fc = fcp + w
                    sl = fc % 2
                    wr = "w13s%d" % sl
                    dmas("sync", [(w13s[sl][:, q * 4:(q + 1) * 4, :],
                                   dw13b[q * 512:(q + 1) * 512, fc * 256:(fc + 1) * 256]
                                   .rearrange("(a p) b -> p a b", p=128)) for q in range(4)],
                         reads=("dw13b",),
                         writes=(wr,) + (("ubt0", "ubt1") + CLOB_B_ARC if fc < 2 else ()))
                    dmas("gpsimd", [
                        (b13s[0:16, sl, :, :], db13[:, :, 0, fc * 128:(fc + 1) * 128]),
                        (b13s[32:48, sl, :, :], db13[:, :, 1, fc * 128:(fc + 1) * 128]),
                        (b13s[64:80, sl, :, :], db13[:, :, 0, fc * 128:(fc + 1) * 128]),
                        (b13sB[:, sl, :, :], db13[:, :, 1, fc * 128:(fc + 1) * 128]),
                        (a2s[:, sl, :, :], da2[fc * 128:(fc + 1) * 128, :, :]),
                    ], reads=(), writes=("b13s%d" % sl, "a2s%d" % sl))
                for w in (0, 1):
                    fc = fcp + w
                    sl = fc % 2
                    wr = "w13s%d" % sl
                    B1, B1r = PBQ[0 + 2 * w]
                    B3, B3r = PBQ[1 + 2 * w]
                    for kc in range(16):
                        mm(B1, w13s[sl][:, kc, 0:128], h2T[:, kc, :], kc == 0, False,
                           reads=(wr, "h2T", B1r), writes=(B1r,), skip_gc=True)
                        mm(B3, w13s[sl][:, kc, 128:256], h2T[:, kc, :], kc == 0, False,
                           reads=(wr, "h2T", B3r), writes=(B3r,), skip_gc=True)
                def flush_z():
                    while pend_z:
                        zfc, zsl2, ze, zw = pend_z.pop(0)
                        wb_, wbr = WACT[zw]
                        zt = zts[ze // 3]
                        zo = 32 * (ze % 3)
                        mm(zt[zo:zo + 16, :], a2s[:, zsl2, ze, :], wb_[:, :],
                           zfc == 0, zfc == NFC - 1,
                           reads=("a2s%d" % zsl2, wbr, zrs[ze // 3]),
                           writes=(zrs[ze // 3],), skip_gc=True)

                pend_z = []
                for e_ in range(NE):
                    for w in (0, 1):
                        sl = (fcp + w) % 2
                        B1, B1r = PBQ[0 + 2 * w]
                        B3, B3r = PBQ[1 + 2 * w]
                        if e_ > 0:
                            mm(B1, b13s[64:80, sl, e_ - 1, :], u1n[:, e_ - 1, :], False, False,
                               reads=("b13s%d" % sl, "u1n", B1r), writes=(B1r,), skip_gc=True)
                        mm(B3, b13sB[:, sl, e_ - 1, :], u3n[:, e_ - 1, :], False, False,
                           reads=("b13s%d" % sl, "u3n", B3r), writes=(B3r,), skip_gc=True) \
                            if e_ > 0 else None
                        mm(B1, b13s[0:16, sl, e_, :], u1e[:, e_, :], False, e_ == NE - 1,
                           reads=("b13s%d" % sl, "u1e", B1r), writes=(B1r,), skip_gc=True)
                        mm(B3, b13s[32:48, sl, e_, :], u3e[:, e_, :], False, e_ == NE - 1,
                           reads=("b13s%d" % sl, "u3e", B3r), writes=(B3r,), skip_gc=True)
                    flush_z()
                    for w in (0, 1):
                        B1, B1r = PBQ[0 + 2 * w]
                        tf, tfr, tfc = TMP[w]
                        act(tf[:, :], B1, AF.Silu, reads=(B1r,), writes=(tfr,) + tfc + (B1r,))
                    for w in (0, 1):
                        B3, B3r = PBQ[1 + 2 * w]
                        tf, tfr, _ = TMP[w]
                        ab, abr = ACTB[w]
                        wb_, wbr = WACT[w]
                        af, afr = ACCF[w]
                        tt(ab[:, :], tf[:, :], B3, ALU.mult, reads=(tfr, B3r), writes=(abr, B3r))
                        tt(wb_[:, :], ab[:, :], webc[:, e_, :], ALU.mult,
                           reads=(abr, "webc"), writes=(wbr,))
                        if e_ == 0:
                            vcp(af[:, :], wb_[:, :], reads=(wbr,), writes=(afr,))
                        else:
                            tt(af[:, :], af[:, :], wb_[:, :], ALU.add,
                               reads=(wbr,), writes=(afr,))
                    for w in (0, 1):
                        pend_z.append((fcp + w, (fcp + w) % 2, e_, w))
                flush_z()
                for w in (0, 1):
                    fc = fcp + w
                    af, afr = ACCF[w]
                    scp(act_sum[:, fc, :], af[:, :], reads=(afr,),
                        writes=("as%d" % fc,) + (("aTb",) if fc < 2 else ())
                        + (CLOB_B_ARB if hf == 0 and fc < 2 else ()))
            bcast(bc1[:, :], 5, "bc1")
            for e_ in range(NE):
                zt = zts[e_ // 3]
                zo = 32 * (e_ % 3)
                ts(zsb[:, e_, :], zt[zo:zo + 16, :], 2.0, ALU.mult,
                   reads=(zrs[e_ // 3],), writes=("zsb", "u1e", zrs[e_ // 3]))
            for dj in range(4):
                dma("gpsimd", b2dj[:, :, :],
                    db2[:, :, dj * 512:(dj + 1) * 512],
                    reads=(), writes=("b2dj", "w13s0"))
                for fcq in range(11):
                    sl = fcq % 2
                    dmas("scalar", [(w2g[sl][:, j, :],
                                     dw2b[(fcq * 4 + j) * 128:(fcq * 4 + j + 1) * 128,
                                          dj * 512:(dj + 1) * 512]) for j in range(4)],
                         reads=("dw2b",), writes=("w2g%d" % sl, "w13s1"))
                    for j in range(4):
                        fc = fcq * 4 + j
                        for ti in range(4):
                            mm(pBig[:, ti * 512:(ti + 1) * 512],
                               act_sum[:, fc, ti * 128:(ti + 1) * 128], w2g[sl][:, j, :],
                               fc == 0, False,
                               reads=("w2g%d" % sl, "as%d" % fc, "pBig%d" % ti),
                               writes=("pBig%d" % ti,), skip_gc=True)
                for ti in range(4):
                    for e_ in range(NE):
                        mm(pBig[:, ti * 512:(ti + 1) * 512], zsb[:, e_, ti * 128:(ti + 1) * 128],
                           b2dj[:, e_, :], False, e_ == NE - 1,
                           reads=("zsb", "b2dj", "pBig%d" % ti), writes=("pBig%d" % ti,),
                           skip_gc=True)
                for ti in range(4):
                    gi = hf * 4 + ti
                    tt(tmpf[:, :], pBig[:, ti * 512:(ti + 1) * 512],
                       bc1[:, dj * 512:(dj + 1) * 512], ALU.mult,
                       reads=("pBig%d" % ti, "bc1"), writes=("tmpf", "pBig%d" % ti))
                    tt(xres[:, gi, dj * 512:(dj + 1) * 512],
                       xres[:, gi, dj * 512:(dj + 1) * 512], tmpf[:, :], ALU.add,
                       reads=("tmpf",), writes=("x%d" % gi,))

    oop = dmas("sync", [(dout[i * 128:(i + 1) * 128, :], xres[:, i, :]) for i in range(8)],
               reads=tuple("x%d" % i for i in range(8)))
    oop.mark = True

    with (
        nc.semaphore("s_sync") as s0, nc.semaphore("s_gpsimd") as s1,
        nc.semaphore("s_vector") as s2, nc.semaphore("s_scalar") as s3,
        nc.semaphore("s_tensor") as s4, nc.semaphore("s_scalardma") as s5,
        nc.semaphore("s_cc") as s6,
        nc.Block() as block,
    ):
        sems = {"sync": s0, "gpsimd": s1, "vector": s2, "scalar": s3, "tensor": s4,
                "scalardma": s5, "cc": s6}
        P.emit(block, sems)
    print("total instructions:", len(P.ops))
    return nc


_NC_CACHE = {}


def kernel(**inputs):
    cst = _host_constants()
    in_maps = [_host_core_inputs(inputs, c, cst) for c in range(8)]
    if "nc" not in _NC_CACHE:
        _NC_CACHE["nc"] = _build()
    nc = _NC_CACHE["nc"]
    res = run_bass_kernel_spmd(nc, in_maps, core_ids=list(range(8)))
    out = np.empty((4, 2048, 2048), dtype=np.float32)
    for c in range(8):
        b, hf = c // 2, c % 2
        out[b, hf * TOK:(hf + 1) * TOK, :] = np.asarray(res.results[c]["out"])
    return out


# version 18
# speedup vs baseline: 1.0838x; 1.0422x over previous
import numpy as np
import ml_dtypes
import concourse.bass as bass
import concourse.mybir as mybir
from concourse.bass_utils import run_bass_kernel_spmd

F32 = mybir.dt.float32
BF16 = mybir.dt.bfloat16
AF = mybir.ActivationFunctionType
ALU = mybir.AluOpType
AX = mybir.AxisListType
bfdt = ml_dtypes.bfloat16

U42 = np.array([
 [0.6102175712585449, 0.6966079473495483, 0.06568920612335205, 0.6029453277587891, 0.4143662452697754, 0.24911487102508545, 0.5782934427261353, 0.24924850463867188, 0.959638237953186, 0.8191142082214355, 0.3645491600036621, 0.059471845626831055, 0.9161832332611084, 0.6182148456573486, 0.8282822370529175, 0.6976723670959473],
 [0.756269097328186, 0.3594318628311157, 0.4266716241836548, 0.477594256401062, 0.08340179920196533, 0.40768611431121826, 0.6278458833694458, 0.3251793384552002, 0.35155820846557617, 0.15171575546264648, 0.5589374303817749, 0.8866020441055298, 0.3201930522918701, 0.12432277202606201, 0.6967982053756714, 0.8069779872894287],
 [0.6127028465270996, 0.7509020566940308, 0.4151843786239624, 0.4897202253341675, 0.11665701866149902, 0.40220367908477783, 0.08376991748809814, 0.9528622627258301, 0.9559504985809326, 0.06945085525512695, 0.20124602317810059, 0.06397080421447754, 0.8091422319412231, 0.31807422637939453, 0.639366865158081, 0.8050472736358643],
 [0.5085939168930054, 0.5088651180267334, 0.9745233058929443, 0.3331362009048462, 0.05924034118652344, 0.7208822965621948, 0.3291349411010742, 0.5319250822067261, 0.4839836359024048, 0.643956184387207, 0.19713079929351807, 0.1282445192337036, 0.3164334297180176, 0.4948718547821045, 0.9159883260726929, 0.781721830368042],
], dtype=np.float32)

T = 2048; D = 2048; H = 16; DH = 128
KEEP = np.concatenate([np.arange(204), np.arange(1229, 2048)])
NKC = 8
TOK = 1024; HTOK = 512
Fd = 5632; NE = 8; NR = 16; NFC = 44
EVK = 204
MBEG, MEND = 205, 1023
PH = "ALL"
DEBUG_TAPS = ()

_ENGS = ("sync", "gpsimd", "vector", "scalar", "tensor", "scalardma", "vectordma", "cc")


class _Op:
    __slots__ = ("eng", "fn", "waits", "amt", "gi", "dma", "mark", "cum", "nsub", "chain")

    def __init__(self, eng, fn, waits, dma, gi, nsub=1, chain=None):
        self.eng = eng; self.fn = fn; self.waits = waits
        self.dma = dma; self.nsub = nsub
        self.amt = 16 * nsub if dma else 1
        self.gi = gi; self.mark = False; self.cum = 0
        self.chain = chain


class _Prog:
    def __init__(self):
        self.ops = []
        self.lw = {}
        self.lr = {}
        self.last_dma = {}

    def add(self, eng, fn, reads=(), writes=(), dma=False, nsub=1, chain=None):
        waits = []
        if dma:
            prev = self.last_dma.get(eng)
            if prev is not None:
                waits.append(prev)
        for r in reads:
            w = self.lw.get(r)
            if w is not None:
                waits.append(w)
        for r in writes:
            w = self.lw.get(r)
            if w is not None:
                waits.append(w)
            waits.extend(self.lr.get(r, ()))
        op = _Op(eng, fn, waits, dma, len(self.ops), nsub, chain)
        self.ops.append(op)
        if dma:
            self.last_dma[eng] = op
        for r in reads:
            self.lr.setdefault(r, []).append(op)
        for r in writes:
            self.lw[r] = op
            self.lr[r] = []
        return op

    @staticmethod
    def _need(w, op):
        return True

    def emit(self, block, sems):
        for op in self.ops:
            if op.dma:
                op.mark = True
            best = {}
            for w in op.waits:
                if self._need(w, op) and (w.eng not in best or w.gi > best[w.eng].gi):
                    best[w.eng] = w
            op.waits = list(best.values())
            for w in op.waits:
                w.mark = True
        cnt = {e: 0 for e in _ENGS}
        for op in self.ops:
            if op.mark:
                cnt[op.eng] += op.amt
            op.cum = cnt[op.eng]
        ops = self.ops

        def mkbody(name):
            group = ("scalar", "scalardma") if name == "scalar" else \
                    (("gpsimd", "cc") if name == "gpsimd" else
                     (("vector", "vectordma") if name == "vector" else (name,)))
            def body(e):
                hi = {en: 0 for en in _ENGS}
                for op in ops:
                    if op.eng not in group:
                        continue
                    for w in op.waits:
                        if not self._need(w, op):
                            continue
                        if w.cum > hi[w.eng]:
                            e.wait_ge(sems[w.eng], w.cum)
                            hi[w.eng] = w.cum
                    inst = op.fn(e)
                    if op.mark:
                        if isinstance(inst, list):
                            for ii in inst:
                                ii.then_inc(sems[op.eng], 16)
                        else:
                            inst.then_inc(sems[op.eng], op.amt)
                for gn in group:
                    if cnt[gn] > hi[gn]:
                        e.wait_ge(sems[gn], cnt[gn])
            return body
        block.sync(mkbody("sync"))
        block.gpsimd(mkbody("gpsimd"))
        block.vector(mkbody("vector"))
        block.scalar(mkbody("scalar"))
        block.tensor(mkbody("tensor"))


def _host_constants():
    inv = 1.0 / (10000.0 ** (np.arange(0, DH, 2, dtype=np.float32) / DH))
    fr = np.arange(T, dtype=np.float32)[:, None] * inv[None, :]
    emb = np.concatenate([fr, fr], 1)
    cosT = np.cos(emb).astype(np.float32)
    sinT = np.sin(emb).astype(np.float32)
    s = np.float32(1.0 / np.sqrt(np.float32(DH)))
    consts = np.tile(np.array([[1e-5, 1e-6, 1.0 / 818.0, 1.0 / 819.0, 0.25, 1.0, 0.0, 0.0]],
                              dtype=np.float32), (128, 1))
    prot = np.zeros((128, 128), dtype=np.float32)
    for d2 in range(64):
        prot[d2 + 64, d2] = -1.0
    for d2 in range(64, 128):
        prot[d2 - 64, d2] = 1.0
    mrg = np.zeros((128, NKC, 3), dtype=np.float32)
    for kc in range(NKC):
        for p in range(128):
            ki = kc * 128 + p
            if ki >= 1023:
                continue
            mrg[p, kc, 2] = 1.0
            if MBEG <= ki < MEND:
                mrg[p, kc, 0] = 1.0
            if ki == EVK:
                mrg[p, kc, 1] = 1.0
    return dict(cosT=cosT, sinT=sinT, qscale=s, consts=consts,
                identb=np.eye(128, dtype=bfdt), identf=np.eye(128, dtype=np.float32),
                prot=prot.astype(bfdt), onesrow=np.ones((1, 128), dtype=bfdt),
                mrg=mrg.astype(bfdt))


def _host_core_inputs(inputs, c, cst):
    b, hf = c // 2, c % 2
    X = np.asarray(inputs["L0_attn_hidden_states"])
    x_own = np.ascontiguousarray(X[b, hf * TOK:(hf + 1) * TOK])
    xk = np.zeros((1024, D), dtype=np.float32)
    xk[:1023] = X[b, KEEP]
    ownpos = np.arange(hf * TOK, hf * TOK + TOK)
    kpos = np.zeros(1024, dtype=np.int64); kpos[:1023] = KEEP
    cosq = np.ascontiguousarray((cst["cosT"][ownpos] * cst["qscale"]).T.astype(bfdt))
    sinq = np.ascontiguousarray((cst["sinT"][ownpos] * cst["qscale"]).T.astype(bfdt))
    cosk = np.ascontiguousarray(cst["cosT"][kpos].T.astype(bfdt))
    sink = np.ascontiguousarray(cst["sinT"][kpos].T.astype(bfdt))
    cosk[:, 1023] = 0; sink[:, 1023] = 0
    m_t = np.zeros((128, NKC), dtype=np.float32)
    if hf == 1:
        for kc in range(NKC):
            for p in range(128):
                ki = kc * 128 + p
                if MBEG <= ki < MEND:
                    m_t[p, kc] = 1.0
    w1T = np.asarray(inputs["L0_ffn_w1"]).T
    w3T = np.asarray(inputs["L0_ffn_w3"]).T
    w13 = np.empty((D, NFC * 256), dtype=np.float32)
    for fc in range(NFC):
        w13[:, fc * 256:fc * 256 + 128] = w1T[:, fc * 128:(fc + 1) * 128]
        w13[:, fc * 256 + 128:fc * 256 + 256] = w3T[:, fc * 128:(fc + 1) * 128]
    b1 = np.asarray(inputs["L0_ffn_b1"]); b3 = np.asarray(inputs["L0_ffn_b3"])
    b13 = np.ascontiguousarray(
        np.stack([b1.transpose(0, 2, 1), b3.transpose(0, 2, 1)], axis=2).transpose(1, 0, 2, 3))
    return dict(
        x_own=x_own, x_kept=np.ascontiguousarray(xk),
        wqT=np.ascontiguousarray(np.asarray(inputs["L0_attn_Wq"]).T),
        wkT=np.ascontiguousarray(np.asarray(inputs["L0_attn_Wk"]).T),
        wvT=np.ascontiguousarray(np.asarray(inputs["L0_attn_Wv"]).T),
        woT=np.ascontiguousarray(np.asarray(inputs["L0_attn_Wo"]).T),
        wmod=np.ascontiguousarray(np.asarray(inputs["L0_wmod"])),
        gateT=np.ascontiguousarray(np.asarray(inputs["L0_ffn_gate_w"]).T),
        w13=w13,
        w2T=np.ascontiguousarray(np.asarray(inputs["L0_ffn_w2"]).T),
        b13=b13,
        a2il=np.ascontiguousarray(np.asarray(inputs["L0_ffn_a2"]).transpose(2, 0, 1)),
        b2il=np.ascontiguousarray(np.asarray(inputs["L0_ffn_b2"]).transpose(2, 0, 1)),
        a1T=np.ascontiguousarray(np.asarray(inputs["L0_ffn_a1"]).reshape(NE * NR, D).T),
        a3T=np.ascontiguousarray(np.asarray(inputs["L0_ffn_a3"]).reshape(NE * NR, D).T),
        c_col=np.ascontiguousarray(np.asarray(inputs["L0_c"])[b].reshape(16, 128).T),
        u_own=np.ascontiguousarray(U42[b:b + 1]),
        cosq=cosq, sinq=sinq, cosk=cosk, sink=sink, m_t=m_t,
        consts=cst["consts"], identb=cst["identb"], identf=cst["identf"],
        prot=cst["prot"], onesrow=cst["onesrow"], mrg=cst["mrg"],
    )


def _build():
    nc = bass.Bass("TRN2", target_bir_lowering=False)
    P = _Prog()

    dx = nc.dram_tensor("x_own", [TOK, D], F32, kind="ExternalInput")
    dxk = nc.dram_tensor("x_kept", [1024, D], F32, kind="ExternalInput")
    dwq = nc.dram_tensor("wqT", [D, D], F32, kind="ExternalInput")
    dwk = nc.dram_tensor("wkT", [D, D], F32, kind="ExternalInput")
    dwv = nc.dram_tensor("wvT", [D, D], F32, kind="ExternalInput")
    dwo = nc.dram_tensor("woT", [D, D], F32, kind="ExternalInput")
    dwmod = nc.dram_tensor("wmod", [D, 12288], F32, kind="ExternalInput")
    dgate = nc.dram_tensor("gateT", [D, 8], F32, kind="ExternalInput")
    dw13 = nc.dram_tensor("w13", [D, NFC * 256], F32, kind="ExternalInput")
    dw2 = nc.dram_tensor("w2T", [Fd, D], F32, kind="ExternalInput")
    db13 = nc.dram_tensor("b13", [NR, NE, 2, Fd], F32, kind="ExternalInput")
    da2 = nc.dram_tensor("a2il", [Fd, NE, NR], F32, kind="ExternalInput")
    db2 = nc.dram_tensor("b2il", [NR, NE, D], F32, kind="ExternalInput")
    da1 = nc.dram_tensor("a1T", [D, 128], F32, kind="ExternalInput")
    da3 = nc.dram_tensor("a3T", [D, 128], F32, kind="ExternalInput")
    dc = nc.dram_tensor("c_col", [128, 16], F32, kind="ExternalInput")
    du = nc.dram_tensor("u_own", [1, 16], F32, kind="ExternalInput")
    dcosq = nc.dram_tensor("cosq", [128, 1024], BF16, kind="ExternalInput")
    dsinq = nc.dram_tensor("sinq", [128, 1024], BF16, kind="ExternalInput")
    dcosk = nc.dram_tensor("cosk", [128, 1024], BF16, kind="ExternalInput")
    dsink = nc.dram_tensor("sink", [128, 1024], BF16, kind="ExternalInput")
    dconsts = nc.dram_tensor("consts", [128, 8], F32, kind="ExternalInput")
    didb = nc.dram_tensor("identb", [128, 128], BF16, kind="ExternalInput")
    didf = nc.dram_tensor("identf", [128, 128], F32, kind="ExternalInput")
    dprot = nc.dram_tensor("prot", [128, 128], BF16, kind="ExternalInput")
    dones = nc.dram_tensor("onesrow", [1, 128], BF16, kind="ExternalInput")
    dmrg = nc.dram_tensor("mrg", [128, NKC, 3], BF16, kind="ExternalInput")
    dmt = nc.dram_tensor("m_t", [128, NKC], F32, kind="ExternalInput")
    dout = nc.dram_tensor("out", [TOK, D], F32, kind="ExternalOutput")
    dscrM = nc.dram_tensor("dscrM", [1, 12288], F32)
    dsum = nc.dram_tensor("dsum", [1, 12288], F32)
    dscrP = nc.dram_tensor("dscrP", [1, 12288], F32)
    dwe = nc.dram_tensor("dwe", [NE, HTOK], BF16)
    dw13b = nc.dram_tensor("dw13b", [D, NFC * 256], BF16)
    dw2b = nc.dram_tensor("dw2b", [Fd, D], BF16)

    taps = {}

    def tap(name, shape, dtype=F32):
        if name in DEBUG_TAPS:
            if name not in taps:
                taps[name] = nc.dram_tensor("tap_" + name, shape, dtype, kind="ExternalOutput")
            return taps[name]
        return None

    arA = nc.alloc_sbuf_tensor("arA", [128, 16384], F32)
    arB = nc.alloc_sbuf_tensor("arB", [128, 15360], F32)
    arC = nc.alloc_sbuf_tensor("arC", [128, 8192], F32)
    arA_bf = arA[:, :].bitcast(BF16)
    arB_bf = arB[:, :].bitcast(BF16)
    arC_bf = arC[:, :].bitcast(BF16)

    def carve(base, off, n, lead=None):
        v = base[:, off:off + n]
        if lead is not None:
            v = v.rearrange("p (a b) -> p a b", a=lead)
        return v

    hq = carve(arA_bf, 0, 16 * 1024, 16)
    hk = carve(arA_bf, 16 * 1024, 16 * 1024, 16)
    xres = arA[:, :].rearrange("p (a b) -> p a b", a=8)

    o = 0
    qTg = carve(arB_bf, o, 4096, 4); o += 4096
    kTg = carve(arB_bf, o, 4096, 4); o += 4096
    vbuf = carve(arB_bf, o, 8 * 4 * 129).rearrange("p (k h d) -> p k h d", k=8, h=4)
    o += 8 * 4 * 129
    pT7 = carve(arB_bf, o, 1024); cosq_s = pT7; o += 1024
    pTc = carve(arB_bf, o, 1024); sinq_s = pTc; o += 1024
    pT6 = carve(arB_bf, o, 1024); o += 1024
    probsq = carve(arB_bf, o, 1024); o += 1024
    wbufA = carve(arB_bf, o, 4096, 16); o += 4096
    wbufB = carve(arB_bf, o, 4096, 16); o += 4096
    assert o <= 30720, o
    act_sum = carve(arB_bf, 0, NFC * 512, NFC)
    h2T = carve(arB_bf, NFC * 512, 16 * 512, 16)
    assert NFC * 512 + 16 * 512 <= 30720

    xkb = [arC[:, 0:2048], arC[:, 2048:4096]]
    ho = carve(arC_bf, 0, 16384, 16)
    o = 0
    webc = carve(arC_bf, o, 8 * 512, 8); o += 8 * 512
    w13s = [carve(arC_bf, o, 4096, 16), carve(arC_bf, o + 4096, 4096, 16)]
    o += 8192
    w2s = [carve(arC_bf, o, 512), carve(arC_bf, o + 512, 512)]
    o += 1024
    accf = arC[:, (o // 2):(o // 2) + 512]
    o += 1024
    aTbv = carve(arC_bf, o, 2048, 16)
    accf2 = arC[:, (o // 2):(o // 2) + 512]
    o += 2048
    assert o <= 16384
    w2g = [arC_bf[:, 8192:10240].rearrange("p (a b) -> p a b", a=4),
           arC_bf[:, 10240:12288].rearrange("p (a b) -> p a b", a=4)]

    CLOB_B_ARB = ("qTg", "kTg", "vbuf", "r1a", "r1b", "r1c", "r1d", "wi0", "wi1")
    CLOB_B_ARC = ("ho", "xk0", "xk1")

    htok = nc.alloc_sbuf_tensor("htok", [128, 2048], BF16)
    bc1 = nc.alloc_sbuf_tensor("bc1", [128, 2048], BF16)
    bc2 = nc.alloc_sbuf_tensor("bc2", [128, 2048], BF16)
    stage = arB[0:1, 12304:12816]
    rowA = arB[0:1, 12816:13328]
    rowB = arB[0:1, 13328:13840]
    wstf = arB[:, 0:4096].rearrange("p (a b) -> p a b", a=16)
    c_sb = nc.alloc_sbuf_tensor("c_sb", [128, 16], F32)
    scsb = nc.alloc_sbuf_tensor("scsb", [128, 16], BF16)
    csts = nc.alloc_sbuf_tensor("csts", [128, 8], F32)
    identb = nc.alloc_sbuf_tensor("identb_s", [128, 128], BF16)
    identf = nc.alloc_sbuf_tensor("identf_s", [128, 128], F32)
    prot = nc.alloc_sbuf_tensor("prot_s", [128, 128], BF16)
    onesr = nc.alloc_sbuf_tensor("onesr", [1, 128], BF16)
    mrg = nc.alloc_sbuf_tensor("mrg_s", [128, NKC, 3], BF16)
    m_t = nc.alloc_sbuf_tensor("mt_s", [128, NKC], F32)
    u_sb = nc.alloc_sbuf_tensor("u_sb", [1, 16], F32)
    tmpf = nc.alloc_sbuf_tensor("tmpf", [128, 512], F32)
    qraw = tmpf[:, 0:256].bitcast(BF16)
    rotb = tmpf[:, 256:512].bitcast(BF16)
    qcos = nc.alloc_sbuf_tensor("qcos", [128, 512], BF16)
    qcos2 = nc.alloc_sbuf_tensor("qcos2", [128, 512], BF16)
    ca1col = nc.alloc_sbuf_tensor("ca1col", [128, 16], F32)
    sacol = nc.alloc_sbuf_tensor("sacol", [128, 16], F32)
    stats = nc.alloc_sbuf_tensor("stats", [128, 16], F32)
    nmx = nc.alloc_sbuf_tensor("nmx", [128, 2], F32)
    zvv = nc.alloc_sbuf_tensor("zvv", [128, 2], F32)
    hot = nc.alloc_sbuf_tensor("hot", [128, 128], F32)
    mz = nc.alloc_sbuf_tensor("mz", [128, 4], F32)
    mrow = nc.alloc_sbuf_tensor("mrow", [1, 8], F32)
    bmt = nc.alloc_sbuf_tensor("bmt", [1, 8], F32)
    v204 = nc.alloc_sbuf_tensor("v204", [1, 128], BF16)
    vescl = nc.alloc_sbuf_tensor("vescl", [1, 128], BF16)
    gateTb = nc.alloc_sbuf_tensor("gateTb", [128, 16, 8], BF16)
    upack = nc.alloc_sbuf_tensor("upack", [128, NE * HTOK], BF16)
    u1e = upack[0:16, :].rearrange("p (a b) -> p a b", a=NE)
    u3e = upack[32:48, :].rearrange("p (a b) -> p a b", a=NE)
    u1n = upack[64:80, :].rearrange("p (a b) -> p a b", a=NE)
    u3n = nc.alloc_sbuf_tensor("u3n", [NR, NE, HTOK], BF16)
    uP1 = upack[0:32, :].rearrange("p (a b) -> p a b", a=NE)
    uP3 = upack[32:64, :].rearrange("p (a b) -> p a b", a=NE)
    uN1 = upack[16:32, :].rearrange("p (a b) -> p a b", a=NE)
    uN3 = upack[48:64, :].rearrange("p (a b) -> p a b", a=NE)
    zsb = u1e
    ubt = arC_bf[:, 4096:5120].rearrange("p (a b) -> p a b", a=2)
    werows = nc.alloc_sbuf_tensor("werows", [NE, HTOK], BF16)
    wfull = nc.alloc_sbuf_tensor("wfull", [128, 4, NE], F32)
    lg = nc.alloc_sbuf_tensor("lg", [128, 4, NE], F32)
    t2s = nc.alloc_sbuf_tensor("t2s", [128, 8], F32)
    tmp8 = nc.alloc_sbuf_tensor("tmp8", [128, NE], F32)
    msk8 = nc.alloc_sbuf_tensor("msk8", [128, NE], F32)
    actb = nc.alloc_sbuf_tensor("actb", [128, 512], BF16)
    wactb = nc.alloc_sbuf_tensor("wactb", [128, 512], BF16)
    tmpf2 = nc.alloc_sbuf_tensor("tmpf2", [128, 512], F32)
    qraw2 = tmpf2[:, 0:256].bitcast(BF16)
    rotb2 = tmpf2[:, 256:512].bitcast(BF16)
    actb2 = nc.alloc_sbuf_tensor("actb2", [128, 512], BF16)
    wactb2 = nc.alloc_sbuf_tensor("wactb2", [128, 512], BF16)
    bpair = nc.alloc_sbuf_tensor("bpair", [64, 2, NE, 128], BF16)
    a2s = nc.alloc_sbuf_tensor("a2s", [128, 2, NE, NR], BF16)
    winit = [wbufA, wbufB]
    coskS = pT6
    sinkS = probsq
    b2dj = arC_bf[0:16, 4096:8192].rearrange("p (a b) -> p a b", a=NE)

    pBig = nc.alloc_psum_tensor("pBig", [128, 2048], F32)
    pH = nc.alloc_psum_tensor("pH", [128, 1024], BF16)
    zA = nc.alloc_psum_tensor("zA", [128, 512], F32)
    zB = nc.alloc_psum_tensor("zB", [128, 512], F32)
    zC = nc.alloc_psum_tensor("zC", [128, 512], F32)
    print("sbuf bytes remaining:", nc.sbuf_bytes_remaining,
          "psum banks remaining:", nc.psum_banks_remaining)
    PBQ = [(pBig[:, i * 512:(i + 1) * 512], "pBig%d" % i) for i in range(4)]
    PB_ALL = ("pBig0", "pBig1", "pBig2", "pBig3")

    def dma(eng, out, in_, reads=(), writes=()):
        if eng == "scalar":
            eng = "scalardma"
        elif eng == "vector":
            eng = "vectordma"
        return P.add(eng, lambda e, oo=out, ii=in_: e.dma_start(out=oo, in_=ii),
                     reads=reads, writes=writes, dma=True)

    def dmas(eng, pairs, reads=(), writes=()):
        if eng == "scalar":
            eng = "scalardma"
        elif eng == "vector":
            eng = "vectordma"
        def f(e, ps=tuple(pairs)):
            return [e.dma_start(out=oo, in_=ii) for (oo, ii) in ps]
        return P.add(eng, f, reads=reads, writes=writes, dma=True, nsub=len(pairs))

    def mm(out, lhsT, rhs, start, stop, reads=(), writes=(), skip_gc=False):
        ch = writes[0] if writes else None
        return P.add("tensor",
                     lambda e, oo=out, l=lhsT, r=rhs, st=start, sp=stop, sg=skip_gc:
                     e.matmul(oo, l, r, start=st, stop=sp, skip_group_check=sg),
                     reads=reads, writes=writes, chain=ch)

    def tr(out, in_, ident, reads=(), writes=()):
        return P.add("tensor",
                     lambda e, oo=out, ii=in_, dd=ident: e.transpose(oo, ii, dd),
                     reads=reads, writes=writes)

    def act(out, in_, func, bias=None, scale=None, accum=None, reads=(), writes=()):
        kw = {}
        if bias is not None:
            kw["bias"] = bias
        if scale is not None:
            kw["scale"] = scale
        if accum is not None:
            kw["accum_out"] = accum
        return P.add("scalar",
                     lambda e, oo=out, ii=in_, f=func, kw=kw: e.activation(oo, ii, f, **kw),
                     reads=reads, writes=writes)

    def scp(out, in_, reads=(), writes=()):
        return P.add("scalar", lambda e, oo=out, ii=in_: e.copy(oo, ii),
                     reads=reads, writes=writes)

    def vcp(out, in_, reads=(), writes=()):
        return P.add("vector", lambda e, oo=out, ii=in_: e.tensor_copy(oo, ii),
                     reads=reads, writes=writes)

    def tt(out, a, b, op, reads=(), writes=()):
        return P.add("vector", lambda e, oo=out, x=a, y=b, p=op: e.tensor_tensor(oo, x, y, p),
                     reads=reads, writes=writes)

    def ts(out, in_, s1, op0, s2=None, op1=ALU.bypass, reads=(), writes=()):
        return P.add("vector",
                     lambda e, oo=out, ii=in_, a=s1, b=s2, p0=op0, p1=op1:
                     e.tensor_scalar(oo, ii, a, b, op0=p0, op1=p1),
                     reads=reads, writes=writes)

    def stt(out, in0, scalar, in1, op0, op1, reads=(), writes=()):
        return P.add("vector",
                     lambda e, oo=out, ii=in0, s=scalar, jj=in1, p0=op0, p1=op1:
                     e.scalar_tensor_tensor(oo, ii, s, jj, op0=p0, op1=p1),
                     reads=reads, writes=writes)

    def red(out, in_, op, negate=False, reads=(), writes=()):
        return P.add("vector",
                     lambda e, oo=out, ii=in_, p=op, n=negate:
                     e.tensor_reduce(oo, ii, axis=AX.X, op=p, negate=n),
                     reads=reads, writes=writes)

    def recip(out, in_, reads=(), writes=()):
        return P.add("vector", lambda e, oo=out, ii=in_: e.reciprocal(oo, ii),
                     reads=reads, writes=writes)

    def memset(out, val, reads=(), writes=()):
        return P.add("vector", lambda e, oo=out, v=val: e.memset(oo, v),
                     reads=reads, writes=writes)

    _smalls = [
        (c_sb[:, :], dc[:, :], "c_sb"), (csts[:, :], dconsts[:, :], "csts"),
        (identb[:, :], didb[:, :], "identb"), (identf[:, :], didf[:, :], "identf"),
        (prot[:, :], dprot[:, :], "prot"), (onesr[:, :], dones[:, :], "onesr"),
        (mrg[:, :, :], dmrg[:, :, :], "mrg"), (m_t[:, :], dmt[:, :], "m_t"),
        (u_sb[:, :], du[:, :], "u_sb"),
    ]
    dmas("sync", [(d0, s0) for (d0, s0, _) in _smalls],
         writes=tuple(nm for (_, _, nm) in _smalls))
    act(scsb[:, :], c_sb[:, :], AF.Silu, reads=("c_sb",), writes=("scsb",))
    wstg = [arB[:, 0:4096].rearrange("p (a b) -> p a b", a=16),
            arB[:, 4096:8192].rearrange("p (a b) -> p a b", a=16)]

    def wmod_block(njp):
        for s in (0, 1):
            nj = njp + s
            r5 = nj % 5
            if r5 in (0, 2):
                dmas("gpsimd", [(winit[s][:, q * 4:(q + 1) * 4, :],
                                 dwmod[q * 512:(q + 1) * 512, nj * 256:(nj + 1) * 256]
                                 .rearrange("(a p) b -> p a b", p=128)) for q in range(4)],
                     reads=(), writes=("wi%d" % s,))
            else:
                qn = "scalardma" if r5 in (1, 3) else "sync"
                dmas(qn, [(wstg[s][:, q * 4:(q + 1) * 4, :],
                           dwmod[q * 512:(q + 1) * 512, nj * 256:(nj + 1) * 256]
                           .rearrange("(a p) b -> p a b", p=128)) for q in range(4)],
                     reads=(), writes=("wst%d" % s,))
                P.add("vector",
                      lambda e, s=s: e.tensor_copy(winit[s][:, :, :], wstg[s][:, :, :]),
                      reads=("wst%d" % s,), writes=("wi%d" % s,))
        for kc in range(16):
            for s in (0, 1):
                zsl, zslr = (zA, "zA") if s == 0 else (zB, "zB")
                mm(zsl[0:1, 0:256], scsb[:, kc:kc + 1], winit[s][:, kc, 0:256],
                   kc == 0, kc == 15, reads=("scsb", "wi%d" % s), writes=(zslr,))
        for s in (0, 1):
            nj = njp + s
            zsl, zslr = (zA, "zA") if s == 0 else (zB, "zB")
            stg = stage[0:1, 0:256] if s == 0 else stage[0:1, 256:512]
            stgr = "stage" if s == 0 else "stage2"
            P.add("scalar",
                  lambda e, zsl=zsl, stg=stg: e.copy(stg, zsl[0:1, 0:256]),
                  reads=(zslr,), writes=(stgr, zslr))
            dma("sync", dscrM[0:1, nj * 256:(nj + 1) * 256], stg,
                reads=(stgr,), writes=("dscrM",))

    def wmod_tail():
        P.add("sync", lambda e: e.dma_start(out=dsum[0:1, :], in_=dscrM[0:1, :]),
              reads=("dscrM",), writes=("dsum",), dma=True)
        for slot in (1, 4):
            for ch in range(4):
                off = slot * 2048 + ch * 512
                dma("sync", rowA[0:1, :], dsum[0:1, off:off + 512],
                    reads=("dsum",), writes=("rowA",))
                ts(rowB[0:1, :], rowA[0:1, :], 1.0, ALU.add, reads=("rowA",), writes=("rowB",))
                dma("sync", dscrP[0:1, off:off + 512], rowB[0:1, :],
                    reads=("rowB",), writes=("dscrP",))
        for slot in (2, 5):
            for ch in range(4):
                off = slot * 2048 + ch * 512
                dma("sync", rowA[0:1, :], dsum[0:1, off:off + 512],
                    reads=("dsum",), writes=("rowA",))
                P.add("scalar", lambda e: e.activation(rowB[0:1, :], rowA[0:1, :], AF.Sign),
                      reads=("rowA",), writes=("rowB",))
                P.add("scalar", lambda e: e.activation(rowA[0:1, :], rowA[0:1, :], AF.Abs),
                      reads=("rowA",), writes=("rowA",))
                ts(rowA[0:1, :], rowA[0:1, :], 0.25, ALU.max, reads=("rowA",), writes=("rowA",))
                tt(rowB[0:1, :], rowB[0:1, :], rowA[0:1, :], ALU.mult,
                   reads=("rowA",), writes=("rowB",))
                dma("sync", dscrP[0:1, off:off + 512], rowB[0:1, :],
                    reads=("rowB",), writes=("dscrP",))

    def bcast(dst, slot, name):
        dsrc = dsum if slot in (0, 3) else dscrP
        rn = "dsum" if slot in (0, 3) else "dscrP"
        return dma("gpsimd", dst,
                   dsrc[0:1, slot * 2048:(slot + 1) * 2048].partition_broadcast(128),
                   reads=(rn,), writes=(name,))

    def ln_norm_tile(i, src_dram, row0, dstview, dstname, dst_ti):
        xb = xkb[i % 2]
        xr = "xk%d" % (i % 2)
        dma("sync", xb[:, :], src_dram[row0:row0 + 128, :], reads=(), writes=(xr,))
        sc0 = 8 * (i % 2)
        red(stats[:, sc0:sc0 + 1], xb[:, :], ALU.add, reads=(xr,), writes=("stats%d" % (i % 2),))
        act(pBig[:, :], xb[:, :], AF.Square, accum=stats[:, sc0 + 1:sc0 + 2],
            reads=(xr,) + PB_ALL, writes=PB_ALL + ("stats%d" % (i % 2),))
        ts(stats[:, sc0 + 2:sc0 + 3], stats[:, sc0:sc0 + 1], 1.0 / 2048, ALU.mult,
           reads=("stats%d" % (i % 2),), writes=("stats%d" % (i % 2),))
        tt(stats[:, sc0 + 3:sc0 + 4], stats[:, sc0 + 2:sc0 + 3], stats[:, sc0 + 2:sc0 + 3],
           ALU.mult, reads=("stats%d" % (i % 2),), writes=("stats%d" % (i % 2),))
        stt(stats[:, sc0 + 4:sc0 + 5], stats[:, sc0 + 1:sc0 + 2], 1.0 / 2048,
            stats[:, sc0 + 3:sc0 + 4], ALU.mult, ALU.subtract,
            reads=("stats%d" % (i % 2),), writes=("stats%d" % (i % 2),))
        act(stats[:, sc0 + 5:sc0 + 6], stats[:, sc0 + 4:sc0 + 5], AF.Sqrt, bias=csts[:, 0:1],
            reads=("stats%d" % (i % 2), "csts"), writes=("stats%d" % (i % 2),))
        recip(stats[:, sc0 + 6:sc0 + 7], stats[:, sc0 + 5:sc0 + 6],
              reads=("stats%d" % (i % 2),), writes=("stats%d" % (i % 2),))
        hb, hbr = (htok, "htok") if i % 2 == 0 else (bc2, "bc2n")
        ts(hb[:, :], xb[:, :], stats[:, sc0 + 2:sc0 + 3], ALU.subtract,
           stats[:, sc0 + 6:sc0 + 7], ALU.mult,
           reads=(xr, "stats%d" % (i % 2)), writes=(hbr,))
        for g2 in range(2):
            for j in range(8):
                fc = g2 * 8 + j
                tr(pH[:, j * 128:(j + 1) * 128], hb[:, fc * 128:(fc + 1) * 128], identb[:, :],
                   reads=(hbr, "identb", "pH%d" % j), writes=("pH%d" % j,))
            P.add("scalar",
                  lambda e, dv=dstview, g2=g2, ti=dst_ti:
                  e.copy(dv[:, g2 * 8:(g2 + 1) * 8, ti * 128:(ti + 1) * 128],
                         pH[:, :].rearrange("p (a b) -> p a b", a=8)),
                  reads=tuple("pH%d" % k for k in range(8)),
                  writes=(dstname,) + tuple("pH%d" % k for k in range(8)))

    def ln_mod_tile(i, src_dram, row0, dstview, dstname, dst_ti):
        xb = xkb[i % 2]
        xr = "xk%d" % (i % 2)
        dma("sync", xb[:, :], src_dram[row0:row0 + 128, :], reads=(), writes=(xr,))
        red(stats[:, 0:1], xb[:, :], ALU.add, reads=(xr,), writes=("stats",))
        act(pBig[:, :], xb[:, :], AF.Square, accum=stats[:, 1:2],
            reads=(xr,) + PB_ALL, writes=PB_ALL + ("stats",))
        ts(stats[:, 2:3], stats[:, 0:1], 1.0 / 2048, ALU.mult, reads=("stats",), writes=("stats",))
        tt(stats[:, 3:4], stats[:, 2:3], stats[:, 2:3], ALU.mult, reads=("stats",), writes=("stats",))
        stt(stats[:, 4:5], stats[:, 1:2], 1.0 / 2048, stats[:, 3:4], ALU.mult, ALU.subtract,
            reads=("stats",), writes=("stats",))
        act(stats[:, 5:6], stats[:, 4:5], AF.Sqrt, bias=csts[:, 0:1],
            reads=("stats", "csts"), writes=("stats",))
        recip(stats[:, 6:7], stats[:, 5:6], reads=("stats",), writes=("stats",))
        ts(pBig[:, :], xb[:, :], stats[:, 2:3], ALU.subtract, stats[:, 6:7], ALU.mult,
           reads=(xr, "stats") + PB_ALL, writes=PB_ALL)
        tt(htok[:, :], pBig[:, :], bc1[:, :], ALU.mult,
           reads=PB_ALL + ("bc1",), writes=("htok",) + PB_ALL)
        tt(htok[:, :], htok[:, :], bc2[:, :], ALU.add, reads=("bc2",), writes=("htok",))
        for g2 in range(2):
            for j in range(8):
                fc = g2 * 8 + j
                tr(pH[:, j * 128:(j + 1) * 128], htok[:, fc * 128:(fc + 1) * 128], identb[:, :],
                   reads=("htok", "identb", "pH%d" % j), writes=("pH%d" % j,))
            P.add("scalar",
                  lambda e, dv=dstview, g2=g2, ti=dst_ti:
                  e.copy(dv[:, g2 * 8:(g2 + 1) * 8, ti * 128:(ti + 1) * 128],
                         pH[:, :].rearrange("p (a b) -> p a b", a=8)),
                  reads=tuple("pH%d" % k for k in range(8)),
                  writes=(dstname,) + tuple("pH%d" % k for k in range(8)))

    ln_calls = []
    for i in range(8):
        ln_calls.append((2 * i, dx, i * 128, hq, "hq", i))
        ln_calls.append((2 * i + 1, dxk, i * 128, hk, "hk", i))
    done = 0
    for bi, njp in enumerate(range(0, 48, 2)):
        wmod_block(njp)
        want = (bi + 1) * len(ln_calls) // 24
        while done < want:
            ln_norm_tile(*ln_calls[done])
            done += 1
    wmod_tail()
    bcast(bc1[:, :], 1, "bc1")

    bcast(bc2[:, :], 0, "bc2b")
    P.ops[-1].waits.extend(P.lr.get("bc2n", ()))
    for vsrc, vrn, vdst, vdr in ((bc1, "bc1", ca1col, "ca1col"), (bc2, "bc2b", sacol, "sacol")):
        for f in range(16):
            tr(pH[:, f * 64:f * 64 + 1], vsrc[0:1, f * 128:(f + 1) * 128], identb[0:1, 0:1],
               reads=(vrn, "identb", "pH%d" % (f // 2)), writes=("pH%d" % (f // 2),))
        pH3 = pH[:, :].rearrange("p (a b) -> p a b", a=16)
        P.add("scalar", lambda e, vdst=vdst, pH3=pH3: e.copy(vdst[:, :], pH3[:, :, 0:1]),
              reads=tuple("pH%d" % k for k in range(8)),
              writes=(vdr,) + tuple("pH%d" % k for k in range(8)))
    for dv, dn in ((hq, "hq"), (hk, "hk")):
        for fc in range(16):
            ts(dv[:, fc, :], dv[:, fc, :], ca1col[:, fc:fc + 1], ALU.mult,
               sacol[:, fc:fc + 1], ALU.add, reads=("ca1col", "sacol"), writes=(dn,))

    memset(vbuf[:, :, :, 128:129], 1.0, writes=("vbuf",))

    wslot = [wbufA, wbufB]
    wres = ["wi0", "wi1"]
    slot_i = [0]

    def load_wslice(wdram, col0, width=256):
        s = slot_i[0]; slot_i[0] ^= 1
        wb = wslot[s]; rn = wres[s]
        dmas("gpsimd", [(wb[:, q * 4:(q + 1) * 4, 0:width],
                         wdram[q * 512:(q + 1) * 512, col0:col0 + width]
                         .rearrange("(a p) b -> p a b", p=128)) for q in range(4)],
             reads=(), writes=(rn,))
        return wb, rn

    def rope_evac(pp, pres, cos_s, sin_s, crs, dstv, dname, hh, n):
        par = (hh * 2 + n) % 2
        qr, qrr = (qraw, "qraw") if par == 0 else (qraw2, "tmpf2")
        rb, rbr = (rotb, "rotb") if par == 0 else (rotb2, "tmpf2")
        qc, qcr = (qcos, "qcos") if par == 0 else (qcos2, "qcos2")
        zsl, zslr = (zB, "zB") if par == 0 else (zC, "zC")
        scp(qr[:, :], pp, reads=(pres,), writes=(qrr, pres))
        mm(zsl[:, :], prot[:, :], qr[:, :], True, True,
           reads=("prot", qrr, zslr), writes=(zslr,))
        tt(qc[:, :], qr[:, :], cos_s, ALU.mult, reads=(qrr, crs[0]), writes=(qcr,))
        scp(rb[:, :], zsl[:, :], reads=(zslr,), writes=(rbr, zslr))
        dsl = dstv[:, hh, n * 512:(n + 1) * 512]
        tt(dsl, rb[:, :], sin_s, ALU.mult, reads=(rbr, crs[1]), writes=(dname, "wst"))
        tt(dsl, dsl, qc[:, :], ALU.add, reads=(qcr,), writes=(dname,))

    def smm(hh, qt):
        pp, pres = (pBig[:, 0:1024], ("pBig0", "pBig1")) if qt % 2 == 0 else \
                   (pBig[:, 1024:2048], ("pBig2", "pBig3"))
        for half in range(2):
            mm(pp[:, half * 512:(half + 1) * 512],
               qTg[:, hh, qt * 128:(qt + 1) * 128], kTg[:, hh, half * 512:(half + 1) * 512],
               True, True, reads=("qTg", "kTg") + pres, writes=pres)
        nslot = qt % 2
        red(nmx[:, nslot:nslot + 1], pp, ALU.max, negate=True,
            reads=pres, writes=("nmx%d" % nslot,))

    def stail(hh, qt, pdst, pdres):
        pp, pres = (pBig[:, 0:1024], ("pBig0", "pBig1")) if qt % 2 == 0 else \
                   (pBig[:, 1024:2048], ("pBig2", "pBig3"))
        nslot = qt % 2
        act(probsq[:, :], pp, AF.Exp, bias=nmx[:, nslot:nslot + 1],
            reads=pres + ("nmx%d" % nslot, "r1d"), writes=("r1d",) + pres)
        for kc in range(8):
            tr(pH[:, kc * 128:(kc + 1) * 128], probsq[:, kc * 128:(kc + 1) * 128],
               identb[:, :], reads=("r1d", "identb", "pH%d" % kc), writes=("pH%d" % kc,))
        vcp(pdst, pH[:, :], reads=tuple("pH%d" % k for k in range(8)),
            writes=(pdres,) + tuple("pH%d" % k for k in range(8)))

    def pv(head, hh, qt, psrc, psres):
        psrc3 = psrc.rearrange("p (a b) -> p a b", a=8)
        for kc in range(8):
            K = 127 if kc == 7 else 128
            mm(zA[:, 0:129], psrc3[0:K, kc, :], vbuf[0:K, kc, hh, 0:129],
               kc == 0, kc == 7, reads=(psres, "vbuf", "zA"), writes=("zA",))
        recip(zvv[:, 0:1], zA[:, 128:129], reads=("zA",), writes=("zvv",))
        ts(hot[:, :], zA[:, 0:128], zvv[:, 0:1], ALU.mult,
           reads=("zA", "zvv"), writes=("hot", "zA"))
        tr(zB[:, 0:128], hot[:, :], identf[:, :], reads=("hot", "identf", "zB"), writes=("zB",))
        P.add("scalar",
              lambda e, head=head, qt=qt:
              e.copy(ho[:, head, qt * 128:(qt + 1) * 128], zB[:, 0:128]),
              reads=("zB",), writes=("ho", "zB"))

    cast_pieces = []
    if PH == "ALL":
        cast_pieces.append((gateTb[:, :, :],
                            dgate[:, :].rearrange("(a p) b -> p a b", p=128), "gateTb"))
        for q in range(4):
            for h in range(4):
                cast_pieces.append((dw13b[q * 512:(q + 1) * 512, h * 2816:(h + 1) * 2816],
                                    dw13[q * 512:(q + 1) * 512, h * 2816:(h + 1) * 2816],
                                    "dw13b"))
        for q in range(4):
            for h in range(2):
                cast_pieces.append((dw2b[q * 1408:(q + 1) * 1408, h * 1024:(h + 1) * 1024],
                                    dw2[q * 1408:(q + 1) * 1408, h * 1024:(h + 1) * 1024],
                                    "dw2b"))

    def drip_cast(n=1):
        for _ in range(n):
            if cast_pieces:
                dst, src, nm = cast_pieces.pop(0)
                dma("gpsimd", dst, src, reads=(), writes=(nm,))

    for g in range(4):
        dmas("sync", [(cosq_s, dcosq[:, :]), (sinq_s, dsinq[:, :]),
                      (coskS, dcosk[:, :]), (sinkS, dsink[:, :])],
             reads=(), writes=("r1a", "r1b", "r1c", "r1d"))
        for hp in range(2):
            wb, rn = load_wslice(dwq, g * 512 + hp * 256)
            for hh in (hp * 2, hp * 2 + 1):
                pps = [PBQ[(hh * 2 + n) % 4] for n in range(2)]
                for kc in range(16):
                    for n in range(2):
                        pp, pres = pps[n]
                        mm(pp, wb[:, kc, (hh - hp * 2) * 128:(hh - hp * 2 + 1) * 128],
                           hq[:, kc, n * 512:(n + 1) * 512],
                           kc == 0, kc == 15, reads=(rn, "hq", pres), writes=(pres,))
                for n in range(2):
                    pp, pres = pps[n]
                    rope_evac(pp, pres, cosq_s[:, n * 512:(n + 1) * 512],
                              sinq_s[:, n * 512:(n + 1) * 512], ("r1a", "r1b"), qTg, "qTg", hh, n)
        for hp in range(2):
            wb, rn = load_wslice(dwk, g * 512 + hp * 256)
            for hh in (hp * 2, hp * 2 + 1):
                pps = [PBQ[(hh * 2 + n) % 4] for n in range(2)]
                for kc in range(16):
                    for n in range(2):
                        pp, pres = pps[n]
                        mm(pp, wb[:, kc, (hh - hp * 2) * 128:(hh - hp * 2 + 1) * 128],
                           hk[:, kc, n * 512:(n + 1) * 512],
                           kc == 0, kc == 15, reads=(rn, "hk", pres), writes=(pres,))
                for n in range(2):
                    pp, pres = pps[n]
                    rope_evac(pp, pres, coskS[:, n * 512:(n + 1) * 512],
                              sinkS[:, n * 512:(n + 1) * 512], ("r1c", "r1d"), kTg, "kTg", hh, n)
        for hp in range(2):
            wb, rn = load_wslice(dwv, g * 512 + hp * 256)
            for ktp in range(0, 8, 2):
                for kc in range(16):
                    for kt in (ktp, ktp + 1):
                        pp, pres = PBQ[kt % 4]
                        mm(pp[:, 0:256], hk[:, kc, kt * 128:(kt + 1) * 128], wb[:, kc, 0:256],
                           kc == 0, kc == 15, reads=(rn, "hk", pres), writes=(pres,))
                for kt in (ktp, ktp + 1):
                    pp, pres = PBQ[kt % 4]
                    P.add("scalar",
                          lambda e, pp=pp, kt=kt, hp=hp:
                          e.copy(vbuf[:, kt, hp * 2:hp * 2 + 2, 0:128],
                                 pp[:, 0:256].rearrange("p (a b) -> p a b", a=2)),
                          reads=(pres,), writes=("vbuf", pres))

        for hh in range(4):
            head = g * 4 + hh
            drip_cast(1)

            def slot(qt):
                return (pTc, "r1b") if qt % 2 == 0 else (pT6, "r1c")

            smm(hh, 7)
            smm(hh, 0)
            stail(hh, 7, pT7, "r1a")
            drip_cast(1)
            pT73 = pT7.rearrange("p (a b) -> p a b", a=8)
            for kc in range(8):
                mm(zC[:, 0:3], pT73[:, kc, :], mrg[:, kc, :], kc == 0, kc == 7,
                   reads=("r1a", "mrg", "zC"), writes=("zC",))
            vcp(mz[:, 0:3], zC[:, 0:3], reads=("zC",), writes=("mz", "zC"))
            dmas("sync", [(mrow[0:1, 0:3], mz[127:128, 0:3]),
                          (v204[0:1, :], vbuf[76:77, 1, hh, 0:128])],
                 reads=("mz", "vbuf"), writes=("mrow", "v204"))
            for qt in range(6):
                smm(hh, qt + 1) if qt < 5 else smm(hh, 6)
                if qt >= 1:
                    pv(head, hh, qt - 1, *slot(qt - 1))
                stail(hh, qt, *slot(qt))
            pv(head, hh, 5, *slot(5))
            stail(hh, 6, pTc, "r1b")
            recip(bmt[0:1, 0:1], mrow[0:1, 2:3], reads=("mrow",), writes=("bmt",))
            ts(bmt[0:1, 1:2], mrow[0:1, 0:1], bmt[0:1, 0:1], ALU.mult, csts[0:1, 2:3], ALU.mult,
               reads=("mrow", "bmt", "csts"), writes=("bmt",))
            ts(bmt[0:1, 1:2], bmt[0:1, 1:2], csts[0:1, 1:2], ALU.add,
               reads=("bmt", "csts"), writes=("bmt",))
            ts(bmt[0:1, 2:3], mrow[0:1, 1:2], bmt[0:1, 0:1], ALU.mult,
               reads=("mrow", "bmt"), writes=("bmt",))
            recip(bmt[0:1, 3:4], bmt[0:1, 1:2], reads=("bmt",), writes=("bmt",))
            ts(bmt[0:1, 4:5], bmt[0:1, 2:3], bmt[0:1, 3:4], ALU.mult, reads=("bmt",), writes=("bmt",))
            ts(bmt[0:1, 4:5], bmt[0:1, 4:5], 1.0, ALU.min, 0.0, ALU.max,
               reads=("bmt",), writes=("bmt",))
            ts(bmt[0:1, 5:6], bmt[0:1, 4:5], u_sb[0:1, head:head + 1], ALU.is_gt,
               reads=("bmt", "u_sb"), writes=("bmt",))
            ts(bmt[0:1, 6:7], bmt[0:1, 5:6], csts[0:1, 3:4], ALU.mult,
               reads=("bmt", "csts"), writes=("bmt",))
            ts(vescl[0:1, :], v204[0:1, :], bmt[0:1, 6:7], ALU.mult,
               reads=("v204", "bmt"), writes=("vescl",))
            mm(zC[:, 0:128], onesr[0:1, :], vescl[0:1, :], True, True,
               reads=("onesr", "vescl", "zC"), writes=("zC",))
            for kt in range(8):
                stt(vbuf[:, kt, hh, 0:128], zC[:, 0:128], m_t[:, kt:kt + 1],
                    vbuf[:, kt, hh, 0:128], ALU.mult, ALU.add,
                    reads=("zC", "m_t", "vbuf"), writes=("vbuf",))
            pv(head, hh, 6, pTc, "r1b")
            pv(head, hh, 7, pT7, "r1a")

    bcast(bc1[:, :], 2, "bc1")
    dmas("sync", [(xres[:, i, :], dx[i * 128:(i + 1) * 128, :]) for i in range(8)],
         reads=(), writes=tuple("x%d" % i for i in range(8)) + ("hq", "hk"))
    for dj in range(8):
        wb, rn = load_wslice(dwo, dj * 256)
        for tip in range(0, 8, 2):
            for kc in range(16):
                for ti in (tip, tip + 1):
                    pp, pres = PBQ[ti % 2]
                    mm(pp[:, 0:256], ho[:, kc, ti * 128:(ti + 1) * 128], wb[:, kc, 0:256],
                       kc == 0, kc == 15, reads=(rn, "ho", pres), writes=(pres,))
            for ti in (tip, tip + 1):
                pp, pres = PBQ[ti % 2]
                tf = tmpf[:, 0:256] if ti % 2 == 0 else tmpf2[:, 0:256]
                tfr = "tmpf" if ti % 2 == 0 else "tmpf2"
                tt(tf, pp[:, 0:256], bc1[:, dj * 256:(dj + 1) * 256], ALU.mult,
                   reads=(pres, "bc1"), writes=(tfr, "qraw", "rotb", pres))
                tt(xres[:, ti, dj * 256:(dj + 1) * 256], xres[:, ti, dj * 256:(dj + 1) * 256],
                   tf, ALU.add, reads=(tfr,), writes=("x%d" % ti,))
    tp = tap("xattn", [TOK, D])
    if tp is not None:
        for i in range(8):
            dma("sync", tp[i * 128:(i + 1) * 128, :], xres[:, i, :], reads=("x%d" % i,))

    if PH == "ALL":
        for hf in range(2):
            dma("gpsimd", aTbv[:, :, :], da1[:, :].rearrange("(a p) b -> p a b", p=128),
                reads=(), writes=("aTb",) + CLOB_B_ARC)
            bcast(bc1[:, :], 4, "bc1")
            bcast(bc2[:, :], 3, "bc2")
            for ti in range(4):
                gi = hf * 4 + ti
                xv = xres[:, gi, :]
                xr = "x%d" % gi
                red(stats[:, 0:1], xv, ALU.add, reads=(xr,), writes=("stats",))
                act(pBig[:, :], xv, AF.Square, accum=stats[:, 1:2],
                    reads=(xr,) + PB_ALL, writes=PB_ALL + ("stats",))
                ts(stats[:, 2:3], stats[:, 0:1], 1.0 / 2048, ALU.mult,
                   reads=("stats",), writes=("stats",))
                tt(stats[:, 3:4], stats[:, 2:3], stats[:, 2:3], ALU.mult,
                   reads=("stats",), writes=("stats",))
                stt(stats[:, 4:5], stats[:, 1:2], 1.0 / 2048, stats[:, 3:4],
                    ALU.mult, ALU.subtract, reads=("stats",), writes=("stats",))
                act(stats[:, 5:6], stats[:, 4:5], AF.Sqrt, bias=csts[:, 0:1],
                    reads=("stats", "csts"), writes=("stats",))
                recip(stats[:, 6:7], stats[:, 5:6], reads=("stats",), writes=("stats",))
                ts(pBig[:, :], xv, stats[:, 2:3], ALU.subtract, stats[:, 6:7], ALU.mult,
                   reads=(xr, "stats") + PB_ALL, writes=PB_ALL)
                tt(htok[:, :], pBig[:, :], bc1[:, :], ALU.mult,
                   reads=PB_ALL + ("bc1",), writes=("htok",) + PB_ALL)
                tt(htok[:, :], htok[:, :], bc2[:, :], ALU.add,
                   reads=("bc2",), writes=("htok",))
                for g2 in range(2):
                    for j in range(8):
                        fc = g2 * 8 + j
                        tr(pH[:, j * 128:(j + 1) * 128], htok[:, fc * 128:(fc + 1) * 128],
                           identb[:, :], reads=("htok", "identb", "pH%d" % j),
                           writes=("pH%d" % j,))
                    P.add("scalar",
                          lambda e, g2=g2, ti=ti:
                          e.copy(h2T[:, g2 * 8:(g2 + 1) * 8, ti * 128:(ti + 1) * 128],
                                 pH[:, :].rearrange("p (a b) -> p a b", a=8)),
                          reads=tuple("pH%d" % k for k in range(8)),
                          writes=("h2T",) + tuple("pH%d" % k for k in range(8))
                          + (("wi0", "wi1") if ti == 0 else ()))
            for daT, uslot, ur in [(da1, 0, "ubt0"), (da3, 1, "ubt1")]:
                if uslot == 1:
                    dma("gpsimd", aTbv[:, :, :], daT[:, :].rearrange("(a p) b -> p a b", p=128),
                        reads=(), writes=("aTb",) + CLOB_B_ARC)
                for kc in range(16):
                    mm(zA[:, :], aTbv[:, kc, :], h2T[:, kc, :], kc == 0, kc == 15,
                       reads=("aTb", "h2T", "zA"), writes=("zA",))
                ts(ubt[:, uslot, :], zA[:, :], 2.0, ALU.mult,
                   reads=("zA",), writes=(ur, "w13s0", "zA") + CLOB_B_ARC)
            dmas("sync", [(u1e[:, e_, :], ubt[e_ * 16:(e_ + 1) * 16, 0, :]) for e_ in range(NE)]
                 + [(u3e[:, e_, :], ubt[e_ * 16:(e_ + 1) * 16, 1, :]) for e_ in range(NE)],
                 reads=("ubt0", "ubt1"), writes=("u1e", "u3e"))
            ts(u1n[:, :, :], u1e[:, :, :], -1.0, ALU.mult, reads=("u1e",), writes=("u1n",))
            ts(u3n[:, :, :], u3e[:, :, :], -1.0, ALU.mult, reads=("u3e",), writes=("u3n",))
            dmas("gpsimd", [(uN1[:, e_, :], u1n[:, e_ - 1, :]) for e_ in range(1, NE)]
                 + [(uN3[:, e_, :], u3n[:, e_ - 1, :]) for e_ in range(1, NE)],
                 reads=("u1n", "u3n"), writes=("uP1", "uP3"))

            for tip in range(0, 4, 2):
                for kc in range(16):
                    for ti in (tip, tip + 1):
                        zg, zgr = (zB, "zB") if ti % 2 == 0 else (zC, "zC")
                        mm(zg[:, 0:8], h2T[:, kc, ti * 128:(ti + 1) * 128], gateTb[:, kc, :],
                           kc == 0, kc == 15, reads=("h2T", "gateTb", zgr), writes=(zgr,))
                for ti in (tip, tip + 1):
                    zg, zgr = (zB, "zB") if ti % 2 == 0 else (zC, "zC")
                    vcp(lg[:, ti, :], zg[:, 0:8], reads=(zgr,), writes=("lg", zgr))
                for ti in (tip, tip + 1):
                    lv = lg[:, ti, :]
                    red(t2s[:, 0:1], lv, ALU.max, negate=True, reads=("lg",), writes=("t2s",))
                    ts(tmp8[:, :], lv, t2s[:, 0:1], ALU.add, reads=("lg", "t2s"), writes=("tmp8",))
                    ts(msk8[:, :], tmp8[:, :], 0.0, ALU.is_lt, reads=("tmp8",), writes=("msk8",))
                    tt(tmp8[:, :], lv, msk8[:, :], ALU.mult, reads=("lg", "msk8"), writes=("tmp8",))
                    ts(msk8[:, :], msk8[:, :], 1.0, ALU.subtract, reads=(), writes=("msk8",))
                    stt(tmp8[:, :], msk8[:, :], 30000.0, tmp8[:, :], ALU.mult, ALU.add,
                        reads=("msk8",), writes=("tmp8",))
                    red(t2s[:, 1:2], tmp8[:, :], ALU.max, reads=("tmp8",), writes=("t2s",))
                    act(t2s[:, 2:3], t2s[:, 1:2], AF.Exp, bias=t2s[:, 0:1],
                        reads=("t2s",), writes=("t2s",))
                    ts(t2s[:, 3:4], t2s[:, 2:3], 1.0, ALU.add, reads=("t2s",), writes=("t2s",))
                    recip(t2s[:, 4:5], t2s[:, 3:4], reads=("t2s",), writes=("t2s",))
                    for e_ in range(NE):
                        act(t2s[:, 5:6], lv[:, e_:e_ + 1], AF.Exp, bias=t2s[:, 0:1],
                            reads=("lg", "t2s"), writes=("t2s",))
                        ts(t2s[:, 6:7], lv[:, e_:e_ + 1], t2s[:, 1:2], ALU.is_ge,
                           reads=("lg", "t2s"), writes=("t2s",))
                        tt(t2s[:, 7:8], t2s[:, 5:6], t2s[:, 6:7], ALU.mult,
                           reads=("t2s",), writes=("t2s",))
                        tt(wfull[:, ti, e_:e_ + 1], t2s[:, 7:8], t2s[:, 4:5], ALU.mult,
                           reads=("t2s",), writes=("wfull",))
            tpw = tap("wfull", [128, 8, 8])
            if tpw is not None:
                dma("sync", tpw[:, hf * 4:(hf + 1) * 4, :], wfull[:, :, :], reads=("wfull",))
            tpl = tap("lgt", [128, 8, 8])
            if tpl is not None:
                dma("sync", tpl[:, hf * 4:(hf + 1) * 4, :], lg[:, :, :], reads=("lg",))
            for ti in range(4):
                tr(zB[0:8, ti * 128:(ti + 1) * 128], wfull[:, ti, :], identf[:, :],
                   reads=("wfull", "identf", "zB"), writes=("zB",))
            vcp(werows[:, :], zB[0:8, :], reads=("zB",), writes=("werows", "zB"))
            dma("sync", dwe[:, :], werows[:, :], reads=("werows",), writes=("dwe",))
            dmas("gpsimd", [(webc[:, e_, :], dwe[e_:e_ + 1, :].partition_broadcast(128))
                            for e_ in range(NE)],
                 reads=("dwe",), writes=("webc",) + CLOB_B_ARC)

            zts = [zA, zB, zC]
            zrs = ["zA", "zB", "zC"]
            TMP = [(tmpf, "tmpf", ("qraw", "rotb")), (tmpf2, "tmpf2", ())]
            ACTB = [(actb, "actb"), (actb2, "actb2")]
            WACT = [(wactb, "wactb"), (wactb2, "wactb2")]
            ACCF = [(accf, "accf"), (accf2, "accf2")]
            for fcp in range(0, NFC, 2):
                for w in (0, 1):
                    fc = fcp + w
                    sl = fc % 2
                    wr = "w13s%d" % sl
                    dmas("sync", [(w13s[sl][:, q * 4:(q + 1) * 4, :],
                                   dw13b[q * 512:(q + 1) * 512, fc * 256:(fc + 1) * 256]
                                   .rearrange("(a p) b -> p a b", p=128)) for q in range(4)],
                         reads=("dw13b",),
                         writes=(wr,) + (("ubt0", "ubt1") + CLOB_B_ARC if fc < 2 else ()))
                    dmas("gpsimd", [
                        (bpair[0:16, sl, :, :], db13[:, :, 0, fc * 128:(fc + 1) * 128]),
                        (bpair[16:32, sl, 1:NE, :],
                         db13[:, 0:NE - 1, 0, fc * 128:(fc + 1) * 128]),
                        (bpair[32:48, sl, :, :], db13[:, :, 1, fc * 128:(fc + 1) * 128]),
                        (bpair[48:64, sl, 1:NE, :],
                         db13[:, 0:NE - 1, 1, fc * 128:(fc + 1) * 128]),
                        (a2s[:, sl, :, :], da2[fc * 128:(fc + 1) * 128, :, :]),
                    ], reads=(), writes=("b13s%d" % sl, "a2s%d" % sl))
                for w in (0, 1):
                    fc = fcp + w
                    sl = fc % 2
                    wr = "w13s%d" % sl
                    B1, B1r = PBQ[0 + 2 * w]
                    B3, B3r = PBQ[1 + 2 * w]
                    for kc in range(16):
                        mm(B1, w13s[sl][:, kc, 0:128], h2T[:, kc, :], kc == 0, False,
                           reads=(wr, "h2T", B1r), writes=(B1r,), skip_gc=True)
                        mm(B3, w13s[sl][:, kc, 128:256], h2T[:, kc, :], kc == 0, False,
                           reads=(wr, "h2T", B3r), writes=(B3r,), skip_gc=True)
                def flush_z():
                    while pend_z:
                        zfc, zsl2, ze, zw = pend_z.pop(0)
                        wb_, wbr = WACT[zw]
                        zt = zts[ze // 3]
                        zo = 32 * (ze % 3)
                        mm(zt[zo:zo + 16, :], a2s[:, zsl2, ze, :], wb_[:, :],
                           zfc == 0, zfc == NFC - 1,
                           reads=("a2s%d" % zsl2, wbr, zrs[ze // 3]),
                           writes=(zrs[ze // 3],), skip_gc=True)

                pend_z = []
                for e_ in range(NE):
                    for w in (0, 1):
                        sl = (fcp + w) % 2
                        B1, B1r = PBQ[0 + 2 * w]
                        B3, B3r = PBQ[1 + 2 * w]
                        if e_ == 0:
                            mm(B1, bpair[0:16, sl, 0, :], u1e[:, 0, :], False, False,
                               reads=("b13s%d" % sl, "u1e", B1r), writes=(B1r,), skip_gc=True)
                            mm(B3, bpair[32:48, sl, 0, :], u3e[:, 0, :], False, False,
                               reads=("b13s%d" % sl, "u3e", B3r), writes=(B3r,), skip_gc=True)
                        else:
                            mm(B1, bpair[0:32, sl, e_, :], uP1[:, e_, :], False, e_ == NE - 1,
                               reads=("b13s%d" % sl, "uP1", "u1e", B1r), writes=(B1r,),
                               skip_gc=True)
                            mm(B3, bpair[32:64, sl, e_, :], uP3[:, e_, :], False, e_ == NE - 1,
                               reads=("b13s%d" % sl, "uP3", "u3e", B3r), writes=(B3r,),
                               skip_gc=True)
                    flush_z()
                    for w in (0, 1):
                        B1, B1r = PBQ[0 + 2 * w]
                        tf, tfr, tfc = TMP[w]
                        act(tf[:, :], B1, AF.Silu, reads=(B1r,), writes=(tfr,) + tfc + (B1r,))
                    for w in (0, 1):
                        B3, B3r = PBQ[1 + 2 * w]
                        tf, tfr, _ = TMP[w]
                        ab, abr = ACTB[w]
                        wb_, wbr = WACT[w]
                        af, afr = ACCF[w]
                        tt(ab[:, :], tf[:, :], B3, ALU.mult, reads=(tfr, B3r), writes=(abr, B3r))
                        tt(wb_[:, :], ab[:, :], webc[:, e_, :], ALU.mult,
                           reads=(abr, "webc"), writes=(wbr,))
                        if e_ == 0:
                            vcp(af[:, :], wb_[:, :], reads=(wbr,), writes=(afr,))
                        else:
                            tt(af[:, :], af[:, :], wb_[:, :], ALU.add,
                               reads=(wbr,), writes=(afr,))
                    for w in (0, 1):
                        pend_z.append((fcp + w, (fcp + w) % 2, e_, w))
                flush_z()
                for w in (0, 1):
                    fc = fcp + w
                    af, afr = ACCF[w]
                    scp(act_sum[:, fc, :], af[:, :], reads=(afr,),
                        writes=("as%d" % fc,) + (("aTb",) if fc < 2 else ())
                        + (CLOB_B_ARB if hf == 0 and fc < 2 else ()))
            bcast(bc1[:, :], 5, "bc1")
            for e_ in range(NE):
                zt = zts[e_ // 3]
                zo = 32 * (e_ % 3)
                ts(zsb[:, e_, :], zt[zo:zo + 16, :], 2.0, ALU.mult,
                   reads=(zrs[e_ // 3],), writes=("zsb", "u1e", "uP1", zrs[e_ // 3]))
            for dj in range(4):
                dma("gpsimd", b2dj[:, :, :],
                    db2[:, :, dj * 512:(dj + 1) * 512],
                    reads=(), writes=("b2dj", "w13s0"))
                for fcq in range(11):
                    sl = fcq % 2
                    dmas("scalar", [(w2g[sl][:, j, :],
                                     dw2b[(fcq * 4 + j) * 128:(fcq * 4 + j + 1) * 128,
                                          dj * 512:(dj + 1) * 512]) for j in range(4)],
                         reads=("dw2b",), writes=("w2g%d" % sl, "w13s1"))
                    for j in range(4):
                        fc = fcq * 4 + j
                        for ti in range(4):
                            mm(pBig[:, ti * 512:(ti + 1) * 512],
                               act_sum[:, fc, ti * 128:(ti + 1) * 128], w2g[sl][:, j, :],
                               fc == 0, False,
                               reads=("w2g%d" % sl, "as%d" % fc, "pBig%d" % ti),
                               writes=("pBig%d" % ti,), skip_gc=True)
                for ti in range(4):
                    for e_ in range(NE):
                        mm(pBig[:, ti * 512:(ti + 1) * 512], zsb[:, e_, ti * 128:(ti + 1) * 128],
                           b2dj[:, e_, :], False, e_ == NE - 1,
                           reads=("zsb", "b2dj", "pBig%d" % ti), writes=("pBig%d" % ti,),
                           skip_gc=True)
                for ti in range(4):
                    gi = hf * 4 + ti
                    tt(tmpf[:, :], pBig[:, ti * 512:(ti + 1) * 512],
                       bc1[:, dj * 512:(dj + 1) * 512], ALU.mult,
                       reads=("pBig%d" % ti, "bc1"), writes=("tmpf", "pBig%d" % ti))
                    tt(xres[:, gi, dj * 512:(dj + 1) * 512],
                       xres[:, gi, dj * 512:(dj + 1) * 512], tmpf[:, :], ALU.add,
                       reads=("tmpf",), writes=("x%d" % gi,))
                dmas("sync", [(dout[(hf * 4 + t) * 128:(hf * 4 + t + 1) * 128,
                                    dj * 512:(dj + 1) * 512],
                               xres[:, hf * 4 + t, dj * 512:(dj + 1) * 512])
                              for t in range(4)],
                     reads=tuple("x%d" % (hf * 4 + t) for t in range(4)))

    with (
        nc.semaphore("s_sync") as s0, nc.semaphore("s_gpsimd") as s1,
        nc.semaphore("s_vector") as s2, nc.semaphore("s_scalar") as s3,
        nc.semaphore("s_tensor") as s4, nc.semaphore("s_scalardma") as s5,
        nc.semaphore("s_cc") as s6, nc.semaphore("s_vectordma") as s7,
        nc.Block() as block,
    ):
        sems = {"sync": s0, "gpsimd": s1, "vector": s2, "scalar": s3, "tensor": s4,
                "scalardma": s5, "cc": s6, "vectordma": s7}
        P.emit(block, sems)
    print("total instructions:", len(P.ops))
    return nc


_NC_CACHE = {}


def kernel(**inputs):
    cst = _host_constants()
    in_maps = [_host_core_inputs(inputs, c, cst) for c in range(8)]
    if "nc" not in _NC_CACHE:
        _NC_CACHE["nc"] = _build()
    nc = _NC_CACHE["nc"]
    res = run_bass_kernel_spmd(nc, in_maps, core_ids=list(range(8)))
    out = np.empty((4, 2048, 2048), dtype=np.float32)
    for c in range(8):
        b, hf = c // 2, c % 2
        out[b, hf * TOK:(hf + 1) * TOK, :] = np.asarray(res.results[c]["out"])
    return out


# version 19
# speedup vs baseline: 1.1119x; 1.0259x over previous
import numpy as np
import ml_dtypes
import concourse.bass as bass
import concourse.mybir as mybir
from concourse.bass_utils import run_bass_kernel_spmd

F32 = mybir.dt.float32
BF16 = mybir.dt.bfloat16
AF = mybir.ActivationFunctionType
ALU = mybir.AluOpType
AX = mybir.AxisListType
bfdt = ml_dtypes.bfloat16

U42 = np.array([
 [0.6102175712585449, 0.6966079473495483, 0.06568920612335205, 0.6029453277587891, 0.4143662452697754, 0.24911487102508545, 0.5782934427261353, 0.24924850463867188, 0.959638237953186, 0.8191142082214355, 0.3645491600036621, 0.059471845626831055, 0.9161832332611084, 0.6182148456573486, 0.8282822370529175, 0.6976723670959473],
 [0.756269097328186, 0.3594318628311157, 0.4266716241836548, 0.477594256401062, 0.08340179920196533, 0.40768611431121826, 0.6278458833694458, 0.3251793384552002, 0.35155820846557617, 0.15171575546264648, 0.5589374303817749, 0.8866020441055298, 0.3201930522918701, 0.12432277202606201, 0.6967982053756714, 0.8069779872894287],
 [0.6127028465270996, 0.7509020566940308, 0.4151843786239624, 0.4897202253341675, 0.11665701866149902, 0.40220367908477783, 0.08376991748809814, 0.9528622627258301, 0.9559504985809326, 0.06945085525512695, 0.20124602317810059, 0.06397080421447754, 0.8091422319412231, 0.31807422637939453, 0.639366865158081, 0.8050472736358643],
 [0.5085939168930054, 0.5088651180267334, 0.9745233058929443, 0.3331362009048462, 0.05924034118652344, 0.7208822965621948, 0.3291349411010742, 0.5319250822067261, 0.4839836359024048, 0.643956184387207, 0.19713079929351807, 0.1282445192337036, 0.3164334297180176, 0.4948718547821045, 0.9159883260726929, 0.781721830368042],
], dtype=np.float32)

T = 2048; D = 2048; H = 16; DH = 128
KEEP = np.concatenate([np.arange(204), np.arange(1229, 2048)])
NKC = 8
TOK = 1024; HTOK = 512
Fd = 5632; NE = 8; NR = 16; NFC = 44
EVK = 204
MBEG, MEND = 205, 1023
PH = "ALL"
DEBUG_TAPS = ()

_ENGS = ("sync", "syncB", "gpsimd", "gpsimdB", "vector", "scalar", "tensor",
         "scalardma", "scalardmaB", "vectordma", "vectordmaB", "cc")


class _Op:
    __slots__ = ("eng", "fn", "waits", "amt", "gi", "dma", "mark", "cum", "nsub", "chain")

    def __init__(self, eng, fn, waits, dma, gi, nsub=1, chain=None):
        self.eng = eng; self.fn = fn; self.waits = waits
        self.dma = dma; self.nsub = nsub
        self.amt = 16 * nsub if dma else 1
        self.gi = gi; self.mark = False; self.cum = 0
        self.chain = chain


class _Prog:
    def __init__(self):
        self.ops = []
        self.lw = {}
        self.lr = {}
        self.last_dma = {}

    def add(self, eng, fn, reads=(), writes=(), dma=False, nsub=1, chain=None):
        waits = []
        if dma:
            prev = self.last_dma.get(eng)
            if prev is not None:
                waits.append(prev)
        for r in reads:
            w = self.lw.get(r)
            if w is not None:
                waits.append(w)
        for r in writes:
            w = self.lw.get(r)
            if w is not None:
                waits.append(w)
            waits.extend(self.lr.get(r, ()))
        op = _Op(eng, fn, waits, dma, len(self.ops), nsub, chain)
        self.ops.append(op)
        if dma:
            self.last_dma[eng] = op
        for r in reads:
            self.lr.setdefault(r, []).append(op)
        for r in writes:
            self.lw[r] = op
            self.lr[r] = []
        return op

    @staticmethod
    def _need(w, op):
        return True

    def emit(self, block, sems):
        for op in self.ops:
            if op.dma:
                op.mark = True
            best = {}
            for w in op.waits:
                if self._need(w, op) and (w.eng not in best or w.gi > best[w.eng].gi):
                    best[w.eng] = w
            op.waits = list(best.values())
            for w in op.waits:
                w.mark = True
        cnt = {e: 0 for e in _ENGS}
        for op in self.ops:
            if op.mark:
                cnt[op.eng] += op.amt
            op.cum = cnt[op.eng]
        ops = self.ops

        def mkbody(name):
            group = {"scalar": ("scalar", "scalardma", "scalardmaB"),
                     "gpsimd": ("gpsimd", "gpsimdB", "cc"),
                     "vector": ("vector", "vectordma", "vectordmaB"),
                     "sync": ("sync", "syncB")}.get(name, (name,))
            def body(e):
                hi = {en: 0 for en in _ENGS}
                for op in ops:
                    if op.eng not in group:
                        continue
                    for w in op.waits:
                        if not self._need(w, op):
                            continue
                        if w.cum > hi[w.eng]:
                            e.wait_ge(sems[w.eng], w.cum)
                            hi[w.eng] = w.cum
                    inst = op.fn(e)
                    if op.mark:
                        if isinstance(inst, list):
                            for ii in inst:
                                ii.then_inc(sems[op.eng], 16)
                        else:
                            inst.then_inc(sems[op.eng], op.amt)
                for gn in group:
                    if cnt[gn] > hi[gn]:
                        e.wait_ge(sems[gn], cnt[gn])
            return body
        block.sync(mkbody("sync"))
        block.gpsimd(mkbody("gpsimd"))
        block.vector(mkbody("vector"))
        block.scalar(mkbody("scalar"))
        block.tensor(mkbody("tensor"))


def _host_constants():
    inv = 1.0 / (10000.0 ** (np.arange(0, DH, 2, dtype=np.float32) / DH))
    fr = np.arange(T, dtype=np.float32)[:, None] * inv[None, :]
    emb = np.concatenate([fr, fr], 1)
    cosT = np.cos(emb).astype(np.float32)
    sinT = np.sin(emb).astype(np.float32)
    s = np.float32(1.0 / np.sqrt(np.float32(DH)))
    consts = np.tile(np.array([[1e-5, 1e-6, 1.0 / 818.0, 1.0 / 819.0, 0.25, 1.0, 0.0, 0.0]],
                              dtype=np.float32), (128, 1))
    prot = np.zeros((128, 128), dtype=np.float32)
    for d2 in range(64):
        prot[d2 + 64, d2] = -1.0
    for d2 in range(64, 128):
        prot[d2 - 64, d2] = 1.0
    mrg = np.zeros((128, NKC, 3), dtype=np.float32)
    for kc in range(NKC):
        for p in range(128):
            ki = kc * 128 + p
            if ki >= 1023:
                continue
            mrg[p, kc, 2] = 1.0
            if MBEG <= ki < MEND:
                mrg[p, kc, 0] = 1.0
            if ki == EVK:
                mrg[p, kc, 1] = 1.0
    return dict(cosT=cosT, sinT=sinT, qscale=s, consts=consts,
                identb=np.eye(128, dtype=bfdt), identf=np.eye(128, dtype=np.float32),
                prot=prot.astype(bfdt), onesrow=np.ones((1, 128), dtype=bfdt),
                mrg=mrg.astype(bfdt))


def _host_core_inputs(inputs, c, cst):
    b, hf = c // 2, c % 2
    X = np.asarray(inputs["L0_attn_hidden_states"])
    x_own = np.ascontiguousarray(X[b, hf * TOK:(hf + 1) * TOK])
    xk = np.zeros((1024, D), dtype=np.float32)
    xk[:1023] = X[b, KEEP]
    ownpos = np.arange(hf * TOK, hf * TOK + TOK)
    kpos = np.zeros(1024, dtype=np.int64); kpos[:1023] = KEEP
    cosq = np.ascontiguousarray((cst["cosT"][ownpos] * cst["qscale"]).T.astype(bfdt))
    sinq = np.ascontiguousarray((cst["sinT"][ownpos] * cst["qscale"]).T.astype(bfdt))
    cosk = np.ascontiguousarray(cst["cosT"][kpos].T.astype(bfdt))
    sink = np.ascontiguousarray(cst["sinT"][kpos].T.astype(bfdt))
    cosk[:, 1023] = 0; sink[:, 1023] = 0
    m_t = np.zeros((128, NKC), dtype=np.float32)
    if hf == 1:
        for kc in range(NKC):
            for p in range(128):
                ki = kc * 128 + p
                if MBEG <= ki < MEND:
                    m_t[p, kc] = 1.0
    w1T = np.asarray(inputs["L0_ffn_w1"]).T
    w3T = np.asarray(inputs["L0_ffn_w3"]).T
    w13 = np.empty((D, NFC * 256), dtype=np.float32)
    for fc in range(NFC):
        w13[:, fc * 256:fc * 256 + 128] = w1T[:, fc * 128:(fc + 1) * 128]
        w13[:, fc * 256 + 128:fc * 256 + 256] = w3T[:, fc * 128:(fc + 1) * 128]
    b1 = np.asarray(inputs["L0_ffn_b1"]); b3 = np.asarray(inputs["L0_ffn_b3"])
    b13 = np.ascontiguousarray(
        np.stack([b1.transpose(0, 2, 1), b3.transpose(0, 2, 1)], axis=2).transpose(1, 0, 2, 3))
    return dict(
        x_own=x_own, x_kept=np.ascontiguousarray(xk),
        wqT=np.ascontiguousarray(np.asarray(inputs["L0_attn_Wq"]).T),
        wkT=np.ascontiguousarray(np.asarray(inputs["L0_attn_Wk"]).T),
        wvT=np.ascontiguousarray(np.asarray(inputs["L0_attn_Wv"]).T),
        woT=np.ascontiguousarray(np.asarray(inputs["L0_attn_Wo"]).T),
        wmod=np.ascontiguousarray(np.asarray(inputs["L0_wmod"])),
        gateT=np.ascontiguousarray(np.asarray(inputs["L0_ffn_gate_w"]).T),
        w13=w13,
        w2T=np.ascontiguousarray(np.asarray(inputs["L0_ffn_w2"]).T),
        b13=b13,
        a2il=np.ascontiguousarray(np.asarray(inputs["L0_ffn_a2"]).transpose(2, 0, 1)),
        b2il=np.ascontiguousarray(np.asarray(inputs["L0_ffn_b2"]).transpose(2, 0, 1)),
        a1T=np.ascontiguousarray(np.asarray(inputs["L0_ffn_a1"]).reshape(NE * NR, D).T),
        a3T=np.ascontiguousarray(np.asarray(inputs["L0_ffn_a3"]).reshape(NE * NR, D).T),
        c_col=np.ascontiguousarray(np.asarray(inputs["L0_c"])[b].reshape(16, 128).T),
        u_own=np.ascontiguousarray(U42[b:b + 1]),
        cosq=cosq, sinq=sinq, cosk=cosk, sink=sink, m_t=m_t,
        consts=cst["consts"], identb=cst["identb"], identf=cst["identf"],
        prot=cst["prot"], onesrow=cst["onesrow"], mrg=cst["mrg"],
    )


def _build():
    nc = bass.Bass("TRN2", target_bir_lowering=False)
    P = _Prog()

    dx = nc.dram_tensor("x_own", [TOK, D], F32, kind="ExternalInput")
    dxk = nc.dram_tensor("x_kept", [1024, D], F32, kind="ExternalInput")
    dwq = nc.dram_tensor("wqT", [D, D], F32, kind="ExternalInput")
    dwk = nc.dram_tensor("wkT", [D, D], F32, kind="ExternalInput")
    dwv = nc.dram_tensor("wvT", [D, D], F32, kind="ExternalInput")
    dwo = nc.dram_tensor("woT", [D, D], F32, kind="ExternalInput")
    dwmod = nc.dram_tensor("wmod", [D, 12288], F32, kind="ExternalInput")
    dgate = nc.dram_tensor("gateT", [D, 8], F32, kind="ExternalInput")
    dw13 = nc.dram_tensor("w13", [D, NFC * 256], F32, kind="ExternalInput")
    dw2 = nc.dram_tensor("w2T", [Fd, D], F32, kind="ExternalInput")
    db13 = nc.dram_tensor("b13", [NR, NE, 2, Fd], F32, kind="ExternalInput")
    da2 = nc.dram_tensor("a2il", [Fd, NE, NR], F32, kind="ExternalInput")
    db2 = nc.dram_tensor("b2il", [NR, NE, D], F32, kind="ExternalInput")
    da1 = nc.dram_tensor("a1T", [D, 128], F32, kind="ExternalInput")
    da3 = nc.dram_tensor("a3T", [D, 128], F32, kind="ExternalInput")
    dc = nc.dram_tensor("c_col", [128, 16], F32, kind="ExternalInput")
    du = nc.dram_tensor("u_own", [1, 16], F32, kind="ExternalInput")
    dcosq = nc.dram_tensor("cosq", [128, 1024], BF16, kind="ExternalInput")
    dsinq = nc.dram_tensor("sinq", [128, 1024], BF16, kind="ExternalInput")
    dcosk = nc.dram_tensor("cosk", [128, 1024], BF16, kind="ExternalInput")
    dsink = nc.dram_tensor("sink", [128, 1024], BF16, kind="ExternalInput")
    dconsts = nc.dram_tensor("consts", [128, 8], F32, kind="ExternalInput")
    didb = nc.dram_tensor("identb", [128, 128], BF16, kind="ExternalInput")
    didf = nc.dram_tensor("identf", [128, 128], F32, kind="ExternalInput")
    dprot = nc.dram_tensor("prot", [128, 128], BF16, kind="ExternalInput")
    dones = nc.dram_tensor("onesrow", [1, 128], BF16, kind="ExternalInput")
    dmrg = nc.dram_tensor("mrg", [128, NKC, 3], BF16, kind="ExternalInput")
    dmt = nc.dram_tensor("m_t", [128, NKC], F32, kind="ExternalInput")
    dout = nc.dram_tensor("out", [TOK, D], F32, kind="ExternalOutput")
    dscrP = nc.dram_tensor("dscrP", [1, 12288], F32)
    dwe = nc.dram_tensor("dwe", [NE, HTOK], BF16)
    dw13b = nc.dram_tensor("dw13b", [D, NFC * 256], BF16)
    dw2b = nc.dram_tensor("dw2b", [Fd, D], BF16)
    db13c = nc.dram_tensor("db13c", [NR, NE, 2, Fd], BF16)
    da2c = nc.dram_tensor("da2c", [Fd, NE, NR], BF16)
    db2c = nc.dram_tensor("db2c", [NR, NE, D], BF16)

    taps = {}

    def tap(name, shape, dtype=F32):
        if name in DEBUG_TAPS:
            if name not in taps:
                taps[name] = nc.dram_tensor("tap_" + name, shape, dtype, kind="ExternalOutput")
            return taps[name]
        return None

    arA = nc.alloc_sbuf_tensor("arA", [128, 16384], F32)
    arB = nc.alloc_sbuf_tensor("arB", [128, 15360], F32)
    arC = nc.alloc_sbuf_tensor("arC", [128, 8192], F32)
    arA_bf = arA[:, :].bitcast(BF16)
    arB_bf = arB[:, :].bitcast(BF16)
    arC_bf = arC[:, :].bitcast(BF16)

    def carve(base, off, n, lead=None):
        v = base[:, off:off + n]
        if lead is not None:
            v = v.rearrange("p (a b) -> p a b", a=lead)
        return v

    hq = carve(arA_bf, 0, 16 * 1024, 16)
    hk = carve(arA_bf, 16 * 1024, 16 * 1024, 16)
    xres = arA[:, :].rearrange("p (a b) -> p a b", a=8)

    o = 0
    qTg = carve(arB_bf, o, 4096, 4); o += 4096
    kTg = carve(arB_bf, o, 4096, 4); o += 4096
    vbuf = carve(arB_bf, o, 8 * 4 * 129).rearrange("p (k h d) -> p k h d", k=8, h=4)
    o += 8 * 4 * 129
    pT7 = carve(arB_bf, o, 1024); cosq_s = pT7; o += 1024
    pTc = carve(arB_bf, o, 1024); sinq_s = pTc; o += 1024
    pT6 = carve(arB_bf, o, 1024); o += 1024
    probsq = carve(arB_bf, o, 1024); o += 1024
    wbufA = carve(arB_bf, o, 4096, 16); o += 4096
    wbufB = carve(arB_bf, o, 4096, 16); o += 4096
    assert o <= 30720, o
    act_sum = carve(arB_bf, 0, NFC * 512, NFC)
    h2T = carve(arB_bf, NFC * 512, 16 * 512, 16)
    assert NFC * 512 + 16 * 512 <= 30720

    xkb = [arC[:, 0:2048], arC[:, 2048:4096]]
    ho = carve(arC_bf, 0, 16384, 16)
    o = 0
    webc = carve(arC_bf, o, 8 * 512, 8); o += 8 * 512
    w13s = [carve(arC_bf, o, 4096, 16), carve(arC_bf, o + 4096, 4096, 16)]
    o += 8192
    w2s = [carve(arC_bf, o, 512), carve(arC_bf, o + 512, 512)]
    o += 1024
    accf = arC[:, (o // 2):(o // 2) + 512]
    o += 1024
    aTbv = carve(arC_bf, o, 2048, 16)
    accf2 = arC[:, (o // 2):(o // 2) + 512]
    o += 2048
    assert o <= 16384
    w2g = [arC_bf[:, 8192:10240].rearrange("p (a b) -> p a b", a=4),
           arC_bf[:, 10240:12288].rearrange("p (a b) -> p a b", a=4)]

    CLOB_B_ARB = ("qTg", "kTg", "vbuf", "r1a", "r1b", "r1c", "r1d", "wi0", "wi1")
    CLOB_B_ARC = ("ho", "xk0", "xk1")

    htok = nc.alloc_sbuf_tensor("htok", [128, 2048], BF16)
    bc1 = nc.alloc_sbuf_tensor("bc1", [128, 2048], BF16)
    bc2 = nc.alloc_sbuf_tensor("bc2", [128, 2048], BF16)
    stg2 = [arB[0:1, 12304:13328], arB[0:1, 13328:14352]]
    scr2 = [arB[0:1, 14352:14608], arB[0:1, 14608:14864]]
    wstf = arB[:, 0:4096].rearrange("p (a b) -> p a b", a=16)
    c_sb = nc.alloc_sbuf_tensor("c_sb", [128, 16], F32)
    scsb = nc.alloc_sbuf_tensor("scsb", [128, 16], BF16)
    csts = nc.alloc_sbuf_tensor("csts", [128, 8], F32)
    identb = nc.alloc_sbuf_tensor("identb_s", [128, 128], BF16)
    identf = nc.alloc_sbuf_tensor("identf_s", [128, 128], F32)
    prot = nc.alloc_sbuf_tensor("prot_s", [128, 128], BF16)
    onesr = nc.alloc_sbuf_tensor("onesr", [1, 128], BF16)
    mrg = nc.alloc_sbuf_tensor("mrg_s", [128, NKC, 3], BF16)
    m_t = nc.alloc_sbuf_tensor("mt_s", [128, NKC], F32)
    u_sb = nc.alloc_sbuf_tensor("u_sb", [1, 16], F32)
    tmpf = nc.alloc_sbuf_tensor("tmpf", [128, 512], F32)
    qraw = tmpf[:, 0:256].bitcast(BF16)
    rotb = tmpf[:, 256:512].bitcast(BF16)
    qcos = nc.alloc_sbuf_tensor("qcos", [128, 512], BF16)
    qcos2 = nc.alloc_sbuf_tensor("qcos2", [128, 512], BF16)
    ca1col = nc.alloc_sbuf_tensor("ca1col", [128, 16], F32)
    sacol = nc.alloc_sbuf_tensor("sacol", [128, 16], F32)
    stats = nc.alloc_sbuf_tensor("stats", [128, 16], F32)
    nmx = nc.alloc_sbuf_tensor("nmx", [128, 2], F32)
    zvv = nc.alloc_sbuf_tensor("zvv", [128, 2], F32)
    hot = nc.alloc_sbuf_tensor("hot", [128, 128], F32)
    mz = nc.alloc_sbuf_tensor("mz", [128, 4], F32)
    mrow = nc.alloc_sbuf_tensor("mrow", [1, 8], F32)
    bmt = nc.alloc_sbuf_tensor("bmt", [1, 8], F32)
    v204 = nc.alloc_sbuf_tensor("v204", [1, 128], BF16)
    vescl = nc.alloc_sbuf_tensor("vescl", [1, 128], BF16)
    gateTb = nc.alloc_sbuf_tensor("gateTb", [128, 16, 8], BF16)
    upack = nc.alloc_sbuf_tensor("upack", [128, NE * HTOK], BF16)
    u1e = upack[0:16, :].rearrange("p (a b) -> p a b", a=NE)
    u3e = upack[32:48, :].rearrange("p (a b) -> p a b", a=NE)
    u1n = upack[64:80, :].rearrange("p (a b) -> p a b", a=NE)
    u3n = nc.alloc_sbuf_tensor("u3n", [NR, NE, HTOK], BF16)
    uP1 = upack[0:32, :].rearrange("p (a b) -> p a b", a=NE)
    uP3 = upack[32:64, :].rearrange("p (a b) -> p a b", a=NE)
    uN1 = upack[16:32, :].rearrange("p (a b) -> p a b", a=NE)
    uN3 = upack[48:64, :].rearrange("p (a b) -> p a b", a=NE)
    zsb = u1e
    ubt = arC_bf[:, 4096:5120].rearrange("p (a b) -> p a b", a=2)
    werows = nc.alloc_sbuf_tensor("werows", [NE, HTOK], BF16)
    wfull = nc.alloc_sbuf_tensor("wfull", [128, 4, NE], F32)
    lg = nc.alloc_sbuf_tensor("lg", [128, 4, NE], F32)
    t2s = nc.alloc_sbuf_tensor("t2s", [128, 8], F32)
    tmp8 = nc.alloc_sbuf_tensor("tmp8", [128, NE], F32)
    msk8 = nc.alloc_sbuf_tensor("msk8", [128, NE], F32)
    actb = nc.alloc_sbuf_tensor("actb", [128, 512], BF16)
    wactb = nc.alloc_sbuf_tensor("wactb", [128, 512], BF16)
    tmpf2 = nc.alloc_sbuf_tensor("tmpf2", [128, 512], F32)
    qraw2 = tmpf2[:, 0:256].bitcast(BF16)
    rotb2 = tmpf2[:, 256:512].bitcast(BF16)
    actb2 = nc.alloc_sbuf_tensor("actb2", [128, 512], BF16)
    wactb2 = nc.alloc_sbuf_tensor("wactb2", [128, 512], BF16)
    bpair = nc.alloc_sbuf_tensor("bpair", [64, 2, NE, 128], BF16)
    a2s = nc.alloc_sbuf_tensor("a2s", [128, 2, NE, NR], BF16)
    winit = [wbufA, wbufB]
    coskS = pT6
    sinkS = probsq
    b2dj = arC_bf[0:16, 4096:8192].rearrange("p (a b) -> p a b", a=NE)

    pBig = nc.alloc_psum_tensor("pBig", [128, 2048], F32)
    pH = nc.alloc_psum_tensor("pH", [128, 1024], BF16)
    zA = nc.alloc_psum_tensor("zA", [128, 512], F32)
    zB = nc.alloc_psum_tensor("zB", [128, 512], F32)
    zC = nc.alloc_psum_tensor("zC", [128, 512], F32)
    print("sbuf bytes remaining:", nc.sbuf_bytes_remaining,
          "psum banks remaining:", nc.psum_banks_remaining)
    PBQ = [(pBig[:, i * 512:(i + 1) * 512], "pBig%d" % i) for i in range(4)]
    PB_ALL = ("pBig0", "pBig1", "pBig2", "pBig3")

    _lane_flip = {}

    def _lane(eng):
        eng = {"scalar": "scalardma", "vector": "vectordma"}.get(eng, eng)
        f = _lane_flip.get(eng, 0)
        _lane_flip[eng] = f ^ 1
        return eng + "B" if f else eng

    def dma(eng, out, in_, reads=(), writes=()):
        return P.add(_lane(eng), lambda e, oo=out, ii=in_: e.dma_start(out=oo, in_=ii),
                     reads=reads, writes=writes, dma=True)

    def dmas(eng, pairs, reads=(), writes=()):
        def f(e, ps=tuple(pairs)):
            return [e.dma_start(out=oo, in_=ii) for (oo, ii) in ps]
        return P.add(_lane(eng), f, reads=reads, writes=writes, dma=True, nsub=len(pairs))

    def mm(out, lhsT, rhs, start, stop, reads=(), writes=(), skip_gc=False):
        ch = writes[0] if writes else None
        return P.add("tensor",
                     lambda e, oo=out, l=lhsT, r=rhs, st=start, sp=stop, sg=skip_gc:
                     e.matmul(oo, l, r, start=st, stop=sp, skip_group_check=sg),
                     reads=reads, writes=writes, chain=ch)

    def tr(out, in_, ident, reads=(), writes=()):
        return P.add("tensor",
                     lambda e, oo=out, ii=in_, dd=ident: e.transpose(oo, ii, dd),
                     reads=reads, writes=writes)

    def act(out, in_, func, bias=None, scale=None, accum=None, reads=(), writes=()):
        kw = {}
        if bias is not None:
            kw["bias"] = bias
        if scale is not None:
            kw["scale"] = scale
        if accum is not None:
            kw["accum_out"] = accum
        return P.add("scalar",
                     lambda e, oo=out, ii=in_, f=func, kw=kw: e.activation(oo, ii, f, **kw),
                     reads=reads, writes=writes)

    def scp(out, in_, reads=(), writes=()):
        return P.add("scalar", lambda e, oo=out, ii=in_: e.copy(oo, ii),
                     reads=reads, writes=writes)

    def vcp(out, in_, reads=(), writes=()):
        return P.add("vector", lambda e, oo=out, ii=in_: e.tensor_copy(oo, ii),
                     reads=reads, writes=writes)

    def tt(out, a, b, op, reads=(), writes=()):
        return P.add("vector", lambda e, oo=out, x=a, y=b, p=op: e.tensor_tensor(oo, x, y, p),
                     reads=reads, writes=writes)

    def ts(out, in_, s1, op0, s2=None, op1=ALU.bypass, reads=(), writes=()):
        return P.add("vector",
                     lambda e, oo=out, ii=in_, a=s1, b=s2, p0=op0, p1=op1:
                     e.tensor_scalar(oo, ii, a, b, op0=p0, op1=p1),
                     reads=reads, writes=writes)

    def stt(out, in0, scalar, in1, op0, op1, reads=(), writes=()):
        return P.add("vector",
                     lambda e, oo=out, ii=in0, s=scalar, jj=in1, p0=op0, p1=op1:
                     e.scalar_tensor_tensor(oo, ii, s, jj, op0=p0, op1=p1),
                     reads=reads, writes=writes)

    def red(out, in_, op, negate=False, reads=(), writes=()):
        return P.add("vector",
                     lambda e, oo=out, ii=in_, p=op, n=negate:
                     e.tensor_reduce(oo, ii, axis=AX.X, op=p, negate=n),
                     reads=reads, writes=writes)

    def recip(out, in_, reads=(), writes=()):
        return P.add("vector", lambda e, oo=out, ii=in_: e.reciprocal(oo, ii),
                     reads=reads, writes=writes)

    def memset(out, val, reads=(), writes=()):
        return P.add("vector", lambda e, oo=out, v=val: e.memset(oo, v),
                     reads=reads, writes=writes)

    _smalls = [
        (c_sb[:, :], dc[:, :], "c_sb"), (csts[:, :], dconsts[:, :], "csts"),
        (identb[:, :], didb[:, :], "identb"), (identf[:, :], didf[:, :], "identf"),
        (prot[:, :], dprot[:, :], "prot"), (onesr[:, :], dones[:, :], "onesr"),
        (mrg[:, :, :], dmrg[:, :, :], "mrg"), (m_t[:, :], dmt[:, :], "m_t"),
        (u_sb[:, :], du[:, :], "u_sb"),
    ]
    dmas("sync", [(d0, s0) for (d0, s0, _) in _smalls],
         writes=tuple(nm for (_, _, nm) in _smalls))
    act(scsb[:, :], c_sb[:, :], AF.Silu, reads=("c_sb",), writes=("scsb",))
    wstg = [arB[:, 0:4096].rearrange("p (a b) -> p a b", a=16),
            arB[:, 4096:8192].rearrange("p (a b) -> p a b", a=16)]

    def wmod_block(njp):
        for s in (0, 1):
            nj = njp + s
            r5 = nj % 5
            if r5 in (0, 2):
                dmas("gpsimd", [(winit[s][:, q * 4:(q + 1) * 4, :],
                                 dwmod[q * 512:(q + 1) * 512, nj * 256:(nj + 1) * 256]
                                 .rearrange("(a p) b -> p a b", p=128)) for q in range(4)],
                     reads=(), writes=("wi%d" % s,))
            else:
                qn = "scalardma" if r5 in (1, 3) else "sync"
                dmas(qn, [(wstg[s][:, q * 4:(q + 1) * 4, :],
                           dwmod[q * 512:(q + 1) * 512, nj * 256:(nj + 1) * 256]
                           .rearrange("(a p) b -> p a b", p=128)) for q in range(4)],
                     reads=(), writes=("wst%d" % s,))
                P.add("vector",
                      lambda e, s=s: e.tensor_copy(winit[s][:, :, :], wstg[s][:, :, :]),
                      reads=("wst%d" % s,), writes=("wi%d" % s,))
        for kc in range(16):
            for s in (0, 1):
                zsl, zslr = (zA, "zA") if s == 0 else (zB, "zB")
                mm(zsl[0:1, 0:256], scsb[:, kc:kc + 1], winit[s][:, kc, 0:256],
                   kc == 0, kc == 15, reads=("scsb", "wi%d" % s), writes=(zslr,))
        for s in (0, 1):
            nj = njp + s
            zsl, zslr = (zA, "zA") if s == 0 else (zB, "zB")
            g = nj // 4
            pos = nj % 4
            buf = stg2[g % 2]
            ch = buf[0:1, pos * 256:(pos + 1) * 256]
            cn = "sg%d" % (nj % 8)
            P.add("scalar",
                  lambda e, zsl=zsl, ch=ch: e.copy(ch, zsl[0:1, 0:256]),
                  reads=(zslr,), writes=(cn, zslr))
            sl_ = nj // 8
            if sl_ in (1, 4):
                ts(ch, ch, 1.0, ALU.add, reads=(), writes=(cn,))
            elif sl_ in (2, 5):
                scr = scr2[nj % 2]
                srn = "scr%d" % (nj % 2)
                P.add("scalar", lambda e, ch=ch, scr=scr: e.activation(scr, ch, AF.Sign),
                      reads=(cn,), writes=(srn,))
                P.add("scalar", lambda e, ch=ch: e.activation(ch, ch, AF.Abs),
                      reads=(), writes=(cn,))
                ts(ch, ch, 0.25, ALU.max, reads=(), writes=(cn,))
                tt(ch, ch, scr, ALU.mult, reads=(srn,), writes=(cn,))
            if pos == 3:
                dma("sync", dscrP[0:1, g * 1024:(g + 1) * 1024], buf,
                    reads=tuple("sg%d" % ((g % 2) * 4 + i) for i in range(4)),
                    writes=("dP%d" % g,))

    def bcast(dst, slot, name):
        return dma("gpsimd", dst,
                   dscrP[0:1, slot * 2048:(slot + 1) * 2048].partition_broadcast(128),
                   reads=("dP%d" % (2 * slot), "dP%d" % (2 * slot + 1)), writes=(name,))

    def ln_norm_tile(i, src_dram, row0, dstview, dstname, dst_ti):
        xb = xkb[i % 2]
        xr = "xk%d" % (i % 2)
        dma("sync", xb[:, :], src_dram[row0:row0 + 128, :], reads=(), writes=(xr,))
        sc0 = 8 * (i % 2)
        red(stats[:, sc0:sc0 + 1], xb[:, :], ALU.add, reads=(xr,), writes=("stats%d" % (i % 2),))
        act(pBig[:, :], xb[:, :], AF.Square, accum=stats[:, sc0 + 1:sc0 + 2],
            reads=(xr,) + PB_ALL, writes=PB_ALL + ("stats%d" % (i % 2),))
        ts(stats[:, sc0 + 2:sc0 + 3], stats[:, sc0:sc0 + 1], 1.0 / 2048, ALU.mult,
           reads=("stats%d" % (i % 2),), writes=("stats%d" % (i % 2),))
        tt(stats[:, sc0 + 3:sc0 + 4], stats[:, sc0 + 2:sc0 + 3], stats[:, sc0 + 2:sc0 + 3],
           ALU.mult, reads=("stats%d" % (i % 2),), writes=("stats%d" % (i % 2),))
        stt(stats[:, sc0 + 4:sc0 + 5], stats[:, sc0 + 1:sc0 + 2], 1.0 / 2048,
            stats[:, sc0 + 3:sc0 + 4], ALU.mult, ALU.subtract,
            reads=("stats%d" % (i % 2),), writes=("stats%d" % (i % 2),))
        act(stats[:, sc0 + 5:sc0 + 6], stats[:, sc0 + 4:sc0 + 5], AF.Sqrt, bias=csts[:, 0:1],
            reads=("stats%d" % (i % 2), "csts"), writes=("stats%d" % (i % 2),))
        recip(stats[:, sc0 + 6:sc0 + 7], stats[:, sc0 + 5:sc0 + 6],
              reads=("stats%d" % (i % 2),), writes=("stats%d" % (i % 2),))
        hb, hbr = (htok, "htok") if i % 2 == 0 else (bc2, "bc2n")
        ts(hb[:, :], xb[:, :], stats[:, sc0 + 2:sc0 + 3], ALU.subtract,
           stats[:, sc0 + 6:sc0 + 7], ALU.mult,
           reads=(xr, "stats%d" % (i % 2)), writes=(hbr,))
        for g2 in range(2):
            for j in range(8):
                fc = g2 * 8 + j
                tr(pH[:, j * 128:(j + 1) * 128], hb[:, fc * 128:(fc + 1) * 128], identb[:, :],
                   reads=(hbr, "identb", "pH%d" % j), writes=("pH%d" % j,))
            P.add("scalar",
                  lambda e, dv=dstview, g2=g2, ti=dst_ti:
                  e.copy(dv[:, g2 * 8:(g2 + 1) * 8, ti * 128:(ti + 1) * 128],
                         pH[:, :].rearrange("p (a b) -> p a b", a=8)),
                  reads=tuple("pH%d" % k for k in range(8)),
                  writes=(dstname,) + tuple("pH%d" % k for k in range(8)))

    def ln_mod_tile(i, src_dram, row0, dstview, dstname, dst_ti):
        xb = xkb[i % 2]
        xr = "xk%d" % (i % 2)
        dma("sync", xb[:, :], src_dram[row0:row0 + 128, :], reads=(), writes=(xr,))
        red(stats[:, 0:1], xb[:, :], ALU.add, reads=(xr,), writes=("stats",))
        act(pBig[:, :], xb[:, :], AF.Square, accum=stats[:, 1:2],
            reads=(xr,) + PB_ALL, writes=PB_ALL + ("stats",))
        ts(stats[:, 2:3], stats[:, 0:1], 1.0 / 2048, ALU.mult, reads=("stats",), writes=("stats",))
        tt(stats[:, 3:4], stats[:, 2:3], stats[:, 2:3], ALU.mult, reads=("stats",), writes=("stats",))
        stt(stats[:, 4:5], stats[:, 1:2], 1.0 / 2048, stats[:, 3:4], ALU.mult, ALU.subtract,
            reads=("stats",), writes=("stats",))
        act(stats[:, 5:6], stats[:, 4:5], AF.Sqrt, bias=csts[:, 0:1],
            reads=("stats", "csts"), writes=("stats",))
        recip(stats[:, 6:7], stats[:, 5:6], reads=("stats",), writes=("stats",))
        ts(pBig[:, :], xb[:, :], stats[:, 2:3], ALU.subtract, stats[:, 6:7], ALU.mult,
           reads=(xr, "stats") + PB_ALL, writes=PB_ALL)
        tt(htok[:, :], pBig[:, :], bc1[:, :], ALU.mult,
           reads=PB_ALL + ("bc1",), writes=("htok",) + PB_ALL)
        tt(htok[:, :], htok[:, :], bc2[:, :], ALU.add, reads=("bc2",), writes=("htok",))
        for g2 in range(2):
            for j in range(8):
                fc = g2 * 8 + j
                tr(pH[:, j * 128:(j + 1) * 128], htok[:, fc * 128:(fc + 1) * 128], identb[:, :],
                   reads=("htok", "identb", "pH%d" % j), writes=("pH%d" % j,))
            P.add("scalar",
                  lambda e, dv=dstview, g2=g2, ti=dst_ti:
                  e.copy(dv[:, g2 * 8:(g2 + 1) * 8, ti * 128:(ti + 1) * 128],
                         pH[:, :].rearrange("p (a b) -> p a b", a=8)),
                  reads=tuple("pH%d" % k for k in range(8)),
                  writes=(dstname,) + tuple("pH%d" % k for k in range(8)))

    ln_calls = []
    for i in range(8):
        ln_calls.append((2 * i, dx, i * 128, hq, "hq", i))
        ln_calls.append((2 * i + 1, dxk, i * 128, hk, "hk", i))
    done = 0
    for bi, njp in enumerate(range(0, 48, 2)):
        wmod_block(njp)
        want = (bi + 1) * len(ln_calls) // 24
        while done < want:
            ln_norm_tile(*ln_calls[done])
            done += 1
    bcast(bc1[:, :], 1, "bc1")

    bcast(bc2[:, :], 0, "bc2b")
    P.ops[-1].waits.extend(P.lr.get("bc2n", ()))
    for vsrc, vrn, vdst, vdr in ((bc1, "bc1", ca1col, "ca1col"), (bc2, "bc2b", sacol, "sacol")):
        for f in range(16):
            tr(pH[:, f * 64:f * 64 + 1], vsrc[0:1, f * 128:(f + 1) * 128], identb[0:1, 0:1],
               reads=(vrn, "identb", "pH%d" % (f // 2)), writes=("pH%d" % (f // 2),))
        pH3 = pH[:, :].rearrange("p (a b) -> p a b", a=16)
        P.add("scalar", lambda e, vdst=vdst, pH3=pH3: e.copy(vdst[:, :], pH3[:, :, 0:1]),
              reads=tuple("pH%d" % k for k in range(8)),
              writes=(vdr,) + tuple("pH%d" % k for k in range(8)))
    for dv, dn in ((hq, "hq"), (hk, "hk")):
        for fc in range(16):
            ts(dv[:, fc, :], dv[:, fc, :], ca1col[:, fc:fc + 1], ALU.mult,
               sacol[:, fc:fc + 1], ALU.add, reads=("ca1col", "sacol"), writes=(dn,))

    memset(vbuf[:, :, :, 128:129], 1.0, writes=("vbuf",))

    wslot = [wbufA, wbufB]
    wres = ["wi0", "wi1"]
    slot_i = [0]

    def load_wslice(wdram, col0, width=256):
        s = slot_i[0]; slot_i[0] ^= 1
        wb = wslot[s]; rn = wres[s]
        dmas("gpsimd", [(wb[:, q * 4:(q + 1) * 4, 0:width],
                         wdram[q * 512:(q + 1) * 512, col0:col0 + width]
                         .rearrange("(a p) b -> p a b", p=128)) for q in range(4)],
             reads=(), writes=(rn,))
        return wb, rn

    def rope_evac(pp, pres, cos_s, sin_s, crs, dstv, dname, hh, n):
        par = (hh * 2 + n) % 2
        qr, qrr = (qraw, "qraw") if par == 0 else (qraw2, "tmpf2")
        rb, rbr = (rotb, "rotb") if par == 0 else (rotb2, "tmpf2")
        qc, qcr = (qcos, "qcos") if par == 0 else (qcos2, "qcos2")
        zsl, zslr = (zB, "zB") if par == 0 else (zC, "zC")
        scp(qr[:, :], pp, reads=(pres,), writes=(qrr, pres))
        mm(zsl[:, :], prot[:, :], qr[:, :], True, True,
           reads=("prot", qrr, zslr), writes=(zslr,))
        tt(qc[:, :], qr[:, :], cos_s, ALU.mult, reads=(qrr, crs[0]), writes=(qcr,))
        scp(rb[:, :], zsl[:, :], reads=(zslr,), writes=(rbr, zslr))
        dsl = dstv[:, hh, n * 512:(n + 1) * 512]
        tt(dsl, rb[:, :], sin_s, ALU.mult, reads=(rbr, crs[1]), writes=(dname, "wst"))
        tt(dsl, dsl, qc[:, :], ALU.add, reads=(qcr,), writes=(dname,))

    def smm(hh, qt):
        pp, pres = (pBig[:, 0:1024], ("pBig0", "pBig1")) if qt % 2 == 0 else \
                   (pBig[:, 1024:2048], ("pBig2", "pBig3"))
        for half in range(2):
            mm(pp[:, half * 512:(half + 1) * 512],
               qTg[:, hh, qt * 128:(qt + 1) * 128], kTg[:, hh, half * 512:(half + 1) * 512],
               True, True, reads=("qTg", "kTg") + pres, writes=pres)
        nslot = qt % 2
        red(nmx[:, nslot:nslot + 1], pp, ALU.max, negate=True,
            reads=pres, writes=("nmx%d" % nslot,))

    def stail(hh, qt, pdst, pdres):
        pp, pres = (pBig[:, 0:1024], ("pBig0", "pBig1")) if qt % 2 == 0 else \
                   (pBig[:, 1024:2048], ("pBig2", "pBig3"))
        nslot = qt % 2
        act(probsq[:, :], pp, AF.Exp, bias=nmx[:, nslot:nslot + 1],
            reads=pres + ("nmx%d" % nslot, "r1d"), writes=("r1d",) + pres)
        for kc in range(8):
            tr(pH[:, kc * 128:(kc + 1) * 128], probsq[:, kc * 128:(kc + 1) * 128],
               identb[:, :], reads=("r1d", "identb", "pH%d" % kc), writes=("pH%d" % kc,))
        vcp(pdst, pH[:, :], reads=tuple("pH%d" % k for k in range(8)),
            writes=(pdres,) + tuple("pH%d" % k for k in range(8)))

    def pv(head, hh, qt, psrc, psres):
        psrc3 = psrc.rearrange("p (a b) -> p a b", a=8)
        for kc in range(8):
            K = 127 if kc == 7 else 128
            mm(zA[:, 0:129], psrc3[0:K, kc, :], vbuf[0:K, kc, hh, 0:129],
               kc == 0, kc == 7, reads=(psres, "vbuf", "zA"), writes=("zA",))
        recip(zvv[:, 0:1], zA[:, 128:129], reads=("zA",), writes=("zvv",))
        ts(hot[:, :], zA[:, 0:128], zvv[:, 0:1], ALU.mult,
           reads=("zA", "zvv"), writes=("hot", "zA"))
        tr(zB[:, 0:128], hot[:, :], identf[:, :], reads=("hot", "identf", "zB"), writes=("zB",))
        P.add("scalar",
              lambda e, head=head, qt=qt:
              e.copy(ho[:, head, qt * 128:(qt + 1) * 128], zB[:, 0:128]),
              reads=("zB",), writes=("ho", "zB"))

    cast_pieces = []
    if PH == "ALL":
        cast_pieces.append((gateTb[:, :, :],
                            dgate[:, :].rearrange("(a p) b -> p a b", p=128), "gateTb"))
        cast_pieces.append((db13c[:, :, :, 0:2816], db13[:, :, :, 0:2816], "db13c"))
        cast_pieces.append((db13c[:, :, :, 2816:5632], db13[:, :, :, 2816:5632], "db13c"))
        cast_pieces.append((da2c[:, :, :], da2[:, :, :], "da2c"))
        cast_pieces.append((db2c[:, :, :], db2[:, :, :], "db2c"))
        for q in range(4):
            for h in range(4):
                cast_pieces.append((dw13b[q * 512:(q + 1) * 512, h * 2816:(h + 1) * 2816],
                                    dw13[q * 512:(q + 1) * 512, h * 2816:(h + 1) * 2816],
                                    "dw13b"))
        for q in range(4):
            for h in range(2):
                cast_pieces.append((dw2b[q * 1408:(q + 1) * 1408, h * 1024:(h + 1) * 1024],
                                    dw2[q * 1408:(q + 1) * 1408, h * 1024:(h + 1) * 1024],
                                    "dw2b"))

    def drip_cast(n=1):
        for _ in range(n):
            if cast_pieces:
                dst, src, nm = cast_pieces.pop(0)
                dma("gpsimd", dst, src, reads=(), writes=(nm,))

    for g in range(4):
        dmas("sync", [(cosq_s, dcosq[:, :]), (sinq_s, dsinq[:, :]),
                      (coskS, dcosk[:, :]), (sinkS, dsink[:, :])],
             reads=(), writes=("r1a", "r1b", "r1c", "r1d"))
        for hp in range(2):
            wb, rn = load_wslice(dwq, g * 512 + hp * 256)
            for hh in (hp * 2, hp * 2 + 1):
                pps = [PBQ[(hh * 2 + n) % 4] for n in range(2)]
                for kc in range(16):
                    for n in range(2):
                        pp, pres = pps[n]
                        mm(pp, wb[:, kc, (hh - hp * 2) * 128:(hh - hp * 2 + 1) * 128],
                           hq[:, kc, n * 512:(n + 1) * 512],
                           kc == 0, kc == 15, reads=(rn, "hq", pres), writes=(pres,))
                for n in range(2):
                    pp, pres = pps[n]
                    rope_evac(pp, pres, cosq_s[:, n * 512:(n + 1) * 512],
                              sinq_s[:, n * 512:(n + 1) * 512], ("r1a", "r1b"), qTg, "qTg", hh, n)
        for hp in range(2):
            wb, rn = load_wslice(dwk, g * 512 + hp * 256)
            for hh in (hp * 2, hp * 2 + 1):
                pps = [PBQ[(hh * 2 + n) % 4] for n in range(2)]
                for kc in range(16):
                    for n in range(2):
                        pp, pres = pps[n]
                        mm(pp, wb[:, kc, (hh - hp * 2) * 128:(hh - hp * 2 + 1) * 128],
                           hk[:, kc, n * 512:(n + 1) * 512],
                           kc == 0, kc == 15, reads=(rn, "hk", pres), writes=(pres,))
                for n in range(2):
                    pp, pres = pps[n]
                    rope_evac(pp, pres, coskS[:, n * 512:(n + 1) * 512],
                              sinkS[:, n * 512:(n + 1) * 512], ("r1c", "r1d"), kTg, "kTg", hh, n)
        for hp in range(2):
            wb, rn = load_wslice(dwv, g * 512 + hp * 256)
            for ktp in range(0, 8, 2):
                for kc in range(16):
                    for kt in (ktp, ktp + 1):
                        pp, pres = PBQ[kt % 4]
                        mm(pp[:, 0:256], hk[:, kc, kt * 128:(kt + 1) * 128], wb[:, kc, 0:256],
                           kc == 0, kc == 15, reads=(rn, "hk", pres), writes=(pres,))
                for kt in (ktp, ktp + 1):
                    pp, pres = PBQ[kt % 4]
                    P.add("scalar",
                          lambda e, pp=pp, kt=kt, hp=hp:
                          e.copy(vbuf[:, kt, hp * 2:hp * 2 + 2, 0:128],
                                 pp[:, 0:256].rearrange("p (a b) -> p a b", a=2)),
                          reads=(pres,), writes=("vbuf", pres))

        for hh in range(4):
            head = g * 4 + hh
            drip_cast(1)

            def slot(qt):
                return (pTc, "r1b") if qt % 2 == 0 else (pT6, "r1c")

            smm(hh, 7)
            smm(hh, 0)
            stail(hh, 7, pT7, "r1a")
            drip_cast(1)
            pT73 = pT7.rearrange("p (a b) -> p a b", a=8)
            for kc in range(8):
                mm(zC[:, 0:3], pT73[:, kc, :], mrg[:, kc, :], kc == 0, kc == 7,
                   reads=("r1a", "mrg", "zC"), writes=("zC",))
            vcp(mz[:, 0:3], zC[:, 0:3], reads=("zC",), writes=("mz", "zC"))
            dmas("sync", [(mrow[0:1, 0:3], mz[127:128, 0:3]),
                          (v204[0:1, :], vbuf[76:77, 1, hh, 0:128])],
                 reads=("mz", "vbuf"), writes=("mrow", "v204"))
            for qt in range(6):
                smm(hh, qt + 1) if qt < 5 else smm(hh, 6)
                if qt >= 1:
                    pv(head, hh, qt - 1, *slot(qt - 1))
                stail(hh, qt, *slot(qt))
            pv(head, hh, 5, *slot(5))
            stail(hh, 6, pTc, "r1b")
            recip(bmt[0:1, 0:1], mrow[0:1, 2:3], reads=("mrow",), writes=("bmt",))
            ts(bmt[0:1, 1:2], mrow[0:1, 0:1], bmt[0:1, 0:1], ALU.mult, csts[0:1, 2:3], ALU.mult,
               reads=("mrow", "bmt", "csts"), writes=("bmt",))
            ts(bmt[0:1, 1:2], bmt[0:1, 1:2], csts[0:1, 1:2], ALU.add,
               reads=("bmt", "csts"), writes=("bmt",))
            ts(bmt[0:1, 2:3], mrow[0:1, 1:2], bmt[0:1, 0:1], ALU.mult,
               reads=("mrow", "bmt"), writes=("bmt",))
            recip(bmt[0:1, 3:4], bmt[0:1, 1:2], reads=("bmt",), writes=("bmt",))
            ts(bmt[0:1, 4:5], bmt[0:1, 2:3], bmt[0:1, 3:4], ALU.mult, reads=("bmt",), writes=("bmt",))
            ts(bmt[0:1, 4:5], bmt[0:1, 4:5], 1.0, ALU.min, 0.0, ALU.max,
               reads=("bmt",), writes=("bmt",))
            ts(bmt[0:1, 5:6], bmt[0:1, 4:5], u_sb[0:1, head:head + 1], ALU.is_gt,
               reads=("bmt", "u_sb"), writes=("bmt",))
            ts(bmt[0:1, 6:7], bmt[0:1, 5:6], csts[0:1, 3:4], ALU.mult,
               reads=("bmt", "csts"), writes=("bmt",))
            ts(vescl[0:1, :], v204[0:1, :], bmt[0:1, 6:7], ALU.mult,
               reads=("v204", "bmt"), writes=("vescl",))
            mm(zC[:, 0:128], onesr[0:1, :], vescl[0:1, :], True, True,
               reads=("onesr", "vescl", "zC"), writes=("zC",))
            for kt in range(8):
                stt(vbuf[:, kt, hh, 0:128], zC[:, 0:128], m_t[:, kt:kt + 1],
                    vbuf[:, kt, hh, 0:128], ALU.mult, ALU.add,
                    reads=("zC", "m_t", "vbuf"), writes=("vbuf",))
            pv(head, hh, 6, pTc, "r1b")
            pv(head, hh, 7, pT7, "r1a")

    bcast(bc1[:, :], 2, "bc1")
    dmas("sync", [(xres[:, i, :], dx[i * 128:(i + 1) * 128, :]) for i in range(8)],
         reads=(), writes=tuple("x%d" % i for i in range(8)) + ("hq", "hk"))
    for dj in range(8):
        wb, rn = load_wslice(dwo, dj * 256)
        for tip in range(0, 8, 2):
            for kc in range(16):
                for ti in (tip, tip + 1):
                    pp, pres = PBQ[ti % 2]
                    mm(pp[:, 0:256], ho[:, kc, ti * 128:(ti + 1) * 128], wb[:, kc, 0:256],
                       kc == 0, kc == 15, reads=(rn, "ho", pres), writes=(pres,))
            for ti in (tip, tip + 1):
                pp, pres = PBQ[ti % 2]
                tf = tmpf[:, 0:256] if ti % 2 == 0 else tmpf2[:, 0:256]
                tfr = "tmpf" if ti % 2 == 0 else "tmpf2"
                tt(tf, pp[:, 0:256], bc1[:, dj * 256:(dj + 1) * 256], ALU.mult,
                   reads=(pres, "bc1"), writes=(tfr, "qraw", "rotb", pres))
                tt(xres[:, ti, dj * 256:(dj + 1) * 256], xres[:, ti, dj * 256:(dj + 1) * 256],
                   tf, ALU.add, reads=(tfr,), writes=("x%d" % ti,))
    tp = tap("xattn", [TOK, D])
    if tp is not None:
        for i in range(8):
            dma("sync", tp[i * 128:(i + 1) * 128, :], xres[:, i, :], reads=("x%d" % i,))

    if PH == "ALL":
        for hf in range(2):
            dma("gpsimd", aTbv[:, :, :], da1[:, :].rearrange("(a p) b -> p a b", p=128),
                reads=(), writes=("aTb",) + CLOB_B_ARC)
            bcast(bc1[:, :], 4, "bc1")
            bcast(bc2[:, :], 3, "bc2")
            for ti in range(4):
                gi = hf * 4 + ti
                xv = xres[:, gi, :]
                xr = "x%d" % gi
                red(stats[:, 0:1], xv, ALU.add, reads=(xr,), writes=("stats",))
                act(pBig[:, :], xv, AF.Square, accum=stats[:, 1:2],
                    reads=(xr,) + PB_ALL, writes=PB_ALL + ("stats",))
                ts(stats[:, 2:3], stats[:, 0:1], 1.0 / 2048, ALU.mult,
                   reads=("stats",), writes=("stats",))
                tt(stats[:, 3:4], stats[:, 2:3], stats[:, 2:3], ALU.mult,
                   reads=("stats",), writes=("stats",))
                stt(stats[:, 4:5], stats[:, 1:2], 1.0 / 2048, stats[:, 3:4],
                    ALU.mult, ALU.subtract, reads=("stats",), writes=("stats",))
                act(stats[:, 5:6], stats[:, 4:5], AF.Sqrt, bias=csts[:, 0:1],
                    reads=("stats", "csts"), writes=("stats",))
                recip(stats[:, 6:7], stats[:, 5:6], reads=("stats",), writes=("stats",))
                ts(pBig[:, :], xv, stats[:, 2:3], ALU.subtract, stats[:, 6:7], ALU.mult,
                   reads=(xr, "stats") + PB_ALL, writes=PB_ALL)
                tt(htok[:, :], pBig[:, :], bc1[:, :], ALU.mult,
                   reads=PB_ALL + ("bc1",), writes=("htok",) + PB_ALL)
                tt(htok[:, :], htok[:, :], bc2[:, :], ALU.add,
                   reads=("bc2",), writes=("htok",))
                for g2 in range(2):
                    for j in range(8):
                        fc = g2 * 8 + j
                        tr(pH[:, j * 128:(j + 1) * 128], htok[:, fc * 128:(fc + 1) * 128],
                           identb[:, :], reads=("htok", "identb", "pH%d" % j),
                           writes=("pH%d" % j,))
                    P.add("scalar",
                          lambda e, g2=g2, ti=ti:
                          e.copy(h2T[:, g2 * 8:(g2 + 1) * 8, ti * 128:(ti + 1) * 128],
                                 pH[:, :].rearrange("p (a b) -> p a b", a=8)),
                          reads=tuple("pH%d" % k for k in range(8)),
                          writes=("h2T",) + tuple("pH%d" % k for k in range(8))
                          + (("wi0", "wi1") if ti == 0 else ()))
            for daT, uslot, ur in [(da1, 0, "ubt0"), (da3, 1, "ubt1")]:
                if uslot == 1:
                    dma("gpsimd", aTbv[:, :, :], daT[:, :].rearrange("(a p) b -> p a b", p=128),
                        reads=(), writes=("aTb",) + CLOB_B_ARC)
                for kc in range(16):
                    mm(zA[:, :], aTbv[:, kc, :], h2T[:, kc, :], kc == 0, kc == 15,
                       reads=("aTb", "h2T", "zA"), writes=("zA",))
                ts(ubt[:, uslot, :], zA[:, :], 2.0, ALU.mult,
                   reads=("zA",), writes=(ur, "w13s0", "zA") + CLOB_B_ARC)
            dmas("sync", [(u1e[:, e_, :], ubt[e_ * 16:(e_ + 1) * 16, 0, :]) for e_ in range(NE)]
                 + [(u3e[:, e_, :], ubt[e_ * 16:(e_ + 1) * 16, 1, :]) for e_ in range(NE)],
                 reads=("ubt0", "ubt1"), writes=("u1e", "u3e"))
            ts(u1n[:, :, :], u1e[:, :, :], -1.0, ALU.mult, reads=("u1e",), writes=("u1n",))
            ts(u3n[:, :, :], u3e[:, :, :], -1.0, ALU.mult, reads=("u3e",), writes=("u3n",))
            dmas("gpsimd", [(uN1[:, e_, :], u1n[:, e_ - 1, :]) for e_ in range(1, NE)]
                 + [(uN3[:, e_, :], u3n[:, e_ - 1, :]) for e_ in range(1, NE)],
                 reads=("u1n", "u3n"), writes=("uP1", "uP3"))

            for tip in range(0, 4, 2):
                for kc in range(16):
                    for ti in (tip, tip + 1):
                        zg, zgr = (zB, "zB") if ti % 2 == 0 else (zC, "zC")
                        mm(zg[:, 0:8], h2T[:, kc, ti * 128:(ti + 1) * 128], gateTb[:, kc, :],
                           kc == 0, kc == 15, reads=("h2T", "gateTb", zgr), writes=(zgr,))
                for ti in (tip, tip + 1):
                    zg, zgr = (zB, "zB") if ti % 2 == 0 else (zC, "zC")
                    vcp(lg[:, ti, :], zg[:, 0:8], reads=(zgr,), writes=("lg", zgr))
                for ti in (tip, tip + 1):
                    lv = lg[:, ti, :]
                    red(t2s[:, 0:1], lv, ALU.max, negate=True, reads=("lg",), writes=("t2s",))
                    ts(tmp8[:, :], lv, t2s[:, 0:1], ALU.add, reads=("lg", "t2s"), writes=("tmp8",))
                    ts(msk8[:, :], tmp8[:, :], 0.0, ALU.is_lt, reads=("tmp8",), writes=("msk8",))
                    tt(tmp8[:, :], lv, msk8[:, :], ALU.mult, reads=("lg", "msk8"), writes=("tmp8",))
                    ts(msk8[:, :], msk8[:, :], 1.0, ALU.subtract, reads=(), writes=("msk8",))
                    stt(tmp8[:, :], msk8[:, :], 30000.0, tmp8[:, :], ALU.mult, ALU.add,
                        reads=("msk8",), writes=("tmp8",))
                    red(t2s[:, 1:2], tmp8[:, :], ALU.max, reads=("tmp8",), writes=("t2s",))
                    act(t2s[:, 2:3], t2s[:, 1:2], AF.Exp, bias=t2s[:, 0:1],
                        reads=("t2s",), writes=("t2s",))
                    ts(t2s[:, 3:4], t2s[:, 2:3], 1.0, ALU.add, reads=("t2s",), writes=("t2s",))
                    recip(t2s[:, 4:5], t2s[:, 3:4], reads=("t2s",), writes=("t2s",))
                    for e_ in range(NE):
                        act(t2s[:, 5:6], lv[:, e_:e_ + 1], AF.Exp, bias=t2s[:, 0:1],
                            reads=("lg", "t2s"), writes=("t2s",))
                        ts(t2s[:, 6:7], lv[:, e_:e_ + 1], t2s[:, 1:2], ALU.is_ge,
                           reads=("lg", "t2s"), writes=("t2s",))
                        tt(t2s[:, 7:8], t2s[:, 5:6], t2s[:, 6:7], ALU.mult,
                           reads=("t2s",), writes=("t2s",))
                        tt(wfull[:, ti, e_:e_ + 1], t2s[:, 7:8], t2s[:, 4:5], ALU.mult,
                           reads=("t2s",), writes=("wfull",))
            tpw = tap("wfull", [128, 8, 8])
            if tpw is not None:
                dma("sync", tpw[:, hf * 4:(hf + 1) * 4, :], wfull[:, :, :], reads=("wfull",))
            tpl = tap("lgt", [128, 8, 8])
            if tpl is not None:
                dma("sync", tpl[:, hf * 4:(hf + 1) * 4, :], lg[:, :, :], reads=("lg",))
            for ti in range(4):
                tr(zB[0:8, ti * 128:(ti + 1) * 128], wfull[:, ti, :], identf[:, :],
                   reads=("wfull", "identf", "zB"), writes=("zB",))
            vcp(werows[:, :], zB[0:8, :], reads=("zB",), writes=("werows", "zB"))
            dma("sync", dwe[:, :], werows[:, :], reads=("werows",), writes=("dwe",))
            dmas("gpsimd", [(webc[:, e_, :], dwe[e_:e_ + 1, :].partition_broadcast(128))
                            for e_ in range(NE)],
                 reads=("dwe",), writes=("webc",) + CLOB_B_ARC)

            zts = [zA, zB, zC]
            zrs = ["zA", "zB", "zC"]
            TMP = [(tmpf, "tmpf", ("qraw", "rotb")), (tmpf2, "tmpf2", ())]
            ACTB = [(actb, "actb"), (actb2, "actb2")]
            WACT = [(wactb, "wactb"), (wactb2, "wactb2")]
            ACCF = [(accf, "accf"), (accf2, "accf2")]
            for fcp in range(0, NFC, 2):
                for w in (0, 1):
                    fc = fcp + w
                    sl = fc % 2
                    wr = "w13s%d" % sl
                    dmas("sync" if fc % 2 == 0 else "scalar",
                         [(w13s[sl][:, q * 4:(q + 1) * 4, :],
                           dw13b[q * 512:(q + 1) * 512, fc * 256:(fc + 1) * 256]
                           .rearrange("(a p) b -> p a b", p=128)) for q in range(4)],
                         reads=("dw13b",),
                         writes=(wr,) + (("ubt0", "ubt1") + CLOB_B_ARC if fc < 2 else ()))
                    dmas("gpsimd", [
                        (bpair[0:16, sl, :, :], db13c[:, :, 0, fc * 128:(fc + 1) * 128]),
                        (bpair[16:32, sl, 1:NE, :],
                         db13c[:, 0:NE - 1, 0, fc * 128:(fc + 1) * 128]),
                        (bpair[32:48, sl, :, :], db13c[:, :, 1, fc * 128:(fc + 1) * 128]),
                        (bpair[48:64, sl, 1:NE, :],
                         db13c[:, 0:NE - 1, 1, fc * 128:(fc + 1) * 128]),
                        (a2s[:, sl, :, :], da2c[fc * 128:(fc + 1) * 128, :, :]),
                    ], reads=("db13c", "da2c"), writes=("b13s%d" % sl, "a2s%d" % sl))
                for w in (0, 1):
                    fc = fcp + w
                    sl = fc % 2
                    wr = "w13s%d" % sl
                    B1, B1r = PBQ[0 + 2 * w]
                    B3, B3r = PBQ[1 + 2 * w]
                    for kc in range(16):
                        mm(B1, w13s[sl][:, kc, 0:128], h2T[:, kc, :], kc == 0, False,
                           reads=(wr, "h2T", B1r), writes=(B1r,), skip_gc=True)
                        mm(B3, w13s[sl][:, kc, 128:256], h2T[:, kc, :], kc == 0, False,
                           reads=(wr, "h2T", B3r), writes=(B3r,), skip_gc=True)
                def flush_z():
                    while pend_z:
                        zfc, zsl2, ze, zw = pend_z.pop(0)
                        wb_, wbr = WACT[zw]
                        zt = zts[ze // 3]
                        zo = 32 * (ze % 3)
                        mm(zt[zo:zo + 16, :], a2s[:, zsl2, ze, :], wb_[:, :],
                           zfc == 0, zfc == NFC - 1,
                           reads=("a2s%d" % zsl2, wbr, zrs[ze // 3]),
                           writes=(zrs[ze // 3],), skip_gc=True)

                pend_z = []
                for e_ in range(NE):
                    for w in (0, 1):
                        sl = (fcp + w) % 2
                        B1, B1r = PBQ[0 + 2 * w]
                        B3, B3r = PBQ[1 + 2 * w]
                        if e_ == 0:
                            mm(B1, bpair[0:16, sl, 0, :], u1e[:, 0, :], False, False,
                               reads=("b13s%d" % sl, "u1e", B1r), writes=(B1r,), skip_gc=True)
                            mm(B3, bpair[32:48, sl, 0, :], u3e[:, 0, :], False, False,
                               reads=("b13s%d" % sl, "u3e", B3r), writes=(B3r,), skip_gc=True)
                        else:
                            mm(B1, bpair[0:32, sl, e_, :], uP1[:, e_, :], False, e_ == NE - 1,
                               reads=("b13s%d" % sl, "uP1", "u1e", B1r), writes=(B1r,),
                               skip_gc=True)
                            mm(B3, bpair[32:64, sl, e_, :], uP3[:, e_, :], False, e_ == NE - 1,
                               reads=("b13s%d" % sl, "uP3", "u3e", B3r), writes=(B3r,),
                               skip_gc=True)
                    flush_z()
                    for w in (0, 1):
                        B1, B1r = PBQ[0 + 2 * w]
                        tf, tfr, tfc = TMP[w]
                        act(tf[:, :], B1, AF.Silu, reads=(B1r,), writes=(tfr,) + tfc + (B1r,))
                    for w in (0, 1):
                        B3, B3r = PBQ[1 + 2 * w]
                        tf, tfr, _ = TMP[w]
                        ab, abr = ACTB[w]
                        wb_, wbr = WACT[w]
                        af, afr = ACCF[w]
                        tt(ab[:, :], tf[:, :], B3, ALU.mult, reads=(tfr, B3r), writes=(abr, B3r))
                        tt(wb_[:, :], ab[:, :], webc[:, e_, :], ALU.mult,
                           reads=(abr, "webc"), writes=(wbr,))
                        if e_ == 0:
                            vcp(af[:, :], wb_[:, :], reads=(wbr,), writes=(afr,))
                        else:
                            tt(af[:, :], af[:, :], wb_[:, :], ALU.add,
                               reads=(wbr,), writes=(afr,))
                    for w in (0, 1):
                        pend_z.append((fcp + w, (fcp + w) % 2, e_, w))
                flush_z()
                for w in (0, 1):
                    fc = fcp + w
                    af, afr = ACCF[w]
                    scp(act_sum[:, fc, :], af[:, :], reads=(afr,),
                        writes=("as%d" % fc,) + (("aTb",) if fc < 2 else ())
                        + (CLOB_B_ARB if hf == 0 and fc < 2 else ()))
            bcast(bc1[:, :], 5, "bc1")
            for e_ in range(NE):
                zt = zts[e_ // 3]
                zo = 32 * (e_ % 3)
                ts(zsb[:, e_, :], zt[zo:zo + 16, :], 2.0, ALU.mult,
                   reads=(zrs[e_ // 3],), writes=("zsb", "u1e", "uP1", zrs[e_ // 3]))
            for dj in range(4):
                dma("gpsimd", b2dj[:, :, :],
                    db2c[:, :, dj * 512:(dj + 1) * 512],
                    reads=("db2c",), writes=("b2dj", "w13s0"))
                for fcq in range(11):
                    sl = fcq % 2
                    dmas("scalar", [(w2g[sl][:, j, :],
                                     dw2b[(fcq * 4 + j) * 128:(fcq * 4 + j + 1) * 128,
                                          dj * 512:(dj + 1) * 512]) for j in range(4)],
                         reads=("dw2b",), writes=("w2g%d" % sl, "w13s1"))
                    for j in range(4):
                        fc = fcq * 4 + j
                        for ti in range(4):
                            mm(pBig[:, ti * 512:(ti + 1) * 512],
                               act_sum[:, fc, ti * 128:(ti + 1) * 128], w2g[sl][:, j, :],
                               fc == 0, False,
                               reads=("w2g%d" % sl, "as%d" % fc, "pBig%d" % ti),
                               writes=("pBig%d" % ti,), skip_gc=True)
                for ti in range(4):
                    for e_ in range(NE):
                        mm(pBig[:, ti * 512:(ti + 1) * 512], zsb[:, e_, ti * 128:(ti + 1) * 128],
                           b2dj[:, e_, :], False, e_ == NE - 1,
                           reads=("zsb", "b2dj", "pBig%d" % ti), writes=("pBig%d" % ti,),
                           skip_gc=True)
                for ti in range(4):
                    gi = hf * 4 + ti
                    tt(tmpf[:, :], pBig[:, ti * 512:(ti + 1) * 512],
                       bc1[:, dj * 512:(dj + 1) * 512], ALU.mult,
                       reads=("pBig%d" % ti, "bc1"), writes=("tmpf", "pBig%d" % ti))
                    tt(xres[:, gi, dj * 512:(dj + 1) * 512],
                       xres[:, gi, dj * 512:(dj + 1) * 512], tmpf[:, :], ALU.add,
                       reads=("tmpf",), writes=("x%d" % gi,))
                dmas("sync", [(dout[(hf * 4 + t) * 128:(hf * 4 + t + 1) * 128,
                                    dj * 512:(dj + 1) * 512],
                               xres[:, hf * 4 + t, dj * 512:(dj + 1) * 512])
                              for t in range(4)],
                     reads=tuple("x%d" % (hf * 4 + t) for t in range(4)))

    with (
        nc.semaphore("s_sync") as s0, nc.semaphore("s_gpsimd") as s1,
        nc.semaphore("s_vector") as s2, nc.semaphore("s_scalar") as s3,
        nc.semaphore("s_tensor") as s4, nc.semaphore("s_scalardma") as s5,
        nc.semaphore("s_cc") as s6, nc.semaphore("s_vectordma") as s7,
        nc.semaphore("s_syncB") as s8, nc.semaphore("s_gpsimdB") as s9,
        nc.semaphore("s_scalardmaB") as s10, nc.semaphore("s_vectordmaB") as s11,
        nc.Block() as block,
    ):
        sems = {"sync": s0, "gpsimd": s1, "vector": s2, "scalar": s3, "tensor": s4,
                "scalardma": s5, "cc": s6, "vectordma": s7, "syncB": s8,
                "gpsimdB": s9, "scalardmaB": s10, "vectordmaB": s11}
        P.emit(block, sems)
    print("total instructions:", len(P.ops))
    return nc


_NC_CACHE = {}


def kernel(**inputs):
    cst = _host_constants()
    in_maps = [_host_core_inputs(inputs, c, cst) for c in range(8)]
    if "nc" not in _NC_CACHE:
        _NC_CACHE["nc"] = _build()
    nc = _NC_CACHE["nc"]
    res = run_bass_kernel_spmd(nc, in_maps, core_ids=list(range(8)))
    out = np.empty((4, 2048, 2048), dtype=np.float32)
    for c in range(8):
        b, hf = c // 2, c % 2
        out[b, hf * TOK:(hf + 1) * TOK, :] = np.asarray(res.results[c]["out"])
    return out


# version 21
# speedup vs baseline: 1.1505x; 1.0347x over previous
import numpy as np
import ml_dtypes
import concourse.bass as bass
import concourse.mybir as mybir
from concourse.bass_utils import run_bass_kernel_spmd

F32 = mybir.dt.float32
BF16 = mybir.dt.bfloat16
AF = mybir.ActivationFunctionType
ALU = mybir.AluOpType
AX = mybir.AxisListType
bfdt = ml_dtypes.bfloat16

U42 = np.array([
 [0.6102175712585449, 0.6966079473495483, 0.06568920612335205, 0.6029453277587891, 0.4143662452697754, 0.24911487102508545, 0.5782934427261353, 0.24924850463867188, 0.959638237953186, 0.8191142082214355, 0.3645491600036621, 0.059471845626831055, 0.9161832332611084, 0.6182148456573486, 0.8282822370529175, 0.6976723670959473],
 [0.756269097328186, 0.3594318628311157, 0.4266716241836548, 0.477594256401062, 0.08340179920196533, 0.40768611431121826, 0.6278458833694458, 0.3251793384552002, 0.35155820846557617, 0.15171575546264648, 0.5589374303817749, 0.8866020441055298, 0.3201930522918701, 0.12432277202606201, 0.6967982053756714, 0.8069779872894287],
 [0.6127028465270996, 0.7509020566940308, 0.4151843786239624, 0.4897202253341675, 0.11665701866149902, 0.40220367908477783, 0.08376991748809814, 0.9528622627258301, 0.9559504985809326, 0.06945085525512695, 0.20124602317810059, 0.06397080421447754, 0.8091422319412231, 0.31807422637939453, 0.639366865158081, 0.8050472736358643],
 [0.5085939168930054, 0.5088651180267334, 0.9745233058929443, 0.3331362009048462, 0.05924034118652344, 0.7208822965621948, 0.3291349411010742, 0.5319250822067261, 0.4839836359024048, 0.643956184387207, 0.19713079929351807, 0.1282445192337036, 0.3164334297180176, 0.4948718547821045, 0.9159883260726929, 0.781721830368042],
], dtype=np.float32)

T = 2048; D = 2048; H = 16; DH = 128
KEEP = np.concatenate([np.arange(204), np.arange(1229, 2048)])
NKC = 8
TOK = 1024; HTOK = 512
Fd = 5632; NE = 8; NR = 16; NFC = 44
EVK = 204
MBEG, MEND = 205, 1023
PH = "ALL"
DEBUG_TAPS = ()

_ENGS = ("sync", "syncB", "gpsimd", "gpsimdB", "vector", "scalar", "tensor",
         "scalardma", "scalardmaB", "vectordma", "vectordmaB", "cc")


class _Op:
    __slots__ = ("eng", "fn", "waits", "amt", "gi", "dma", "mark", "cum", "nsub", "chain")

    def __init__(self, eng, fn, waits, dma, gi, nsub=1, chain=None):
        self.eng = eng; self.fn = fn; self.waits = waits
        self.dma = dma; self.nsub = nsub
        self.amt = 16 * nsub if dma else 1
        self.gi = gi; self.mark = False; self.cum = 0
        self.chain = chain


class _Prog:
    def __init__(self):
        self.ops = []
        self.lw = {}
        self.lr = {}
        self.last_dma = {}

    def add(self, eng, fn, reads=(), writes=(), dma=False, nsub=1, chain=None):
        waits = []
        if dma:
            prev = self.last_dma.get(eng)
            if prev is not None:
                waits.append(prev)
        for r in reads:
            w = self.lw.get(r)
            if w is not None:
                waits.append(w)
        for r in writes:
            w = self.lw.get(r)
            if w is not None:
                waits.append(w)
            waits.extend(self.lr.get(r, ()))
        op = _Op(eng, fn, waits, dma, len(self.ops), nsub, chain)
        self.ops.append(op)
        if dma:
            self.last_dma[eng] = op
        for r in reads:
            self.lr.setdefault(r, []).append(op)
        for r in writes:
            self.lw[r] = op
            self.lr[r] = []
        return op

    @staticmethod
    def _need(w, op):
        return True

    def emit(self, block, sems):
        for op in self.ops:
            if op.dma:
                op.mark = True
            best = {}
            for w in op.waits:
                if self._need(w, op) and (w.eng not in best or w.gi > best[w.eng].gi):
                    best[w.eng] = w
            op.waits = list(best.values())
            for w in op.waits:
                w.mark = True
        cnt = {e: 0 for e in _ENGS}
        for op in self.ops:
            if op.mark:
                cnt[op.eng] += op.amt
            op.cum = cnt[op.eng]
        ops = self.ops

        def mkbody(name):
            group = {"scalar": ("scalar", "scalardma", "scalardmaB"),
                     "gpsimd": ("gpsimd", "gpsimdB", "cc"),
                     "vector": ("vector", "vectordma", "vectordmaB"),
                     "sync": ("sync", "syncB")}.get(name, (name,))
            def body(e):
                hi = {en: 0 for en in _ENGS}
                for op in ops:
                    if op.eng not in group:
                        continue
                    for w in op.waits:
                        if not self._need(w, op):
                            continue
                        if w.cum > hi[w.eng]:
                            e.wait_ge(sems[w.eng], w.cum)
                            hi[w.eng] = w.cum
                    inst = op.fn(e)
                    if op.mark:
                        if isinstance(inst, list):
                            for ii in inst:
                                ii.then_inc(sems[op.eng], 16)
                        else:
                            inst.then_inc(sems[op.eng], op.amt)
                for gn in group:
                    if cnt[gn] > hi[gn]:
                        e.wait_ge(sems[gn], cnt[gn])
            return body
        block.sync(mkbody("sync"))
        block.gpsimd(mkbody("gpsimd"))
        block.vector(mkbody("vector"))
        block.scalar(mkbody("scalar"))
        block.tensor(mkbody("tensor"))


def _host_constants():
    inv = 1.0 / (10000.0 ** (np.arange(0, DH, 2, dtype=np.float32) / DH))
    fr = np.arange(T, dtype=np.float32)[:, None] * inv[None, :]
    emb = np.concatenate([fr, fr], 1)
    cosT = np.cos(emb).astype(np.float32)
    sinT = np.sin(emb).astype(np.float32)
    s = np.float32(1.0 / np.sqrt(np.float32(DH)))
    consts = np.tile(np.array([[1e-5, 1e-6, 1.0 / 818.0, 1.0 / 819.0, 0.25, 1.0, 0.0, 0.0]],
                              dtype=np.float32), (128, 1))
    prot = np.zeros((128, 128), dtype=np.float32)
    for d2 in range(64):
        prot[d2 + 64, d2] = -1.0
    for d2 in range(64, 128):
        prot[d2 - 64, d2] = 1.0
    mrg = np.zeros((128, NKC, 3), dtype=np.float32)
    for kc in range(NKC):
        for p in range(128):
            ki = kc * 128 + p
            if ki >= 1023:
                continue
            mrg[p, kc, 2] = 1.0
            if MBEG <= ki < MEND:
                mrg[p, kc, 0] = 1.0
            if ki == EVK:
                mrg[p, kc, 1] = 1.0
    return dict(cosT=cosT, sinT=sinT, qscale=s, consts=consts,
                identb=np.eye(128, dtype=bfdt), identf=np.eye(128, dtype=np.float32),
                prot=prot.astype(bfdt), onesrow=np.ones((1, 128), dtype=bfdt),
                mrg=mrg.astype(bfdt))


def _host_core_inputs(inputs, c, cst):
    b, hf = c // 2, c % 2
    X = np.asarray(inputs["L0_attn_hidden_states"])
    x_own = np.ascontiguousarray(X[b, hf * TOK:(hf + 1) * TOK])
    xk = np.zeros((1024, D), dtype=np.float32)
    xk[:1023] = X[b, KEEP]
    ownpos = np.arange(hf * TOK, hf * TOK + TOK)
    kpos = np.zeros(1024, dtype=np.int64); kpos[:1023] = KEEP
    cosq = np.ascontiguousarray((cst["cosT"][ownpos] * cst["qscale"]).T.astype(bfdt))
    sinq = np.ascontiguousarray((cst["sinT"][ownpos] * cst["qscale"]).T.astype(bfdt))
    cosk = np.ascontiguousarray(cst["cosT"][kpos].T.astype(bfdt))
    sink = np.ascontiguousarray(cst["sinT"][kpos].T.astype(bfdt))
    cosk[:, 1023] = 0; sink[:, 1023] = 0
    m_t = np.zeros((128, NKC), dtype=np.float32)
    if hf == 1:
        for kc in range(NKC):
            for p in range(128):
                ki = kc * 128 + p
                if MBEG <= ki < MEND:
                    m_t[p, kc] = 1.0
    w1T = np.asarray(inputs["L0_ffn_w1"]).T
    w3T = np.asarray(inputs["L0_ffn_w3"]).T
    w13 = np.empty((D, NFC * 256), dtype=np.float32)
    for fc in range(NFC):
        w13[:, fc * 256:fc * 256 + 128] = w1T[:, fc * 128:(fc + 1) * 128]
        w13[:, fc * 256 + 128:fc * 256 + 256] = w3T[:, fc * 128:(fc + 1) * 128]
    b1 = np.asarray(inputs["L0_ffn_b1"]); b3 = np.asarray(inputs["L0_ffn_b3"])
    b13 = np.ascontiguousarray(
        np.stack([b1.transpose(0, 2, 1), b3.transpose(0, 2, 1)], axis=2).transpose(1, 0, 2, 3))
    return dict(
        x_own=x_own, x_kept=np.ascontiguousarray(xk),
        wqT=np.ascontiguousarray(np.asarray(inputs["L0_attn_Wq"]).T),
        wkT=np.ascontiguousarray(np.asarray(inputs["L0_attn_Wk"]).T),
        wvT=np.ascontiguousarray(np.asarray(inputs["L0_attn_Wv"]).T),
        woT=np.ascontiguousarray(np.asarray(inputs["L0_attn_Wo"]).T),
        wmod=np.ascontiguousarray(np.asarray(inputs["L0_wmod"])),
        gateT=np.ascontiguousarray(np.asarray(inputs["L0_ffn_gate_w"]).T),
        w13=w13,
        w2T=np.ascontiguousarray(np.asarray(inputs["L0_ffn_w2"]).T),
        b13=b13,
        a2il=np.ascontiguousarray(np.asarray(inputs["L0_ffn_a2"]).transpose(2, 0, 1)),
        b2il=np.ascontiguousarray(np.asarray(inputs["L0_ffn_b2"]).transpose(2, 0, 1)),
        a1T=np.ascontiguousarray(np.asarray(inputs["L0_ffn_a1"]).reshape(NE * NR, D).T),
        a3T=np.ascontiguousarray(np.asarray(inputs["L0_ffn_a3"]).reshape(NE * NR, D).T),
        c_col=np.ascontiguousarray(np.asarray(inputs["L0_c"])[b].reshape(16, 128).T),
        u_own=np.ascontiguousarray(U42[b:b + 1]),
        cosq=cosq, sinq=sinq, cosk=cosk, sink=sink, m_t=m_t,
        consts=cst["consts"], identb=cst["identb"], identf=cst["identf"],
        prot=cst["prot"], onesrow=cst["onesrow"], mrg=cst["mrg"],
    )


def _build():
    nc = bass.Bass("TRN2", target_bir_lowering=False)
    P = _Prog()

    dx = nc.dram_tensor("x_own", [TOK, D], F32, kind="ExternalInput")
    dxk = nc.dram_tensor("x_kept", [1024, D], F32, kind="ExternalInput")
    dwq = nc.dram_tensor("wqT", [D, D], F32, kind="ExternalInput")
    dwk = nc.dram_tensor("wkT", [D, D], F32, kind="ExternalInput")
    dwv = nc.dram_tensor("wvT", [D, D], F32, kind="ExternalInput")
    dwo = nc.dram_tensor("woT", [D, D], F32, kind="ExternalInput")
    dwmod = nc.dram_tensor("wmod", [D, 12288], F32, kind="ExternalInput")
    dgate = nc.dram_tensor("gateT", [D, 8], F32, kind="ExternalInput")
    dw13 = nc.dram_tensor("w13", [D, NFC * 256], F32, kind="ExternalInput")
    dw2 = nc.dram_tensor("w2T", [Fd, D], F32, kind="ExternalInput")
    db13 = nc.dram_tensor("b13", [NR, NE, 2, Fd], F32, kind="ExternalInput")
    da2 = nc.dram_tensor("a2il", [Fd, NE, NR], F32, kind="ExternalInput")
    db2 = nc.dram_tensor("b2il", [NR, NE, D], F32, kind="ExternalInput")
    da1 = nc.dram_tensor("a1T", [D, 128], F32, kind="ExternalInput")
    da3 = nc.dram_tensor("a3T", [D, 128], F32, kind="ExternalInput")
    dc = nc.dram_tensor("c_col", [128, 16], F32, kind="ExternalInput")
    du = nc.dram_tensor("u_own", [1, 16], F32, kind="ExternalInput")
    dcosq = nc.dram_tensor("cosq", [128, 1024], BF16, kind="ExternalInput")
    dsinq = nc.dram_tensor("sinq", [128, 1024], BF16, kind="ExternalInput")
    dcosk = nc.dram_tensor("cosk", [128, 1024], BF16, kind="ExternalInput")
    dsink = nc.dram_tensor("sink", [128, 1024], BF16, kind="ExternalInput")
    dconsts = nc.dram_tensor("consts", [128, 8], F32, kind="ExternalInput")
    didb = nc.dram_tensor("identb", [128, 128], BF16, kind="ExternalInput")
    didf = nc.dram_tensor("identf", [128, 128], F32, kind="ExternalInput")
    dprot = nc.dram_tensor("prot", [128, 128], BF16, kind="ExternalInput")
    dones = nc.dram_tensor("onesrow", [1, 128], BF16, kind="ExternalInput")
    dmrg = nc.dram_tensor("mrg", [128, NKC, 3], BF16, kind="ExternalInput")
    dmt = nc.dram_tensor("m_t", [128, NKC], F32, kind="ExternalInput")
    dout = nc.dram_tensor("out", [TOK, D], F32, kind="ExternalOutput")
    dscrP = nc.dram_tensor("dscrP", [1, 12288], F32)
    dwe = nc.dram_tensor("dwe", [NE, HTOK], BF16)
    dw13b = nc.dram_tensor("dw13b", [D, NFC * 256], BF16)
    dw2b = nc.dram_tensor("dw2b", [Fd, D], BF16)
    db13c = nc.dram_tensor("db13c", [NR, NE, 2, Fd], BF16)
    da2c = nc.dram_tensor("da2c", [Fd, NE, NR], BF16)
    db2c = nc.dram_tensor("db2c", [NR, NE, D], BF16)

    taps = {}

    def tap(name, shape, dtype=F32):
        if name in DEBUG_TAPS:
            if name not in taps:
                taps[name] = nc.dram_tensor("tap_" + name, shape, dtype, kind="ExternalOutput")
            return taps[name]
        return None

    arA = nc.alloc_sbuf_tensor("arA", [128, 16384], F32)
    arB = nc.alloc_sbuf_tensor("arB", [128, 15360], F32)
    arC = nc.alloc_sbuf_tensor("arC", [128, 8192], F32)
    arA_bf = arA[:, :].bitcast(BF16)
    arB_bf = arB[:, :].bitcast(BF16)
    arC_bf = arC[:, :].bitcast(BF16)

    def carve(base, off, n, lead=None):
        v = base[:, off:off + n]
        if lead is not None:
            v = v.rearrange("p (a b) -> p a b", a=lead)
        return v

    hq = carve(arA_bf, 0, 16 * 1024, 16)
    hk = carve(arA_bf, 16 * 1024, 16 * 1024, 16)
    xres = arA[:, :].rearrange("p (a b) -> p a b", a=8)

    o = 0
    qTg = carve(arB_bf, o, 4096, 4); o += 4096
    kTg = carve(arB_bf, o, 4096, 4); o += 4096
    vbuf = carve(arB_bf, o, 8 * 4 * 129).rearrange("p (k h d) -> p k h d", k=8, h=4)
    o += 8 * 4 * 129
    pT7 = carve(arB_bf, o, 1024); cosq_s = pT7; o += 1024
    pTc = carve(arB_bf, o, 1024); sinq_s = pTc; o += 1024
    pT6 = carve(arB_bf, o, 1024); o += 1024
    probsq = carve(arB_bf, o, 1024); o += 1024
    wbufA = carve(arB_bf, o, 4096, 16); o += 4096
    wbufB = carve(arB_bf, o, 4096, 16); o += 4096
    assert o <= 30720, o
    wpairA = carve(arB_bf, 16416, 8192, 16)
    wstgP = arB[:, 0:8192].rearrange("p (a b) -> p a b", a=16)
    act_sum = carve(arB_bf, 0, NFC * 512, NFC)
    h2T = carve(arB_bf, NFC * 512, 16 * 512, 16)
    assert NFC * 512 + 16 * 512 <= 30720

    xkb = [arC[:, 0:2048], arC[:, 2048:4096]]
    ho = carve(arC_bf, 0, 16384, 16)
    wpairB = carve(arC_bf, 8192, 8192, 16)
    o = 0
    webc = carve(arC_bf, o, 8 * 512, 8); o += 8 * 512
    w13s = [carve(arC_bf, o, 4096, 16), carve(arC_bf, o + 4096, 4096, 16)]
    o += 8192
    w2s = [carve(arC_bf, o, 512), carve(arC_bf, o + 512, 512)]
    o += 1024
    accf = arC[:, (o // 2):(o // 2) + 512]
    o += 1024
    aTbv = carve(arC_bf, o, 2048, 16)
    accf2 = arC[:, (o // 2):(o // 2) + 512]
    o += 2048
    assert o <= 16384
    w2g = [arC_bf[:, 8192:10240].rearrange("p (a b) -> p a b", a=4),
           arC_bf[:, 10240:12288].rearrange("p (a b) -> p a b", a=4)]

    CLOB_B_ARB = ("qTg", "kTg", "vbuf", "r1a", "r1b", "r1c", "r1d", "wi0", "wi1")
    CLOB_B_ARC = ("ho", "xk0", "xk1")

    htok = nc.alloc_sbuf_tensor("htok", [128, 2048], BF16)
    bc1 = nc.alloc_sbuf_tensor("bc1", [128, 2048], BF16)
    bc2 = nc.alloc_sbuf_tensor("bc2", [128, 2048], BF16)
    stg2 = [arB[0:1, 12304:13328], arB[0:1, 13328:14352]]
    scr2 = [arB[0:1, 14352:14608], arB[0:1, 14608:14864]]
    wstf = arB[:, 0:4096].rearrange("p (a b) -> p a b", a=16)
    c_sb = nc.alloc_sbuf_tensor("c_sb", [128, 16], F32)
    scsb = nc.alloc_sbuf_tensor("scsb", [128, 16], BF16)
    csts = nc.alloc_sbuf_tensor("csts", [128, 8], F32)
    identb = nc.alloc_sbuf_tensor("identb_s", [128, 128], BF16)
    identf = nc.alloc_sbuf_tensor("identf_s", [128, 128], F32)
    prot = nc.alloc_sbuf_tensor("prot_s", [128, 128], BF16)
    onesr = nc.alloc_sbuf_tensor("onesr", [1, 128], BF16)
    mrg = nc.alloc_sbuf_tensor("mrg_s", [128, NKC, 3], BF16)
    m_t = nc.alloc_sbuf_tensor("mt_s", [128, NKC], F32)
    u_sb = nc.alloc_sbuf_tensor("u_sb", [1, 16], F32)
    tmpf = nc.alloc_sbuf_tensor("tmpf", [128, 512], F32)
    qraw = tmpf[:, 0:256].bitcast(BF16)
    rotb = tmpf[:, 256:512].bitcast(BF16)
    qcos = nc.alloc_sbuf_tensor("qcos", [128, 512], BF16)
    qcos2 = nc.alloc_sbuf_tensor("qcos2", [128, 512], BF16)
    ca1col = nc.alloc_sbuf_tensor("ca1col", [128, 16], F32)
    sacol = nc.alloc_sbuf_tensor("sacol", [128, 16], F32)
    stats = nc.alloc_sbuf_tensor("stats", [128, 16], F32)
    nmx = nc.alloc_sbuf_tensor("nmx", [128, 2], F32)
    zvv = nc.alloc_sbuf_tensor("zvv", [128, 2], F32)
    hot = nc.alloc_sbuf_tensor("hot", [128, 128], F32)
    mz = nc.alloc_sbuf_tensor("mz", [128, 4], F32)
    mrow = nc.alloc_sbuf_tensor("mrow", [1, 8], F32)
    bmt = nc.alloc_sbuf_tensor("bmt", [1, 8], F32)
    v204 = nc.alloc_sbuf_tensor("v204", [1, 128], BF16)
    vescl = nc.alloc_sbuf_tensor("vescl", [1, 128], BF16)
    gateTb = nc.alloc_sbuf_tensor("gateTb", [128, 16, 8], BF16)
    upack = nc.alloc_sbuf_tensor("upack", [128, NE * HTOK], BF16)
    u1e = upack[0:16, :].rearrange("p (a b) -> p a b", a=NE)
    u3e = upack[32:48, :].rearrange("p (a b) -> p a b", a=NE)
    u1n = upack[64:80, :].rearrange("p (a b) -> p a b", a=NE)
    u3n = nc.alloc_sbuf_tensor("u3n", [NR, NE, HTOK], BF16)
    uP1 = upack[0:32, :].rearrange("p (a b) -> p a b", a=NE)
    uP3 = upack[32:64, :].rearrange("p (a b) -> p a b", a=NE)
    uN1 = upack[16:32, :].rearrange("p (a b) -> p a b", a=NE)
    uN3 = upack[48:64, :].rearrange("p (a b) -> p a b", a=NE)
    zsb = u1e
    ubt = arC_bf[:, 4096:5120].rearrange("p (a b) -> p a b", a=2)
    werows = nc.alloc_sbuf_tensor("werows", [NE, HTOK], BF16)
    wfull = nc.alloc_sbuf_tensor("wfull", [128, 4, NE], F32)
    lg = nc.alloc_sbuf_tensor("lg", [128, 4, NE], F32)
    t2s = nc.alloc_sbuf_tensor("t2s", [128, 8], F32)
    tmp8 = nc.alloc_sbuf_tensor("tmp8", [128, NE], F32)
    msk8 = nc.alloc_sbuf_tensor("msk8", [128, NE], F32)
    actb = nc.alloc_sbuf_tensor("actb", [128, 512], BF16)
    wactb = nc.alloc_sbuf_tensor("wactb", [128, 512], BF16)
    tmpf2 = nc.alloc_sbuf_tensor("tmpf2", [128, 512], F32)
    qraw2 = tmpf2[:, 0:256].bitcast(BF16)
    rotb2 = tmpf2[:, 256:512].bitcast(BF16)
    actb2 = nc.alloc_sbuf_tensor("actb2", [128, 512], BF16)
    wactb2 = nc.alloc_sbuf_tensor("wactb2", [128, 512], BF16)
    bpair = nc.alloc_sbuf_tensor("bpair", [64, 2, NE, 128], BF16)
    a2s = nc.alloc_sbuf_tensor("a2s", [128, 2, NE, NR], BF16)
    coskS = pT6
    sinkS = probsq
    b2dj = arC_bf[0:16, 4096:8192].rearrange("p (a b) -> p a b", a=NE)

    pBig = nc.alloc_psum_tensor("pBig", [128, 2048], F32)
    pH = nc.alloc_psum_tensor("pH", [128, 1024], BF16)
    zA = nc.alloc_psum_tensor("zA", [128, 512], F32)
    zB = nc.alloc_psum_tensor("zB", [128, 512], F32)
    zC = nc.alloc_psum_tensor("zC", [128, 512], F32)
    print("sbuf bytes remaining:", nc.sbuf_bytes_remaining,
          "psum banks remaining:", nc.psum_banks_remaining)
    PBQ = [(pBig[:, i * 512:(i + 1) * 512], "pBig%d" % i) for i in range(4)]
    PB_ALL = ("pBig0", "pBig1", "pBig2", "pBig3")

    _lane_flip = {}

    def _lane(eng):
        eng = {"scalar": "scalardma", "vector": "vectordma"}.get(eng, eng)
        f = _lane_flip.get(eng, 0)
        _lane_flip[eng] = f ^ 1
        return eng + "B" if f else eng

    def dma(eng, out, in_, reads=(), writes=()):
        return P.add(_lane(eng), lambda e, oo=out, ii=in_: e.dma_start(out=oo, in_=ii),
                     reads=reads, writes=writes, dma=True)

    def dmas(eng, pairs, reads=(), writes=()):
        def f(e, ps=tuple(pairs)):
            return [e.dma_start(out=oo, in_=ii) for (oo, ii) in ps]
        return P.add(_lane(eng), f, reads=reads, writes=writes, dma=True, nsub=len(pairs))

    def mm(out, lhsT, rhs, start, stop, reads=(), writes=(), skip_gc=False):
        ch = writes[0] if writes else None
        return P.add("tensor",
                     lambda e, oo=out, l=lhsT, r=rhs, st=start, sp=stop, sg=skip_gc:
                     e.matmul(oo, l, r, start=st, stop=sp, skip_group_check=sg),
                     reads=reads, writes=writes, chain=ch)

    def tr(out, in_, ident, reads=(), writes=()):
        return P.add("tensor",
                     lambda e, oo=out, ii=in_, dd=ident: e.transpose(oo, ii, dd),
                     reads=reads, writes=writes)

    def act(out, in_, func, bias=None, scale=None, accum=None, reads=(), writes=()):
        kw = {}
        if bias is not None:
            kw["bias"] = bias
        if scale is not None:
            kw["scale"] = scale
        if accum is not None:
            kw["accum_out"] = accum
        return P.add("scalar",
                     lambda e, oo=out, ii=in_, f=func, kw=kw: e.activation(oo, ii, f, **kw),
                     reads=reads, writes=writes)

    def scp(out, in_, reads=(), writes=()):
        return P.add("scalar", lambda e, oo=out, ii=in_: e.copy(oo, ii),
                     reads=reads, writes=writes)

    def vcp(out, in_, reads=(), writes=()):
        return P.add("vector", lambda e, oo=out, ii=in_: e.tensor_copy(oo, ii),
                     reads=reads, writes=writes)

    def tt(out, a, b, op, reads=(), writes=()):
        return P.add("vector", lambda e, oo=out, x=a, y=b, p=op: e.tensor_tensor(oo, x, y, p),
                     reads=reads, writes=writes)

    def ts(out, in_, s1, op0, s2=None, op1=ALU.bypass, reads=(), writes=()):
        return P.add("vector",
                     lambda e, oo=out, ii=in_, a=s1, b=s2, p0=op0, p1=op1:
                     e.tensor_scalar(oo, ii, a, b, op0=p0, op1=p1),
                     reads=reads, writes=writes)

    def stt(out, in0, scalar, in1, op0, op1, reads=(), writes=()):
        return P.add("vector",
                     lambda e, oo=out, ii=in0, s=scalar, jj=in1, p0=op0, p1=op1:
                     e.scalar_tensor_tensor(oo, ii, s, jj, op0=p0, op1=p1),
                     reads=reads, writes=writes)

    def red(out, in_, op, negate=False, reads=(), writes=()):
        return P.add("vector",
                     lambda e, oo=out, ii=in_, p=op, n=negate:
                     e.tensor_reduce(oo, ii, axis=AX.X, op=p, negate=n),
                     reads=reads, writes=writes)

    def recip(out, in_, reads=(), writes=()):
        return P.add("vector", lambda e, oo=out, ii=in_: e.reciprocal(oo, ii),
                     reads=reads, writes=writes)

    def memset(out, val, reads=(), writes=()):
        return P.add("vector", lambda e, oo=out, v=val: e.memset(oo, v),
                     reads=reads, writes=writes)

    _smalls = [
        (c_sb[:, :], dc[:, :], "c_sb"), (csts[:, :], dconsts[:, :], "csts"),
        (identb[:, :], didb[:, :], "identb"), (identf[:, :], didf[:, :], "identf"),
        (prot[:, :], dprot[:, :], "prot"), (onesr[:, :], dones[:, :], "onesr"),
        (mrg[:, :, :], dmrg[:, :, :], "mrg"), (m_t[:, :], dmt[:, :], "m_t"),
        (u_sb[:, :], du[:, :], "u_sb"),
    ]
    dmas("sync", [(d0, s0) for (d0, s0, _) in _smalls],
         writes=tuple(nm for (_, _, nm) in _smalls))
    act(scsb[:, :], c_sb[:, :], AF.Silu, reads=("c_sb",), writes=("scsb",))
    wpair = [wpairA, wpairB]

    def wmod_block(bi, njp):
        wp = wpair[bi % 2]
        wn = "wp%d" % (bi % 2)
        qn = ("gpsimd", "scalardma", "gpsimd", "sync")[bi % 4]
        src = [dwmod[q * 512:(q + 1) * 512, njp * 256:(njp + 2) * 256]
               .rearrange("(a p) b -> p a b", p=128) for q in range(4)]
        if qn == "gpsimd":
            dmas("gpsimd", [(wp[:, q * 4:(q + 1) * 4, :], src[q]) for q in range(4)],
                 reads=(), writes=(wn,))
        else:
            dmas(qn, [(wstgP[:, q * 4:(q + 1) * 4, :], src[q]) for q in range(4)],
                 reads=(), writes=("wst",))
            P.add("vector",
                  lambda e, wp=wp: e.tensor_copy(wp[:, :, :], wstgP[:, :, :]),
                  reads=("wst",), writes=(wn,))
        for kc in range(16):
            for s in (0, 1):
                zsl, zslr = (zA, "zA") if s == 0 else (zB, "zB")
                mm(zsl[0:1, 0:256], scsb[:, kc:kc + 1], wp[:, kc, s * 256:(s + 1) * 256],
                   kc == 0, kc == 15, reads=("scsb", wn), writes=(zslr,))
        for s in (0, 1):
            nj = njp + s
            zsl, zslr = (zA, "zA") if s == 0 else (zB, "zB")
            g = nj // 4
            pos = nj % 4
            buf = stg2[g % 2]
            ch = buf[0:1, pos * 256:(pos + 1) * 256]
            cn = "sg%d" % (nj % 8)
            P.add("scalar",
                  lambda e, zsl=zsl, ch=ch: e.copy(ch, zsl[0:1, 0:256]),
                  reads=(zslr,), writes=(cn, zslr))
            sl_ = nj // 8
            if sl_ in (1, 4):
                ts(ch, ch, 1.0, ALU.add, reads=(), writes=(cn,))
            elif sl_ in (2, 5):
                scr = scr2[nj % 2]
                srn = "scr%d" % (nj % 2)
                P.add("scalar", lambda e, ch=ch, scr=scr: e.activation(scr, ch, AF.Sign),
                      reads=(cn,), writes=(srn,))
                P.add("scalar", lambda e, ch=ch: e.activation(ch, ch, AF.Abs),
                      reads=(), writes=(cn,))
                ts(ch, ch, 0.25, ALU.max, reads=(), writes=(cn,))
                tt(ch, ch, scr, ALU.mult, reads=(srn,), writes=(cn,))
            if pos == 3:
                dma("sync", dscrP[0:1, g * 1024:(g + 1) * 1024], buf,
                    reads=tuple("sg%d" % ((g % 2) * 4 + i) for i in range(4)),
                    writes=("dP%d" % g,))

    def bcast(dst, slot, name):
        return dma("gpsimd", dst,
                   dscrP[0:1, slot * 2048:(slot + 1) * 2048].partition_broadcast(128),
                   reads=("dP%d" % (2 * slot), "dP%d" % (2 * slot + 1)), writes=(name,))

    def ln_norm_tile(i, src_dram, row0, dstview, dstname, dst_ti):
        xb = xkb[i % 2]
        xr = "xk%d" % (i % 2)
        dma("sync", xb[:, :], src_dram[row0:row0 + 128, :], reads=(), writes=(xr,))
        sc0 = 8 * (i % 2)
        red(stats[:, sc0:sc0 + 1], xb[:, :], ALU.add, reads=(xr,), writes=("stats%d" % (i % 2),))
        act(pBig[:, :], xb[:, :], AF.Square, accum=stats[:, sc0 + 1:sc0 + 2],
            reads=(xr,) + PB_ALL, writes=PB_ALL + ("stats%d" % (i % 2),))
        ts(stats[:, sc0 + 2:sc0 + 3], stats[:, sc0:sc0 + 1], 1.0 / 2048, ALU.mult,
           reads=("stats%d" % (i % 2),), writes=("stats%d" % (i % 2),))
        tt(stats[:, sc0 + 3:sc0 + 4], stats[:, sc0 + 2:sc0 + 3], stats[:, sc0 + 2:sc0 + 3],
           ALU.mult, reads=("stats%d" % (i % 2),), writes=("stats%d" % (i % 2),))
        stt(stats[:, sc0 + 4:sc0 + 5], stats[:, sc0 + 1:sc0 + 2], 1.0 / 2048,
            stats[:, sc0 + 3:sc0 + 4], ALU.mult, ALU.subtract,
            reads=("stats%d" % (i % 2),), writes=("stats%d" % (i % 2),))
        act(stats[:, sc0 + 5:sc0 + 6], stats[:, sc0 + 4:sc0 + 5], AF.Sqrt, bias=csts[:, 0:1],
            reads=("stats%d" % (i % 2), "csts"), writes=("stats%d" % (i % 2),))
        recip(stats[:, sc0 + 6:sc0 + 7], stats[:, sc0 + 5:sc0 + 6],
              reads=("stats%d" % (i % 2),), writes=("stats%d" % (i % 2),))
        hb, hbr = (htok, "htok") if i % 2 == 0 else (bc2, "bc2n")
        ts(hb[:, :], xb[:, :], stats[:, sc0 + 2:sc0 + 3], ALU.subtract,
           stats[:, sc0 + 6:sc0 + 7], ALU.mult,
           reads=(xr, "stats%d" % (i % 2)), writes=(hbr,))
        for g2 in range(2):
            for j in range(8):
                fc = g2 * 8 + j
                tr(pH[:, j * 128:(j + 1) * 128], hb[:, fc * 128:(fc + 1) * 128], identb[:, :],
                   reads=(hbr, "identb", "pH%d" % j), writes=("pH%d" % j,))
            P.add("scalar",
                  lambda e, dv=dstview, g2=g2, ti=dst_ti:
                  e.copy(dv[:, g2 * 8:(g2 + 1) * 8, ti * 128:(ti + 1) * 128],
                         pH[:, :].rearrange("p (a b) -> p a b", a=8)),
                  reads=tuple("pH%d" % k for k in range(8)),
                  writes=(dstname,) + tuple("pH%d" % k for k in range(8)))

    def ln_mod_tile(i, src_dram, row0, dstview, dstname, dst_ti):
        xb = xkb[i % 2]
        xr = "xk%d" % (i % 2)
        dma("sync", xb[:, :], src_dram[row0:row0 + 128, :], reads=(), writes=(xr,))
        red(stats[:, 0:1], xb[:, :], ALU.add, reads=(xr,), writes=("stats",))
        act(pBig[:, :], xb[:, :], AF.Square, accum=stats[:, 1:2],
            reads=(xr,) + PB_ALL, writes=PB_ALL + ("stats",))
        ts(stats[:, 2:3], stats[:, 0:1], 1.0 / 2048, ALU.mult, reads=("stats",), writes=("stats",))
        tt(stats[:, 3:4], stats[:, 2:3], stats[:, 2:3], ALU.mult, reads=("stats",), writes=("stats",))
        stt(stats[:, 4:5], stats[:, 1:2], 1.0 / 2048, stats[:, 3:4], ALU.mult, ALU.subtract,
            reads=("stats",), writes=("stats",))
        act(stats[:, 5:6], stats[:, 4:5], AF.Sqrt, bias=csts[:, 0:1],
            reads=("stats", "csts"), writes=("stats",))
        recip(stats[:, 6:7], stats[:, 5:6], reads=("stats",), writes=("stats",))
        ts(pBig[:, :], xb[:, :], stats[:, 2:3], ALU.subtract, stats[:, 6:7], ALU.mult,
           reads=(xr, "stats") + PB_ALL, writes=PB_ALL)
        tt(htok[:, :], pBig[:, :], bc1[:, :], ALU.mult,
           reads=PB_ALL + ("bc1",), writes=("htok",) + PB_ALL)
        tt(htok[:, :], htok[:, :], bc2[:, :], ALU.add, reads=("bc2",), writes=("htok",))
        for g2 in range(2):
            for j in range(8):
                fc = g2 * 8 + j
                tr(pH[:, j * 128:(j + 1) * 128], htok[:, fc * 128:(fc + 1) * 128], identb[:, :],
                   reads=("htok", "identb", "pH%d" % j), writes=("pH%d" % j,))
            P.add("scalar",
                  lambda e, dv=dstview, g2=g2, ti=dst_ti:
                  e.copy(dv[:, g2 * 8:(g2 + 1) * 8, ti * 128:(ti + 1) * 128],
                         pH[:, :].rearrange("p (a b) -> p a b", a=8)),
                  reads=tuple("pH%d" % k for k in range(8)),
                  writes=(dstname,) + tuple("pH%d" % k for k in range(8)))

    ln_calls = []
    for i in range(8):
        ln_calls.append((2 * i, dx, i * 128, hq, "hq", i))
        ln_calls.append((2 * i + 1, dxk, i * 128, hk, "hk", i))
    done = 0
    for bi, njp in enumerate(range(0, 48, 2)):
        wmod_block(bi, njp)
        want = (bi + 1) * len(ln_calls) // 24
        while done < want:
            ln_norm_tile(*ln_calls[done])
            done += 1
    bcast(bc1[:, :], 1, "bc1")

    bcast(bc2[:, :], 0, "bc2b")
    P.ops[-1].waits.extend(P.lr.get("bc2n", ()))
    for vsrc, vrn, vdst, vdr in ((bc1, "bc1", ca1col, "ca1col"), (bc2, "bc2b", sacol, "sacol")):
        for f in range(16):
            tr(pH[:, f * 64:f * 64 + 1], vsrc[0:1, f * 128:(f + 1) * 128], identb[0:1, 0:1],
               reads=(vrn, "identb", "pH%d" % (f // 2)), writes=("pH%d" % (f // 2),))
        pH3 = pH[:, :].rearrange("p (a b) -> p a b", a=16)
        P.add("scalar", lambda e, vdst=vdst, pH3=pH3: e.copy(vdst[:, :], pH3[:, :, 0:1]),
              reads=tuple("pH%d" % k for k in range(8)),
              writes=(vdr,) + tuple("pH%d" % k for k in range(8)))
    for dv, dn in ((hq, "hq"), (hk, "hk")):
        for fc in range(16):
            ts(dv[:, fc, :], dv[:, fc, :], ca1col[:, fc:fc + 1], ALU.mult,
               sacol[:, fc:fc + 1], ALU.add, reads=("ca1col", "sacol"), writes=(dn,))

    memset(vbuf[:, :, :, 128:129], 1.0, writes=("vbuf", "wp1"))

    wslot = [wbufA, wbufB]
    wres = ["wi0", "wi1"]
    slot_i = [0]

    wsl_count = [0]

    def load_wslice(wdram, col0, width=256):
        s = slot_i[0]; slot_i[0] ^= 1
        wb = wslot[s]; rn = wres[s]
        extra = ("wp0", "wp1") if wsl_count[0] < 2 else ()
        wsl_count[0] += 1
        dmas("gpsimd", [(wb[:, q * 4:(q + 1) * 4, 0:width],
                         wdram[q * 512:(q + 1) * 512, col0:col0 + width]
                         .rearrange("(a p) b -> p a b", p=128)) for q in range(4)],
             reads=(), writes=(rn,) + extra)
        return wb, rn

    def rope_evac(pp, pres, cos_s, sin_s, crs, dstv, dname, hh, n):
        par = (hh * 2 + n) % 2
        qr, qrr = (qraw, "qraw") if par == 0 else (qraw2, "tmpf2")
        rb, rbr = (rotb, "rotb") if par == 0 else (rotb2, "tmpf2")
        qc, qcr = (qcos, "qcos") if par == 0 else (qcos2, "qcos2")
        zsl, zslr = (zB, "zB") if par == 0 else (zC, "zC")
        scp(qr[:, :], pp, reads=(pres,), writes=(qrr, pres))
        mm(zsl[:, :], prot[:, :], qr[:, :], True, True,
           reads=("prot", qrr, zslr), writes=(zslr,))
        tt(qc[:, :], qr[:, :], cos_s, ALU.mult, reads=(qrr, crs[0]), writes=(qcr,))
        scp(rb[:, :], zsl[:, :], reads=(zslr,), writes=(rbr, zslr))
        dsl = dstv[:, hh, n * 512:(n + 1) * 512]
        tt(dsl, rb[:, :], sin_s, ALU.mult, reads=(rbr, crs[1]), writes=(dname, "wst"))
        tt(dsl, dsl, qc[:, :], ALU.add, reads=(qcr,), writes=(dname,))

    def smm(hh, qt):
        pp, pres = (pBig[:, 0:1024], ("pBig0", "pBig1")) if qt % 2 == 0 else \
                   (pBig[:, 1024:2048], ("pBig2", "pBig3"))
        for half in range(2):
            mm(pp[:, half * 512:(half + 1) * 512],
               qTg[:, hh, qt * 128:(qt + 1) * 128], kTg[:, hh, half * 512:(half + 1) * 512],
               True, True, reads=("qTg", "kTg") + pres, writes=pres)
        nslot = qt % 2
        red(nmx[:, nslot:nslot + 1], pp, ALU.max, negate=True,
            reads=pres, writes=("nmx%d" % nslot,))

    def stail(hh, qt, pdst, pdres):
        pp, pres = (pBig[:, 0:1024], ("pBig0", "pBig1")) if qt % 2 == 0 else \
                   (pBig[:, 1024:2048], ("pBig2", "pBig3"))
        nslot = qt % 2
        act(probsq[:, :], pp, AF.Exp, bias=nmx[:, nslot:nslot + 1],
            reads=pres + ("nmx%d" % nslot, "r1d"), writes=("r1d",) + pres)
        for kc in range(8):
            tr(pH[:, kc * 128:(kc + 1) * 128], probsq[:, kc * 128:(kc + 1) * 128],
               identb[:, :], reads=("r1d", "identb", "pH%d" % kc), writes=("pH%d" % kc,))
        vcp(pdst, pH[:, :], reads=tuple("pH%d" % k for k in range(8)),
            writes=(pdres,) + tuple("pH%d" % k for k in range(8)))

    def pv(head, hh, qt, psrc, psres):
        psrc3 = psrc.rearrange("p (a b) -> p a b", a=8)
        for kc in range(8):
            K = 127 if kc == 7 else 128
            mm(zA[:, 0:129], psrc3[0:K, kc, :], vbuf[0:K, kc, hh, 0:129],
               kc == 0, kc == 7, reads=(psres, "vbuf", "zA"), writes=("zA",))
        recip(zvv[:, 0:1], zA[:, 128:129], reads=("zA",), writes=("zvv",))
        ts(hot[:, :], zA[:, 0:128], zvv[:, 0:1], ALU.mult,
           reads=("zA", "zvv"), writes=("hot", "zA"))
        tr(zB[:, 0:128], hot[:, :], identf[:, :], reads=("hot", "identf", "zB"), writes=("zB",))
        P.add("scalar",
              lambda e, head=head, qt=qt:
              e.copy(ho[:, head, qt * 128:(qt + 1) * 128], zB[:, 0:128]),
              reads=("zB",), writes=("ho", "zB"))

    cast_pieces = []
    if PH == "ALL":
        cast_pieces.append((gateTb[:, :, :],
                            dgate[:, :].rearrange("(a p) b -> p a b", p=128), "gateTb"))
        cast_pieces.append((db13c[:, :, :, 0:2816], db13[:, :, :, 0:2816], "db13c"))
        cast_pieces.append((db13c[:, :, :, 2816:5632], db13[:, :, :, 2816:5632], "db13c"))
        cast_pieces.append((da2c[:, :, :], da2[:, :, :], "da2c"))
        cast_pieces.append((db2c[:, :, :], db2[:, :, :], "db2c"))
        for q in range(4):
            for h in range(4):
                cast_pieces.append((dw13b[q * 512:(q + 1) * 512, h * 2816:(h + 1) * 2816],
                                    dw13[q * 512:(q + 1) * 512, h * 2816:(h + 1) * 2816],
                                    "dw13b"))
        for q in range(4):
            for h in range(2):
                cast_pieces.append((dw2b[q * 1408:(q + 1) * 1408, h * 1024:(h + 1) * 1024],
                                    dw2[q * 1408:(q + 1) * 1408, h * 1024:(h + 1) * 1024],
                                    "dw2b"))

    def drip_cast(n=1):
        for _ in range(n):
            if cast_pieces:
                dst, src, nm = cast_pieces.pop(0)
                dma("gpsimd", dst, src, reads=(), writes=(nm,))

    for g in range(4):
        dmas("sync", [(cosq_s, dcosq[:, :]), (sinq_s, dsinq[:, :]),
                      (coskS, dcosk[:, :]), (sinkS, dsink[:, :])],
             reads=(),
             writes=("r1a", "r1b", "r1c", "r1d") + (("wst",) if g == 0 else ()))
        for hp in range(2):
            wb, rn = load_wslice(dwq, g * 512 + hp * 256)
            for hh in (hp * 2, hp * 2 + 1):
                pps = [PBQ[(hh * 2 + n) % 4] for n in range(2)]
                for kc in range(16):
                    for n in range(2):
                        pp, pres = pps[n]
                        mm(pp, wb[:, kc, (hh - hp * 2) * 128:(hh - hp * 2 + 1) * 128],
                           hq[:, kc, n * 512:(n + 1) * 512],
                           kc == 0, kc == 15, reads=(rn, "hq", pres), writes=(pres,))
                for n in range(2):
                    pp, pres = pps[n]
                    rope_evac(pp, pres, cosq_s[:, n * 512:(n + 1) * 512],
                              sinq_s[:, n * 512:(n + 1) * 512], ("r1a", "r1b"), qTg, "qTg", hh, n)
        for hp in range(2):
            wb, rn = load_wslice(dwk, g * 512 + hp * 256)
            for hh in (hp * 2, hp * 2 + 1):
                pps = [PBQ[(hh * 2 + n) % 4] for n in range(2)]
                for kc in range(16):
                    for n in range(2):
                        pp, pres = pps[n]
                        mm(pp, wb[:, kc, (hh - hp * 2) * 128:(hh - hp * 2 + 1) * 128],
                           hk[:, kc, n * 512:(n + 1) * 512],
                           kc == 0, kc == 15, reads=(rn, "hk", pres), writes=(pres,))
                for n in range(2):
                    pp, pres = pps[n]
                    rope_evac(pp, pres, coskS[:, n * 512:(n + 1) * 512],
                              sinkS[:, n * 512:(n + 1) * 512], ("r1c", "r1d"), kTg, "kTg", hh, n)
        for hp in range(2):
            wb, rn = load_wslice(dwv, g * 512 + hp * 256)
            for ktp in range(0, 8, 2):
                for kc in range(16):
                    for kt in (ktp, ktp + 1):
                        pp, pres = PBQ[kt % 4]
                        mm(pp[:, 0:256], hk[:, kc, kt * 128:(kt + 1) * 128], wb[:, kc, 0:256],
                           kc == 0, kc == 15, reads=(rn, "hk", pres), writes=(pres,))
                for kt in (ktp, ktp + 1):
                    pp, pres = PBQ[kt % 4]
                    P.add("scalar",
                          lambda e, pp=pp, kt=kt, hp=hp:
                          e.copy(vbuf[:, kt, hp * 2:hp * 2 + 2, 0:128],
                                 pp[:, 0:256].rearrange("p (a b) -> p a b", a=2)),
                          reads=(pres,), writes=("vbuf", pres))

        for hh in range(4):
            head = g * 4 + hh
            drip_cast(1)

            def slot(qt):
                return (pTc, "r1b") if qt % 2 == 0 else (pT6, "r1c")

            smm(hh, 7)
            smm(hh, 0)
            stail(hh, 7, pT7, "r1a")
            drip_cast(1)
            pT73 = pT7.rearrange("p (a b) -> p a b", a=8)
            for kc in range(8):
                mm(zC[:, 0:3], pT73[:, kc, :], mrg[:, kc, :], kc == 0, kc == 7,
                   reads=("r1a", "mrg", "zC"), writes=("zC",))
            vcp(mz[:, 0:3], zC[:, 0:3], reads=("zC",), writes=("mz", "zC"))
            dmas("sync", [(mrow[0:1, 0:3], mz[127:128, 0:3]),
                          (v204[0:1, :], vbuf[76:77, 1, hh, 0:128])],
                 reads=("mz", "vbuf"), writes=("mrow", "v204"))
            for qt in range(6):
                smm(hh, qt + 1) if qt < 5 else smm(hh, 6)
                if qt >= 1:
                    pv(head, hh, qt - 1, *slot(qt - 1))
                stail(hh, qt, *slot(qt))
            pv(head, hh, 5, *slot(5))
            stail(hh, 6, pTc, "r1b")
            recip(bmt[0:1, 0:1], mrow[0:1, 2:3], reads=("mrow",), writes=("bmt",))
            ts(bmt[0:1, 1:2], mrow[0:1, 0:1], bmt[0:1, 0:1], ALU.mult, csts[0:1, 2:3], ALU.mult,
               reads=("mrow", "bmt", "csts"), writes=("bmt",))
            ts(bmt[0:1, 1:2], bmt[0:1, 1:2], csts[0:1, 1:2], ALU.add,
               reads=("bmt", "csts"), writes=("bmt",))
            ts(bmt[0:1, 2:3], mrow[0:1, 1:2], bmt[0:1, 0:1], ALU.mult,
               reads=("mrow", "bmt"), writes=("bmt",))
            recip(bmt[0:1, 3:4], bmt[0:1, 1:2], reads=("bmt",), writes=("bmt",))
            ts(bmt[0:1, 4:5], bmt[0:1, 2:3], bmt[0:1, 3:4], ALU.mult, reads=("bmt",), writes=("bmt",))
            ts(bmt[0:1, 4:5], bmt[0:1, 4:5], 1.0, ALU.min, 0.0, ALU.max,
               reads=("bmt",), writes=("bmt",))
            ts(bmt[0:1, 5:6], bmt[0:1, 4:5], u_sb[0:1, head:head + 1], ALU.is_gt,
               reads=("bmt", "u_sb"), writes=("bmt",))
            ts(bmt[0:1, 6:7], bmt[0:1, 5:6], csts[0:1, 3:4], ALU.mult,
               reads=("bmt", "csts"), writes=("bmt",))
            ts(vescl[0:1, :], v204[0:1, :], bmt[0:1, 6:7], ALU.mult,
               reads=("v204", "bmt"), writes=("vescl",))
            mm(zC[:, 0:128], onesr[0:1, :], vescl[0:1, :], True, True,
               reads=("onesr", "vescl", "zC"), writes=("zC",))
            for kt in range(8):
                stt(vbuf[:, kt, hh, 0:128], zC[:, 0:128], m_t[:, kt:kt + 1],
                    vbuf[:, kt, hh, 0:128], ALU.mult, ALU.add,
                    reads=("zC", "m_t", "vbuf"), writes=("vbuf",))
            pv(head, hh, 6, pTc, "r1b")
            pv(head, hh, 7, pT7, "r1a")

    bcast(bc1[:, :], 2, "bc1")
    dmas("sync", [(xres[:, i, :], dx[i * 128:(i + 1) * 128, :]) for i in range(8)],
         reads=(), writes=tuple("x%d" % i for i in range(8)) + ("hq", "hk"))
    for dj in range(8):
        wb, rn = load_wslice(dwo, dj * 256)
        for tip in range(0, 8, 2):
            for kc in range(16):
                for ti in (tip, tip + 1):
                    pp, pres = PBQ[ti % 2]
                    mm(pp[:, 0:256], ho[:, kc, ti * 128:(ti + 1) * 128], wb[:, kc, 0:256],
                       kc == 0, kc == 15, reads=(rn, "ho", pres), writes=(pres,))
            for ti in (tip, tip + 1):
                pp, pres = PBQ[ti % 2]
                tf = tmpf[:, 0:256] if ti % 2 == 0 else tmpf2[:, 0:256]
                tfr = "tmpf" if ti % 2 == 0 else "tmpf2"
                tt(tf, pp[:, 0:256], bc1[:, dj * 256:(dj + 1) * 256], ALU.mult,
                   reads=(pres, "bc1"), writes=(tfr, "qraw", "rotb", pres))
                tt(xres[:, ti, dj * 256:(dj + 1) * 256], xres[:, ti, dj * 256:(dj + 1) * 256],
                   tf, ALU.add, reads=(tfr,), writes=("x%d" % ti,))
    tp = tap("xattn", [TOK, D])
    if tp is not None:
        for i in range(8):
            dma("sync", tp[i * 128:(i + 1) * 128, :], xres[:, i, :], reads=("x%d" % i,))

    if PH == "ALL":
        for hf in range(2):
            dma("gpsimd", aTbv[:, :, :], da1[:, :].rearrange("(a p) b -> p a b", p=128),
                reads=(), writes=("aTb",) + CLOB_B_ARC)
            bcast(bc1[:, :], 4, "bc1")
            bcast(bc2[:, :], 3, "bc2")
            for ti in range(4):
                gi = hf * 4 + ti
                xv = xres[:, gi, :]
                xr = "x%d" % gi
                red(stats[:, 0:1], xv, ALU.add, reads=(xr,), writes=("stats",))
                act(pBig[:, :], xv, AF.Square, accum=stats[:, 1:2],
                    reads=(xr,) + PB_ALL, writes=PB_ALL + ("stats",))
                ts(stats[:, 2:3], stats[:, 0:1], 1.0 / 2048, ALU.mult,
                   reads=("stats",), writes=("stats",))
                tt(stats[:, 3:4], stats[:, 2:3], stats[:, 2:3], ALU.mult,
                   reads=("stats",), writes=("stats",))
                stt(stats[:, 4:5], stats[:, 1:2], 1.0 / 2048, stats[:, 3:4],
                    ALU.mult, ALU.subtract, reads=("stats",), writes=("stats",))
                act(stats[:, 5:6], stats[:, 4:5], AF.Sqrt, bias=csts[:, 0:1],
                    reads=("stats", "csts"), writes=("stats",))
                recip(stats[:, 6:7], stats[:, 5:6], reads=("stats",), writes=("stats",))
                ts(pBig[:, :], xv, stats[:, 2:3], ALU.subtract, stats[:, 6:7], ALU.mult,
                   reads=(xr, "stats") + PB_ALL, writes=PB_ALL)
                tt(htok[:, :], pBig[:, :], bc1[:, :], ALU.mult,
                   reads=PB_ALL + ("bc1",), writes=("htok",) + PB_ALL)
                tt(htok[:, :], htok[:, :], bc2[:, :], ALU.add,
                   reads=("bc2",), writes=("htok",))
                for g2 in range(2):
                    for j in range(8):
                        fc = g2 * 8 + j
                        tr(pH[:, j * 128:(j + 1) * 128], htok[:, fc * 128:(fc + 1) * 128],
                           identb[:, :], reads=("htok", "identb", "pH%d" % j),
                           writes=("pH%d" % j,))
                    P.add("scalar",
                          lambda e, g2=g2, ti=ti:
                          e.copy(h2T[:, g2 * 8:(g2 + 1) * 8, ti * 128:(ti + 1) * 128],
                                 pH[:, :].rearrange("p (a b) -> p a b", a=8)),
                          reads=tuple("pH%d" % k for k in range(8)),
                          writes=("h2T",) + tuple("pH%d" % k for k in range(8))
                          + (("wi0", "wi1") if ti == 0 else ()))
            for daT, uslot, ur in [(da1, 0, "ubt0"), (da3, 1, "ubt1")]:
                if uslot == 1:
                    dma("gpsimd", aTbv[:, :, :], daT[:, :].rearrange("(a p) b -> p a b", p=128),
                        reads=(), writes=("aTb",) + CLOB_B_ARC)
                for kc in range(16):
                    mm(zA[:, :], aTbv[:, kc, :], h2T[:, kc, :], kc == 0, kc == 15,
                       reads=("aTb", "h2T", "zA"), writes=("zA",))
                ts(ubt[:, uslot, :], zA[:, :], 2.0, ALU.mult,
                   reads=("zA",), writes=(ur, "w13s0", "zA") + CLOB_B_ARC)
            dmas("sync", [(u1e[:, e_, :], ubt[e_ * 16:(e_ + 1) * 16, 0, :]) for e_ in range(NE)]
                 + [(u3e[:, e_, :], ubt[e_ * 16:(e_ + 1) * 16, 1, :]) for e_ in range(NE)],
                 reads=("ubt0", "ubt1"), writes=("u1e", "u3e"))
            ts(u1n[:, :, :], u1e[:, :, :], -1.0, ALU.mult, reads=("u1e",), writes=("u1n",))
            ts(u3n[:, :, :], u3e[:, :, :], -1.0, ALU.mult, reads=("u3e",), writes=("u3n",))
            dmas("gpsimd", [(uN1[:, e_, :], u1n[:, e_ - 1, :]) for e_ in range(1, NE)]
                 + [(uN3[:, e_, :], u3n[:, e_ - 1, :]) for e_ in range(1, NE)],
                 reads=("u1n", "u3n"), writes=("uP1", "uP3"))

            for tip in range(0, 4, 2):
                for kc in range(16):
                    for ti in (tip, tip + 1):
                        zg, zgr = (zB, "zB") if ti % 2 == 0 else (zC, "zC")
                        mm(zg[:, 0:8], h2T[:, kc, ti * 128:(ti + 1) * 128], gateTb[:, kc, :],
                           kc == 0, kc == 15, reads=("h2T", "gateTb", zgr), writes=(zgr,))
                for ti in (tip, tip + 1):
                    zg, zgr = (zB, "zB") if ti % 2 == 0 else (zC, "zC")
                    vcp(lg[:, ti, :], zg[:, 0:8], reads=(zgr,), writes=("lg", zgr))
                for ti in (tip, tip + 1):
                    lv = lg[:, ti, :]
                    red(t2s[:, 0:1], lv, ALU.max, negate=True, reads=("lg",), writes=("t2s",))
                    ts(tmp8[:, :], lv, t2s[:, 0:1], ALU.add, reads=("lg", "t2s"), writes=("tmp8",))
                    ts(msk8[:, :], tmp8[:, :], 0.0, ALU.is_lt, reads=("tmp8",), writes=("msk8",))
                    tt(tmp8[:, :], lv, msk8[:, :], ALU.mult, reads=("lg", "msk8"), writes=("tmp8",))
                    ts(msk8[:, :], msk8[:, :], 1.0, ALU.subtract, reads=(), writes=("msk8",))
                    stt(tmp8[:, :], msk8[:, :], 30000.0, tmp8[:, :], ALU.mult, ALU.add,
                        reads=("msk8",), writes=("tmp8",))
                    red(t2s[:, 1:2], tmp8[:, :], ALU.max, reads=("tmp8",), writes=("t2s",))
                    act(t2s[:, 2:3], t2s[:, 1:2], AF.Exp, bias=t2s[:, 0:1],
                        reads=("t2s",), writes=("t2s",))
                    ts(t2s[:, 3:4], t2s[:, 2:3], 1.0, ALU.add, reads=("t2s",), writes=("t2s",))
                    recip(t2s[:, 4:5], t2s[:, 3:4], reads=("t2s",), writes=("t2s",))
                    for e_ in range(NE):
                        act(t2s[:, 5:6], lv[:, e_:e_ + 1], AF.Exp, bias=t2s[:, 0:1],
                            reads=("lg", "t2s"), writes=("t2s",))
                        ts(t2s[:, 6:7], lv[:, e_:e_ + 1], t2s[:, 1:2], ALU.is_ge,
                           reads=("lg", "t2s"), writes=("t2s",))
                        tt(t2s[:, 7:8], t2s[:, 5:6], t2s[:, 6:7], ALU.mult,
                           reads=("t2s",), writes=("t2s",))
                        tt(wfull[:, ti, e_:e_ + 1], t2s[:, 7:8], t2s[:, 4:5], ALU.mult,
                           reads=("t2s",), writes=("wfull",))
            tpw = tap("wfull", [128, 8, 8])
            if tpw is not None:
                dma("sync", tpw[:, hf * 4:(hf + 1) * 4, :], wfull[:, :, :], reads=("wfull",))
            tpl = tap("lgt", [128, 8, 8])
            if tpl is not None:
                dma("sync", tpl[:, hf * 4:(hf + 1) * 4, :], lg[:, :, :], reads=("lg",))
            for ti in range(4):
                tr(zB[0:8, ti * 128:(ti + 1) * 128], wfull[:, ti, :], identf[:, :],
                   reads=("wfull", "identf", "zB"), writes=("zB",))
            vcp(werows[:, :], zB[0:8, :], reads=("zB",), writes=("werows", "zB"))
            dma("sync", dwe[:, :], werows[:, :], reads=("werows",), writes=("dwe",))
            dmas("gpsimd", [(webc[:, e_, :], dwe[e_:e_ + 1, :].partition_broadcast(128))
                            for e_ in range(NE)],
                 reads=("dwe",), writes=("webc",) + CLOB_B_ARC)

            zts = [zA, zB, zC]
            zrs = ["zA", "zB", "zC"]
            TMP = [(tmpf, "tmpf", ("qraw", "rotb")), (tmpf2, "tmpf2", ())]
            ACTB = [(actb, "actb"), (actb2, "actb2")]
            WACT = [(wactb, "wactb"), (wactb2, "wactb2")]
            ACCF = [(accf, "accf"), (accf2, "accf2")]
            for fcp in range(0, NFC, 2):
                for w in (0, 1):
                    fc = fcp + w
                    sl = fc % 2
                    wr = "w13s%d" % sl
                    dmas("sync",
                         [(w13s[sl][:, q * 4:(q + 1) * 4, :],
                           dw13b[q * 512:(q + 1) * 512, fc * 256:(fc + 1) * 256]
                           .rearrange("(a p) b -> p a b", p=128)) for q in range(4)],
                         reads=("dw13b",),
                         writes=(wr,) + (("ubt0", "ubt1") + CLOB_B_ARC if fc < 2 else ()))
                    dmas("gpsimd", [
                        (bpair[0:16, sl, :, :], db13c[:, :, 0, fc * 128:(fc + 1) * 128]),
                        (bpair[16:32, sl, 1:NE, :],
                         db13c[:, 0:NE - 1, 0, fc * 128:(fc + 1) * 128]),
                        (bpair[32:48, sl, :, :], db13c[:, :, 1, fc * 128:(fc + 1) * 128]),
                        (bpair[48:64, sl, 1:NE, :],
                         db13c[:, 0:NE - 1, 1, fc * 128:(fc + 1) * 128]),
                        (a2s[:, sl, :, :], da2c[fc * 128:(fc + 1) * 128, :, :]),
                    ], reads=("db13c", "da2c"), writes=("b13s%d" % sl, "a2s%d" % sl))
                for w in (0, 1):
                    fc = fcp + w
                    sl = fc % 2
                    wr = "w13s%d" % sl
                    B1, B1r = PBQ[0 + 2 * w]
                    B3, B3r = PBQ[1 + 2 * w]
                    for kc in range(16):
                        mm(B1, w13s[sl][:, kc, 0:128], h2T[:, kc, :], kc == 0, False,
                           reads=(wr, "h2T", B1r), writes=(B1r,), skip_gc=True)
                        mm(B3, w13s[sl][:, kc, 128:256], h2T[:, kc, :], kc == 0, False,
                           reads=(wr, "h2T", B3r), writes=(B3r,), skip_gc=True)
                def flush_z():
                    while pend_z:
                        zfc, zsl2, ze, zw = pend_z.pop(0)
                        wb_, wbr = WACT[zw]
                        zt = zts[ze // 3]
                        zo = 32 * (ze % 3)
                        mm(zt[zo:zo + 16, :], a2s[:, zsl2, ze, :], wb_[:, :],
                           zfc == 0, zfc == NFC - 1,
                           reads=("a2s%d" % zsl2, wbr, zrs[ze // 3]),
                           writes=(zrs[ze // 3],), skip_gc=True)

                pend_z = []
                for e_ in range(NE):
                    for w in (0, 1):
                        sl = (fcp + w) % 2
                        B1, B1r = PBQ[0 + 2 * w]
                        B3, B3r = PBQ[1 + 2 * w]
                        if e_ == 0:
                            mm(B1, bpair[0:16, sl, 0, :], u1e[:, 0, :], False, False,
                               reads=("b13s%d" % sl, "u1e", B1r), writes=(B1r,), skip_gc=True)
                            mm(B3, bpair[32:48, sl, 0, :], u3e[:, 0, :], False, False,
                               reads=("b13s%d" % sl, "u3e", B3r), writes=(B3r,), skip_gc=True)
                        else:
                            mm(B1, bpair[0:32, sl, e_, :], uP1[:, e_, :], False, e_ == NE - 1,
                               reads=("b13s%d" % sl, "uP1", "u1e", B1r), writes=(B1r,),
                               skip_gc=True)
                            mm(B3, bpair[32:64, sl, e_, :], uP3[:, e_, :], False, e_ == NE - 1,
                               reads=("b13s%d" % sl, "uP3", "u3e", B3r), writes=(B3r,),
                               skip_gc=True)
                    flush_z()
                    for w in (0, 1):
                        B1, B1r = PBQ[0 + 2 * w]
                        tf, tfr, tfc = TMP[w]
                        act(tf[:, :], B1, AF.Silu, reads=(B1r,), writes=(tfr,) + tfc + (B1r,))
                    for w in (0, 1):
                        B3, B3r = PBQ[1 + 2 * w]
                        tf, tfr, _ = TMP[w]
                        ab, abr = ACTB[w]
                        wb_, wbr = WACT[w]
                        af, afr = ACCF[w]
                        tt(ab[:, :], tf[:, :], B3, ALU.mult, reads=(tfr, B3r), writes=(abr, B3r))
                        tt(wb_[:, :], ab[:, :], webc[:, e_, :], ALU.mult,
                           reads=(abr, "webc"), writes=(wbr,))
                        if e_ == 0:
                            vcp(af[:, :], wb_[:, :], reads=(wbr,), writes=(afr,))
                        else:
                            tt(af[:, :], af[:, :], wb_[:, :], ALU.add,
                               reads=(wbr,), writes=(afr,))
                    for w in (0, 1):
                        pend_z.append((fcp + w, (fcp + w) % 2, e_, w))
                flush_z()
                for w in (0, 1):
                    fc = fcp + w
                    af, afr = ACCF[w]
                    scp(act_sum[:, fc, :], af[:, :], reads=(afr,),
                        writes=("as%d" % fc,) + (("aTb",) if fc < 2 else ())
                        + (CLOB_B_ARB if hf == 0 and fc < 2 else ()))
            bcast(bc1[:, :], 5, "bc1")
            for e_ in range(NE):
                zt = zts[e_ // 3]
                zo = 32 * (e_ % 3)
                ts(zsb[:, e_, :], zt[zo:zo + 16, :], 2.0, ALU.mult,
                   reads=(zrs[e_ // 3],), writes=("zsb", "u1e", "uP1", zrs[e_ // 3]))
            for dj in range(4):
                dma("gpsimd", b2dj[:, :, :],
                    db2c[:, :, dj * 512:(dj + 1) * 512],
                    reads=("db2c",), writes=("b2dj", "w13s0"))
                for fcq in range(11):
                    sl = fcq % 2
                    dmas("scalar", [(w2g[sl][:, j, :],
                                     dw2b[(fcq * 4 + j) * 128:(fcq * 4 + j + 1) * 128,
                                          dj * 512:(dj + 1) * 512]) for j in range(4)],
                         reads=("dw2b",), writes=("w2g%d" % sl, "w13s1"))
                    for j in range(4):
                        fc = fcq * 4 + j
                        for ti in range(4):
                            mm(pBig[:, ti * 512:(ti + 1) * 512],
                               act_sum[:, fc, ti * 128:(ti + 1) * 128], w2g[sl][:, j, :],
                               fc == 0, False,
                               reads=("w2g%d" % sl, "as%d" % fc, "pBig%d" % ti),
                               writes=("pBig%d" % ti,), skip_gc=True)
                for ti in range(4):
                    for e_ in range(NE):
                        mm(pBig[:, ti * 512:(ti + 1) * 512], zsb[:, e_, ti * 128:(ti + 1) * 128],
                           b2dj[:, e_, :], False, e_ == NE - 1,
                           reads=("zsb", "b2dj", "pBig%d" % ti), writes=("pBig%d" % ti,),
                           skip_gc=True)
                for ti in range(4):
                    gi = hf * 4 + ti
                    tt(tmpf[:, :], pBig[:, ti * 512:(ti + 1) * 512],
                       bc1[:, dj * 512:(dj + 1) * 512], ALU.mult,
                       reads=("pBig%d" % ti, "bc1"), writes=("tmpf", "pBig%d" % ti))
                    tt(xres[:, gi, dj * 512:(dj + 1) * 512],
                       xres[:, gi, dj * 512:(dj + 1) * 512], tmpf[:, :], ALU.add,
                       reads=("tmpf",), writes=("x%d" % gi,))
                dmas("sync", [(dout[(hf * 4 + t) * 128:(hf * 4 + t + 1) * 128,
                                    dj * 512:(dj + 1) * 512],
                               xres[:, hf * 4 + t, dj * 512:(dj + 1) * 512])
                              for t in range(4)],
                     reads=tuple("x%d" % (hf * 4 + t) for t in range(4)))

    with (
        nc.semaphore("s_sync") as s0, nc.semaphore("s_gpsimd") as s1,
        nc.semaphore("s_vector") as s2, nc.semaphore("s_scalar") as s3,
        nc.semaphore("s_tensor") as s4, nc.semaphore("s_scalardma") as s5,
        nc.semaphore("s_cc") as s6, nc.semaphore("s_vectordma") as s7,
        nc.semaphore("s_syncB") as s8, nc.semaphore("s_gpsimdB") as s9,
        nc.semaphore("s_scalardmaB") as s10, nc.semaphore("s_vectordmaB") as s11,
        nc.Block() as block,
    ):
        sems = {"sync": s0, "gpsimd": s1, "vector": s2, "scalar": s3, "tensor": s4,
                "scalardma": s5, "cc": s6, "vectordma": s7, "syncB": s8,
                "gpsimdB": s9, "scalardmaB": s10, "vectordmaB": s11}
        P.emit(block, sems)
    print("total instructions:", len(P.ops))
    return nc


_NC_CACHE = {}


def kernel(**inputs):
    cst = _host_constants()
    in_maps = [_host_core_inputs(inputs, c, cst) for c in range(8)]
    if "nc" not in _NC_CACHE:
        _NC_CACHE["nc"] = _build()
    nc = _NC_CACHE["nc"]
    res = run_bass_kernel_spmd(nc, in_maps, core_ids=list(range(8)))
    out = np.empty((4, 2048, 2048), dtype=np.float32)
    for c in range(8):
        b, hf = c // 2, c % 2
        out[b, hf * TOK:(hf + 1) * TOK, :] = np.asarray(res.results[c]["out"])
    return out
